# version 05
# baseline (speedup 1.0000x reference)
import math
import numpy as np
import ml_dtypes
import concourse.bass as bass
import concourse.mybir as mybir
from concourse.bass_utils import run_bass_kernel_spmd

F32 = mybir.dt.float32
BF16 = mybir.dt.bfloat16
I32 = mybir.dt.int32
AF = mybir.ActivationFunctionType
ALU = mybir.AluOpType
AXX = mybir.AxisListType.X

NCORE = 8
D = 2048
T = 8192
NCH = 16
TPC = 1024
NBLK = 16
BLK = 512
NFAT = 16

AB = []
for _r in range(NCORE):
    AB += [_r, NBLK - 1 - _r]
FAT_OF_BLOCK = {b: f for f, b in enumerate(AB)}
OWN_BLOCKS = [(c, NBLK - 1 - c) for c in range(NCORE)]

def gcol_of_abs(tok):
    b = tok // BLK
    return FAT_OF_BLOCK[b] * BLK + tok % BLK

M_L0F = 10.0
M_L0A = 12.0
M_L2A = 40.0
M_L3A = 12.0

LN_EPS = 1e-5
SUBLAYERS = ["l0f", "l0a", "l0f2", "l1f", "l1a", "l1f2",
             "l2f", "l2a", "l2f2", "l3f", "l3a", "l3f2"]
BASE = {"l0f2": "l0f", "l1f2": "l1f", "l2f2": "l2f", "l3f2": "l3f"}
LAUNCH_GROUPS = [list(SUBLAYERS)]
TAPS = []
TAP_RESULTS = []


def _host_consts():
    pos = np.array([AB[f] * BLK + i for f in range(NFAT) for i in range(BLK)],
                   np.float64)
    fr2 = 1.0 / (10000.0 ** (np.arange(32, dtype=np.float64) / 32))
    ang2 = pos[:, None] * fr2[None, :]
    r2c = np.repeat(np.cos(ang2), 2, -1).T
    r2s = np.repeat(np.sin(ang2), 2, -1).T
    r2s = np.concatenate([r2s[32:64], r2s[0:32]], 0)
    r2c = r2c.astype(ml_dtypes.bfloat16)
    r2s = r2s.astype(ml_dtypes.bfloat16)
    fr3 = 1.0 / (10000.0 ** (np.arange(0, 64, 2, dtype=np.float64) / 64))
    ang3 = pos[:, None] * fr3[None, :]
    r3c = np.cos(ang3).T
    r3s = np.sin(ang3).T
    r3c = np.concatenate([r3c, r3c], 0)
    r3s = np.concatenate([r3s, r3s], 0)
    r3c = r3c.astype(ml_dtypes.bfloat16)
    r3s = r3s.astype(ml_dtypes.bfloat16)
    iL = np.arange(128)[:, None]
    jL = np.arange(128)[None, :]
    e8 = np.zeros((16, 128), ml_dtypes.bfloat16)
    for h in range(16):
        e8[h, 8 * h:8 * h + 8] = 1
    return {
        "c_rope2c": r2c, "c_rope2s": r2s, "c_rope3c": r3c, "c_rope3s": r3s,
        "c_ident": np.eye(128, dtype=ml_dtypes.bfloat16),
        "c_maskL": (iL <= jL).astype(ml_dtypes.bfloat16),
        "c_maskU": (iL > jL).astype(ml_dtypes.bfloat16),
        "c_mask120": (np.arange(128) < 120).astype(np.float32)[:, None],
        "c_ones_col": np.ones((128, 1), ml_dtypes.bfloat16),
        "c_ones_row": np.ones((1, 128), ml_dtypes.bfloat16),
        "c_i16": np.eye(16, dtype=ml_dtypes.bfloat16),
        "c_onesrf": np.ones((1, 128), np.float32),
        "c_e32": np.full((128, 512), 0x3F000000, np.int32),
        "c_mc": np.tile(np.array([[-M_L0F, -M_L0A, -M_L2A, -M_L3A,
                                   1e-5, 1e-10, 1e-11, 2.0]], np.float32),
                        (128, 1)),
    }


DRAM_CONSTS = {"c_rope2c": ([64, T], BF16), "c_rope2s": ([64, T], BF16),
               "c_rope3c": ([64, T], BF16), "c_rope3s": ([64, T], BF16)}
SBUF_CONSTS = {"c_ident": ([128, 128], BF16), "c_e32": ([128, 512], I32),
               "c_maskL": ([128, 128], BF16), "c_maskU": ([128, 128], BF16),
               "c_mask120": ([128, 1], F32),
               "c_ones_col": ([128, 1], BF16), "c_ones_row": ([1, 128], BF16),
               "c_onesrf": ([1, 128], F32), "c_i16": ([16, 16], BF16),
               "c_mc": ([128, 8], F32)}


class Op:
    __slots__ = ("engine", "fn", "deps", "dma", "cc", "inc", "idx", "tag")

    def __init__(self, engine, fn, deps, dma, cc, idx, tag):
        self.engine, self.fn, self.deps = engine, fn, deps
        self.dma, self.cc, self.idx, self.tag = dma, cc, idx, tag
        self.inc = False


RES_ALIAS = {}
MULTI_ALIAS = {"esKH": ("kslot0", "kslot1", "kslot2", "kslot3")}
for _b, _names in {
    "SB16": ["rep_ones", "rep_bc", "blA", "blB", "blC", "gT", "brb", "wrT",
             "fm_y"],
    "SG": ["rp_m"],
    "ES": ["lnb0", "lnb512", "esK", "esV", "esd", "ra_t", "esKC", "pb2",
           "pb3"],
    "PB": ["pb0", "pb1", "rp_t", "rp_u", "rp_w"],
    "ST": ["st_w", "st_c", "st_l", "st_s", "st_m", "br", "gtm", "fm_s",
           "fm_s2", "fm_mu", "fm_va", "fm_m2", "fm_l", "fm_r"],
    "SC_A": ["dq_mf", "dq_fix", "gtge", "gtsu", "fm_q", "fd0", "sgt", "sgt2", "sgt3", "sgt4", "sgt5", "sgt6"],
    "SC_B": ["fd512"],
    "SC_C": ["dq_r", "apl0"],
    "SC_D": ["apl512", "rp_std", "rp_rec", "rp_im"],
    "SC2": ["gbt", "st_gb", "r1A", "r1B", "lrb", "lr32", "gl_t", "dq_sf"],
    "RT": ["RTpo", "RTff", "lnq0", "lnq512"],
    "QB": ["dq_pk", "dq_t0", "dq_t1", "dq_m", "dq_s", "dq_B"],
    "LR": ["lr0"],
}.items():
    for _n in _names:
        RES_ALIAS[_n] = _b


class Emitter:
    ENGINES = ("tensor", "scalar", "vector", "gpsimd", "sync")

    def __init__(self):
        self.ops = []
        self.last_writer = {}
        self.readers = {}

    def op(self, engine, fn, r=(), w=(), dma=False, cc=False, tag=""):
        r = tuple(r) + tuple(RES_ALIAS[x] for x in r if x in RES_ALIAS)
        w = tuple(w) + tuple(RES_ALIAS[x] for x in w if x in RES_ALIAS)
        for x in tuple(r):
            if x in MULTI_ALIAS:
                r = r + MULTI_ALIAS[x]
        for x in tuple(w):
            if x in MULTI_ALIAS:
                w = w + MULTI_ALIAS[x]
        deps = set()
        for x in r:
            lw = self.last_writer.get(x)
            if lw is not None:
                deps.add(lw)
        for x in w:
            lw = self.last_writer.get(x)
            if lw is not None:
                deps.add(lw)
            for rd in self.readers.get(x, ()):
                deps.add(rd)
        i = len(self.ops)
        self.ops.append(Op(engine, fn, deps, dma, cc, i, tag))
        for x in r:
            self.readers.setdefault(x, []).append(i)
        for x in w:
            self.last_writer[x] = i
            self.readers[x] = []
        return i

    def claim(self, *res, tag="claim"):
        self.op("vector", lambda eng: eng.engine_nop(), w=res, tag=tag)

    def emit(self, nc, block):
        NPAR = 6
        ops_by_engine = {e: [o for o in self.ops if o.engine == e]
                         for e in self.ENGINES}
        consumers = {i: [] for i in range(len(self.ops))}
        for o in self.ops:
            for d in o.deps:
                consumers[d].append(o.idx)
        for o in self.ops:
            if o.dma or o.cc:
                o.inc = True
            else:
                for cidx in consumers[o.idx]:
                    c = self.ops[cidx]
                    if (c.engine != o.engine or c.dma or o.dma
                            or o.engine in ("scalar", "vector", "gpsimd")):
                        o.inc = True
                        break

        semkey = {}
        semval = {}
        qcount = {}
        for o in self.ops:
            if not o.inc:
                continue
            if o.dma or o.cc:
                q = o.engine + ("_cc" if o.cc else "_dma")
                n = qcount.get(q, 0)
                qcount[q] = n + 1
                k = f"{q}{n % NPAR}"
                semkey[o.idx] = k
                semval[o.idx] = ((n // NPAR) + 1) * (16 if o.dma else 1)
            else:
                k = o.engine
                n = qcount.get(k, 0) + 1
                qcount[k] = n
                semkey[o.idx] = k
                semval[o.idx] = n
        sems = {}
        for idx, k in semkey.items():
            if k not in sems:
                sems[k] = nc.semaphore("sem_" + k).__enter__()
        emitter = self

        def run_engine(eng, ename):
            waited = {}

            def wait(k, v):
                if waited.get(k, -1) >= v:
                    return
                eng.wait_ge(sems[k], v)
                waited[k] = v

            for o in ops_by_engine[ename]:
                need = {}
                for d in o.deps:
                    p = emitter.ops[d]
                    if (p.engine == ename and not p.dma and not p.cc
                            and not o.dma and ename == "tensor"):
                        continue
                    if not p.inc:
                        raise RuntimeError(f"dep without inc: {p.tag} <- {o.tag}")
                    k, v = semkey[p.idx], semval[p.idx]
                    if v > need.get(k, -1):
                        need[k] = v
                for k, v in sorted(need.items()):
                    wait(k, v)
                if o.fn is None:
                    continue
                if o.inc and (o.dma or o.cc):
                    k, v = semkey[o.idx], semval[o.idx]
                    step = 16 if o.dma else 1
                    if v - step > 0:
                        wait(k, v - step)
                inst = o.fn(eng)
                if o.inc:
                    if inst is None:
                        inst = eng.nop()
                    inst.then_inc(sems[semkey[o.idx]], 16 if o.dma else 1)
            if ename in ("gpsimd", "sync"):
                totals = {}
                for o in ops_by_engine[ename]:
                    if o.inc and (o.dma or o.cc):
                        k, v = semkey[o.idx], semval[o.idx]
                        totals[k] = max(totals.get(k, 0), v)
                for k, v in sorted(totals.items()):
                    wait(k, v)

        for ename, meth in (("sync", block.sync), ("tensor", block.tensor),
                            ("scalar", block.scalar), ("vector", block.vector),
                            ("gpsimd", block.gpsimd)):
            if ops_by_engine[ename]:
                meth(lambda eng, _e=ename: run_engine(eng, _e))


def wspec_all():
    S = {}
    def add(name, rows, cols, kind, **kw):
        S[name] = dict(rows=rows, cols=cols, kind=kind, **kw)
    add("l0f_wq", D, 256, "l0qk", side="q")
    add("l0f_wk", D, 256, "l0qk", side="k")
    add("l0f_wv", D, 256, "cast", src="hs_l0f_wv")
    add("l0f_wp", D, D, "cast", src="L0_ffn_Wp")
    add("l0f_wosl", 256, D, "cast", src="hs_l0f_wosl")
    add("l0a_wq", D, 256, "T", src="hs_l0a_wq")
    add("l0a_wk", D, 256, "T", src="hs_l0a_wk")
    add("l0a_wv", D, 256, "T", src="hs_l0a_wv")
    add("l0a_wosl", 256, D, "cast", src="hs_l0a_wosl")
    add("l1a_wq", D, 256, "T", src="hs_l1a_wq")
    add("l1a_wk", D, 256, "T", src="hs_l1a_wk")
    add("l1a_wv", D, 256, "T", src="hs_l1a_wv")
    add("l1a_wosl", 256, D, "cast", src="hs_l1a_wosl")
    add("l1f_w1", D, 8192, "T", src="L1_ffn_W1")
    add("l1f_w2", 8192, D, "T", src="L1_ffn_W2")
    add("l2a_wq", D, 256, "T", src="hs_l2a_wq")
    add("l2a_wkv", D, 256, "T", src="L2_attn_W_kv")
    add("l2a_wosl", 256, D, "cast_vw", src="hs_l2a_wosl")
    add("l2f_g", 16384, D, "deq_gu", pk="hs_l2f_gpk", sc="hs_l2f_gsc")
    add("l2f_u", 16384, D, "deq_gu", pk="hs_l2f_upk", sc="hs_l2f_usc")
    add("l2f_d", 16384, D, "deq_d", pk="hs_l2f_dpk", sc="hs_l2f_dsc")
    add("l3a_wq", D, 256, "T", src="hs_l3a_wq", scale="rms_w")
    add("l3a_wkv", D, 128, "T", src="hs_l3a_wkv", scale="rms_w")
    add("l3a_wosl", 256, D, "cast", src="hs_l3a_wosl")
    add("l3f_w1", D, 8192, "T3", src="L3_ffn_w1")
    add("l3f_w2", 8192, D, "T3b", src="L3_ffn_w2")
    return S


WSPEC = wspec_all()

SUB_W = {
    "l0f": ["l0f_wq", "l0f_wk", "l0f_wv", "l0f_wp", "l0f_wosl"],
    "l0a": ["l0a_wq", "l0a_wk", "l0a_wv", "l0a_wosl"],
    "l1f": ["l1f_w1", "l1f_w2"],
    "l1a": ["l1a_wq", "l1a_wk", "l1a_wv", "l1a_wosl"],
    "l2f": ["l2f_g", "l2f_u", "l2f_d"],
    "l2a": ["l2a_wq", "l2a_wkv", "l2a_wosl"],
    "l3f": ["l3f_w1", "l3f_w2"],
    "l3a": ["l3a_wq", "l3a_wkv", "l3a_wosl"],
}

SUB_IN = {
    "l0f": ["L0_ffn_Wp", "hs_l0f_wstd", "hs_l0f_wrec", "hs_l0f_waq",
            "hs_l0f_wak", "hs_l0f_waql", "hs_l0f_wakl", "hs_l0f_wv",
            "hs_l0f_wosl"],
    "l0a": ["hs_l0a_wq", "hs_l0a_wk", "hs_l0a_wv", "hs_l0a_wosl",
            "L0_attn_global_tokens", "L0_attn_ln_scale", "L0_attn_ln_bias"],
    "l1f": ["L1_ffn_W1", "L1_ffn_W2", "L1_ffn_Wr", "L1_ffn_br"],
    "l1a": ["hs_l1a_wq", "hs_l1a_wk", "hs_l1a_wv", "hs_l1a_wosl",
            "L1_attn_gamma", "L1_attn_beta"],
    "l2f": ["hs_l2f_gpk", "hs_l2f_gsc", "hs_l2f_upk", "hs_l2f_usc",
            "hs_l2f_dpk", "hs_l2f_dsc"],
    "l2a": ["hs_l2a_wq", "L2_attn_W_kv", "hs_l2a_wkh", "hs_l2a_wvh",
            "hs_l2a_wosl"],
    "l3f": ["L3_ffn_wg", "L3_ffn_w1", "L3_ffn_w2"],
    "l3a": ["hs_l3a_wq", "hs_l3a_wkv", "hs_l3a_wosl", "L3_attn_rms_w"],
}


def make_host_slices(inp, core):
    h0 = 2 * core
    out = {}
    wa = inp["L0_ffn_Wa"]
    out["hs_l0f_waq"] = np.ascontiguousarray(wa[:, 120 * h0:120 * (h0 + 2)])
    out["hs_l0f_wak"] = np.ascontiguousarray(wa[:, 1920 + 120 * h0:1920 + 120 * (h0 + 2)])
    out["hs_l0f_waql"] = np.ascontiguousarray(wa[:, 5888 + 8 * h0:5888 + 8 * (h0 + 2)])
    out["hs_l0f_wakl"] = np.ascontiguousarray(wa[:, 6016 + 8 * h0:6016 + 8 * (h0 + 2)])
    out["hs_l0f_wv"] = np.ascontiguousarray(wa[:, 3840 + 128 * h0:3840 + 128 * (h0 + 2)])
    out["hs_l0f_wstd"] = np.ascontiguousarray(inp["L0_ffn_w_std"][h0:h0 + 2])
    out["hs_l0f_wrec"] = np.ascontiguousarray(inp["L0_ffn_w_rec"][h0:h0 + 2])
    out["hs_l0f_wosl"] = np.ascontiguousarray(inp["L0_ffn_Wp"][128 * h0:128 * (h0 + 2), :])
    for tag, pre in (("l0a", "L0_attn_"), ("l1a", "L1_attn_")):
        out[f"hs_{tag}_wq"] = np.ascontiguousarray(inp[pre + "Wq"][128 * h0:128 * (h0 + 2), :])
        out[f"hs_{tag}_wk"] = np.ascontiguousarray(inp[pre + "Wk"][128 * h0:128 * (h0 + 2), :])
        out[f"hs_{tag}_wv"] = np.ascontiguousarray(inp[pre + "Wv"][128 * h0:128 * (h0 + 2), :])
        out[f"hs_{tag}_wosl"] = np.ascontiguousarray(
            inp[pre + "Wo"][:, 128 * h0:128 * (h0 + 2)].T)
    out["hs_l2a_wq"] = np.ascontiguousarray(inp["L2_attn_W_q"][128 * h0:128 * (h0 + 2), :])
    out["hs_l2a_wkh"] = np.ascontiguousarray(inp["L2_attn_W_k_heads"][h0:h0 + 2])
    out["hs_l2a_wvh"] = np.ascontiguousarray(inp["L2_attn_W_v_heads"][h0:h0 + 2])
    out["hs_l2a_wosl"] = np.ascontiguousarray(
        inp["L2_attn_W_o"][:, 128 * h0:128 * (h0 + 2)].T)
    n0 = 2048 * core
    out["hs_l2f_gpk"] = np.ascontiguousarray(inp["L2_ffn_gate_W_packed"][n0:n0 + 2048, :])
    out["hs_l2f_gsc"] = np.ascontiguousarray(inp["L2_ffn_gate_scales"][n0:n0 + 2048, :])
    out["hs_l2f_upk"] = np.ascontiguousarray(inp["L2_ffn_up_W_packed"][n0:n0 + 2048, :])
    out["hs_l2f_usc"] = np.ascontiguousarray(inp["L2_ffn_up_scales"][n0:n0 + 2048, :])
    out["hs_l2f_dpk"] = np.ascontiguousarray(
        inp["L2_ffn_down_W_packed"][:, 1024 * core:1024 * (core + 1)])
    out["hs_l2f_dsc"] = np.ascontiguousarray(
        inp["L2_ffn_down_scales"][:, 64 * core:64 * (core + 1)])
    win = inp["L3_attn_w_in"]
    g = core
    out["hs_l3a_wq"] = np.ascontiguousarray(win[256 * g:256 * (g + 1), :])
    out["hs_l3a_wkv"] = np.ascontiguousarray(np.concatenate(
        [win[2048 + 64 * g:2048 + 64 * (g + 1), :],
         win[2560 + 64 * g:2560 + 64 * (g + 1), :]], 0))
    out["hs_l3a_wosl"] = np.ascontiguousarray(
        inp["L3_attn_w_out"][:, 256 * g:256 * (g + 1)].T)
    return out


class Ctx:
    def __init__(self, nc, em, subs):
        self.nc, self.em, self.subs = nc, em, subs
        self.inp = {}
        self.wc = {}
        self.sb = {}
        self.ps = {}
        self.dram = {}


def alloc_common(ctx):
    nc = ctx.nc
    A = nc.alloc_sbuf_tensor
    ctx.sb["R"] = A("R", [128, NCH, TPC], F32)
    ctx.sb["XN"] = A("XN", [128, NCH, TPC], BF16)
    ctx.sb["QB"] = A("QB", [128, 16384], BF16)
    ctx.sb["OB"] = A("OB", [128, 16384], BF16)
    ctx.sb["WB"] = A("WB", [128, 8192], BF16)
    ctx.sb["SC"] = A("SC", [128, 2048], F32)
    ctx.sb["SC2"] = A("SC2", [128, 1024], F32)
    ctx.sb["ES"] = A("ES", [128, 1024], BF16)
    ctx.sb["PB"] = A("PB", [128, 1024], BF16)
    ctx.sb["KVA"] = A("KVA", [128, 2384], BF16)
    ctx.sb["RT"] = A("RT", [128, 1024], BF16)
    ctx.sb["DN"] = A("DN", [128, 144], F32)
    ctx.sb["SG"] = A("SG", [128, 48], F32)
    ctx.sb["SB16"] = A("SB16", [128, 256], BF16)
    ctx.sb["ST"] = A("ST", [128, 32], F32)
    ctx.sb["LR"] = A("LR", [64, 512], F32)
    ctx.sb["RP"] = A("RP", [128, 1024], BF16)
    ctx.sb["OF"] = ctx.sb["RP"]
    ctx.sb["LS"] = A("LS", [128, 32], F32)
    for i in range(7):
        ctx.ps[i] = nc.alloc_psum_tensor(f"ps{i}", [128, 512], F32)
    ctx.ps["tr"] = nc.alloc_psum_tensor("ps_tr", [128, 1024], BF16)
    for name, (shape, dt) in SBUF_CONSTS.items():
        ctx.sb[name] = A("s_" + name, shape, dt)
        def f(eng, n=name):
            return eng.dma_start(out=ctx.sb[n][:], in_=ctx.inp[n])
        ctx.em.op("sync", f, w=(name,), dma=True, tag="ldc")


def ln_to_xn(ctx, kind="ln", ln_scale=False, r1_rows=False):
    em = ctx.em
    R, XN, SC, LR = ctx.sb["R"], ctx.sb["XN"], ctx.sb["SC"], ctx.sb["LR"]
    ones_col = ctx.sb["c_ones_col"]
    onesrf = ctx.sb["c_onesrf"]
    vscale = {"ln": 1.0, "ln_ln": 1.0 + 1e-5, "ln_rms": 1.0 + 1e-6}[kind]
    mccol = {"ln": 4, "ln_ln": 5, "ln_rms": 6}[kind]
    for fat in range(2):
        t0 = fat * 512
        for c in range(NCH):
            sl = (c % 2) * 512
            def fcp2(eng, c=c, t0=t0, sl=sl):
                return eng.tensor_copy(out=ctx.sb["ES"][:, sl:sl + 512],
                                       in_=R[:, c, t0:t0 + 512])
            em.op("vector", fcp2, r=("R",), w=(f"lnb{sl}",), tag="ln.cp")
            def fsq(eng, c=c, sl=sl):
                return eng.activation(out=ctx.sb["RT"][:, sl:sl + 512],
                                      in_=ctx.sb["ES"][:, sl:sl + 512],
                                      func=AF.Square)
            em.op("scalar", fsq, r=(f"lnb{sl}",), w=(f"lnq{sl}",), tag="ln.sq")
            def fs1(eng, c=c, sl=sl):
                return eng.matmul(ctx.ps[5][0:1, 0:512], lhsT=ones_col[:],
                                  rhs=ctx.sb["ES"][:, sl:sl + 512],
                                  start=(c == 0), stop=(c == NCH - 1))
            em.op("tensor", fs1, r=(f"lnb{sl}", "c_ones_col"), w=("ps5",),
                  tag="ln.s1")
            def fs2(eng, c=c, sl=sl):
                return eng.matmul(ctx.ps[6][0:1, 0:512], lhsT=ones_col[:],
                                  rhs=ctx.sb["RT"][:, sl:sl + 512],
                                  start=(c == 0), stop=(c == NCH - 1))
            em.op("tensor", fs2, r=(f"lnq{sl}", "c_ones_col"), w=("ps6",),
                  tag="ln.s2")
        def fmu(eng):
            return eng.tensor_scalar(out=ctx.ps[5][0:1, 0:512],
                                     in0=ctx.ps[5][0:1, 0:512],
                                     scalar1=1.0 / D, scalar2=None, op0=ALU.mult)
        em.op("vector", fmu, r=("ps5",), w=("ps5",), tag="ln.mu")
        def fes(eng):
            return eng.tensor_scalar(out=ctx.ps[4][0:1, 0:512],
                                     in0=ctx.ps[6][0:1, 0:512],
                                     scalar1=1.0 / D, scalar2=None, op0=ALU.mult)
        em.op("vector", fes, r=("ps6",), w=("ps4",), tag="ln.es")
        def fcesq(eng):
            return eng.tensor_copy(out=LR[0:1, :], in_=ctx.ps[4][0:1, 0:512])
        em.op("vector", fcesq, r=("ps4",), w=("lr0",), tag="ln.ce")
        def fmu2(eng):
            return eng.activation(out=ctx.ps[6][0:1, 0:512],
                                  in_=ctx.ps[5][0:1, 0:512], func=AF.Square)
        em.op("scalar", fmu2, r=("ps5",), w=("ps6",), tag="ln.mu2")
        def fvar(eng):
            return eng.tensor_tensor(out=ctx.ps[6][0:1, 0:512], in0=LR[0:1, :],
                                     in1=ctx.ps[6][0:1, 0:512], op=ALU.subtract)
        em.op("vector", fvar, r=("lr0", "ps6"), w=("ps6",), tag="ln.var")
        def fln(eng):
            return eng.activation(out=ctx.ps[4][0:1, 0:512],
                                  in_=ctx.ps[6][0:1, 0:512], func=AF.Ln,
                                  bias=ctx.sb["c_mc"][0:1, mccol:mccol + 1],
                                  scale=vscale)
        em.op("scalar", fln, r=("ps6", "c_mc"), w=("ps4",), tag="ln.ln")
        def fexp(eng):
            return eng.activation(out=ctx.ps[3][0:1, 0:512],
                                  in_=ctx.ps[4][0:1, 0:512], func=AF.Exp,
                                  scale=-0.5)
        em.op("scalar", fexp, r=("ps4",), w=("ps3",), tag="ln.exp")
        def fcr(eng):
            return eng.tensor_copy(out=LR[0:1, :], in_=ctx.ps[3][0:1, 0:512])
        em.op("vector", fcr, r=("ps3", "lr0"), w=("lr0",), tag="ln.cr")
        def fbA(eng):
            return eng.matmul(ctx.ps[2][:, 0:512], lhsT=onesrf[:],
                              rhs=LR[0:1, :], start=True, stop=True)
        em.op("tensor", fbA, r=("lr0", "c_onesrf"), w=("ps2",), tag="ln.bA")
        def fcA(eng):
            return eng.tensor_copy(out=SC[:, 0:512], in_=ctx.ps[2][:, 0:512])
        em.op("vector", fcA, r=("ps2",), w=("SC_A",), tag="ln.cA")
        def fs2r(eng):
            return eng.tensor_scalar(out=ctx.sb["SC2"][0:1, 512:1024], in0=ctx.ps[5][0:1, 0:512],
                                     scalar1=-1.0, scalar2=None, op0=ALU.mult)
        em.op("vector", fs2r, r=("ps5",), w=("lr32",), tag="ln.s2a")
        def fs2b(eng):
            return eng.tensor_mul(out=ctx.sb["SC2"][0:1, 512:1024],
                                  in0=ctx.sb["SC2"][0:1, 512:1024],
                                  in1=LR[0:1, :])
        em.op("vector", fs2b, r=("lr32", "lr0"), w=("lr32",), tag="ln.s2b")
        def fbB(eng):
            return eng.matmul(ctx.ps[2][:, 0:512], lhsT=onesrf[:],
                              rhs=ctx.sb["SC2"][0:1, 512:1024], start=True, stop=True)
        em.op("tensor", fbB, r=("lr32", "c_onesrf"), w=("ps2",), tag="ln.bB")
        def fcB(eng):
            return eng.tensor_copy(out=SC[:, 512:1024], in_=ctx.ps[2][:, 0:512])
        em.op("vector", fcB, r=("ps2",), w=("SC_B",), tag="ln.cB")
        if r1_rows:
            def fln1(eng):
                return eng.activation(out=ctx.ps[4][0:1, 0:512],
                                      in_=ctx.ps[6][0:1, 0:512], func=AF.Ln,
                                      bias=ctx.sb["c_mc"][0:1, 4:5])
            em.op("scalar", fln1, r=("ps6", "c_mc"), w=("ps4",), tag="ln.l1")
            def fex1(eng):
                return eng.activation(out=ctx.ps[3][0:1, 0:512],
                                      in_=ctx.ps[4][0:1, 0:512], func=AF.Exp,
                                      scale=-0.5)
            em.op("scalar", fex1, r=("ps4",), w=("ps3",), tag="ln.e1")
            def fp1(eng):
                return eng.tensor_scalar(out=LR[0:1, :], in0=ctx.ps[3][0:1, 0:512],
                                         scalar1=1.0, scalar2=None, op0=ALU.add)
            em.op("vector", fp1, r=("ps3", "lr0"), w=("lr0",), tag="ln.p1")
            def fbA1(eng):
                return eng.matmul(ctx.ps[2][:, 0:512], lhsT=onesrf[:],
                                  rhs=LR[0:1, :], start=True, stop=True)
            em.op("tensor", fbA1, r=("lr0", "c_onesrf"), w=("ps2",), tag="ln.b1")
            def fcA1(eng):
                return eng.tensor_copy(out=ctx.sb["SC2"][:, 0:512],
                                       in_=ctx.ps[2][:, 0:512])
            em.op("vector", fcA1, r=("ps2",), w=("r1A",), tag="ln.c1")
            def fs7a(eng):
                return eng.tensor_scalar(out=ctx.sb["SC2"][0:1, 512:1024],
                                         in0=ctx.ps[5][0:1, 0:512],
                                         scalar1=-1.0, scalar2=None, op0=ALU.mult)
            em.op("vector", fs7a, r=("ps5", "lr32"), w=("lr32",), tag="ln.s7a")
            def fs7b(eng):
                return eng.tensor_tensor(out=ctx.sb["SC2"][0:1, 512:1024],
                                         in0=ctx.sb["SC2"][0:1, 512:1024],
                                         in1=ctx.ps[3][0:1, 0:512], op=ALU.mult)
            em.op("vector", fs7b, r=("lr32", "ps3"), w=("lr32",), tag="ln.s7b")
            def fbB1(eng):
                return eng.matmul(ctx.ps[2][:, 0:512], lhsT=onesrf[:],
                                  rhs=ctx.sb["SC2"][0:1, 512:1024], start=True, stop=True)
            em.op("tensor", fbB1, r=("lr32", "c_onesrf"), w=("ps2",), tag="ln.b2")
            def fcB1(eng):
                return eng.tensor_copy(out=ctx.sb["SC2"][:, 512:1024],
                                       in_=ctx.ps[2][:, 0:512])
            em.op("vector", fcB1, r=("ps2",), w=("r1B",), tag="ln.c2")
        for c in range(NCH):
            ename = "vector" if c % 2 == 0 else "gpsimd"
            slot = 0 if ename == "vector" else 512
            def fa1(eng, c=c, t0=t0, slot=slot):
                return eng.tensor_mul(out=ctx.sb["SC"][:, 1024 + slot:1536 + slot],
                                      in0=R[:, c, t0:t0 + 512],
                                      in1=SC[:, 0:512])
            em.op(ename, fa1, r=("R", "SC_A"), w=(f"apl{slot}",), tag="ln.a1")
            if not ln_scale:
                def fa2(eng, c=c, t0=t0, slot=slot):
                    return eng.tensor_add(out=XN[:, c, t0:t0 + 512],
                                          in0=ctx.sb["SC"][:, 1024 + slot:1536 + slot],
                                          in1=SC[:, 512:1024])
                em.op(ename, fa2, r=(f"apl{slot}", "SC_B"), w=("XN",), tag="ln.a2")
            else:
                def fa2(eng, c=c, t0=t0, slot=slot):
                    return eng.tensor_add(out=ctx.sb["SC"][:, 1024 + slot:1536 + slot],
                                          in0=ctx.sb["SC"][:, 1024 + slot:1536 + slot],
                                          in1=SC[:, 512:1024])
                em.op(ename, fa2, r=(f"apl{slot}", "SC_B"), w=(f"apl{slot}",),
                      tag="ln.a2s")
                def fa3(eng, c=c, t0=t0, slot=slot):
                    return eng.tensor_scalar(out=XN[:, c, t0:t0 + 512],
                                             in0=ctx.sb["SC"][:, 1024 + slot:1536 + slot],
                                             scalar1=ctx.sb["LS"][:, c:c + 1],
                                             scalar2=ctx.sb["LS"][:, 16 + c:17 + c],
                                             op0=ALU.mult, op1=ALU.add)
                em.op(ename, fa3, r=(f"apl{slot}", "LS"), w=("XN",), tag="ln.a3")
        if r1_rows:
            for c in range(NCH):
                ename = "vector" if c % 2 == 0 else "gpsimd"
                def fu1(eng, c=c, t0=t0):
                    return eng.tensor_mul(out=R[:, c, t0:t0 + 512],
                                          in0=R[:, c, t0:t0 + 512],
                                          in1=ctx.sb["SC2"][:, 0:512])
                em.op(ename, fu1, r=("R", "r1A", "XN"), w=("R",), tag="ru.1")
                def fu2(eng, c=c, t0=t0):
                    return eng.tensor_add(out=R[:, c, t0:t0 + 512],
                                          in0=R[:, c, t0:t0 + 512],
                                          in1=ctx.sb["SC2"][:, 512:1024])
                em.op(ename, fu2, r=("R", "r1B"), w=("R",), tag="ru.2")


def add_tokmajor_dram_to_R(ctx, dram_ap, scale, dt_=F32, extra_r=()):
    em = ctx.em
    R = ctx.sb["R"]
    for ts_ in range(8):
        for g0 in range(0, 16, 8):
            def fl(eng, ts_=ts_, g0=g0):
                return eng.dma_start(out=ctx.sb["RT"][:, 0:1024],
                                     in_=dram_ap[ts_ * 128:(ts_ + 1) * 128,
                                                 g0 * 128:(g0 + 8) * 128])
            em.op("gpsimd", fl, r=("RT",) + extra_r, w=("RT",), dma=True,
                  tag="ra.ld")
            for j in range(8):
                def ft(eng, j=j):
                    return eng.transpose(ctx.ps["tr"][:, j * 128:(j + 1) * 128],
                                         ctx.sb["RT"][:, j * 128:(j + 1) * 128],
                                         ctx.sb["c_ident"][:])
                em.op("tensor", ft, r=("RT", "c_ident"), w=("ps_tr",), tag="ra.t")
            for j in range(8):
                c = g0 + j
                fat, t0 = (0, 0) if ts_ < 4 else (1, 512)
                tt0 = (ts_ % 4) * 128
                if scale == 1.0:
                    def fa(eng, c=c, t0=t0, tt0=tt0, j=j):
                        return eng.tensor_tensor(
                            out=R[:, c, t0 + tt0:t0 + tt0 + 128],
                            in0=ctx.ps["tr"][:, j * 128:(j + 1) * 128],
                            in1=R[:, c, t0 + tt0:t0 + tt0 + 128], op=ALU.add)
                    em.op("vector", fa, r=("ps_tr", "R"), w=("R",), tag="ra.a")
                else:
                    def fa0(eng, j=j):
                        return eng.tensor_scalar(
                            out=ctx.sb["ES"][:, 0:128],
                            in0=ctx.ps["tr"][:, j * 128:(j + 1) * 128],
                            scalar1=scale, scalar2=None, op0=ALU.mult)
                    em.op("vector", fa0, r=("ps_tr",), w=("ra_t",), tag="ra.s")
                    def fa(eng, c=c, t0=t0, tt0=tt0):
                        return eng.tensor_tensor(
                            out=R[:, c, t0 + tt0:t0 + tt0 + 128],
                            in0=ctx.sb["ES"][:, 0:128],
                            in1=R[:, c, t0 + tt0:t0 + tt0 + 128], op=ALU.add)
                    em.op("vector", fa, r=("ra_t", "R"), w=("R",), tag="ra.a")




def _emit_T_block(ctx, name, src_block, dst, dst_r0, dst_c0, n_in, n_out,
                  scale=None):
    em = ctx.em
    OBf = ctx.sb["OB"]
    for rc in range(n_in // 128):
        def fl(eng, rc=rc):
            return eng.dma_start(
                out=ctx.sb["WB"][:, 0:n_out].rearrange("p (c f) -> p c f", f=128),
                in_=src_block[:, rc * 128:(rc + 1) * 128]
                    .rearrange("(c p) f -> p c f", p=128))
        em.op("gpsimd", fl, w=("WB",), dma=True, tag="pT.ld")
        for fc0 in range(0, n_out // 128, 8):
            g = min(8, n_out // 128 - fc0)
            for j in range(g):
                def ft(eng, fc=fc0 + j, j=j):
                    return eng.transpose(
                        ctx.ps["tr"][:, j * 128:(j + 1) * 128],
                        ctx.sb["WB"][:, fc * 128:(fc + 1) * 128],
                        ctx.sb["c_ident"][:])
                em.op("tensor", ft, r=("WB", "c_ident"), w=("ps_tr",), tag="pT.t")
            def fcp(eng, fc0=fc0, g=g):
                return eng.tensor_copy(out=OBf[:, fc0 * 128:(fc0 + g) * 128],
                                       in_=ctx.ps["tr"][:, 0:g * 128])
            em.op("vector", fcp, r=("ps_tr",), w=("OB_P",), tag="pT.cp")
        if scale is not None:
            sc_ap, sc_res = scale(dst_r0 // 128 + rc)
            def fsc(eng, sc_ap=sc_ap):
                return eng.tensor_scalar(out=OBf[:, 0:n_out], in0=OBf[:, 0:n_out],
                                         scalar1=sc_ap, scalar2=None, op0=ALU.mult)
            em.op("vector", fsc, r=("OB_P", sc_res), w=("OB_P",), tag="pT.sc")
        def fst(eng, rc=rc):
            return eng.dma_start(
                out=dst[dst_r0 + rc * 128:dst_r0 + (rc + 1) * 128,
                        dst_c0:dst_c0 + n_out],
                in_=OBf[:, 0:n_out])
        em.op("sync", fst, r=("OB_P",), w=(("wc", name),), dma=True, tag="pT.st")


def nib_decode(ctx, b_res, plane, hw):
    em = ctx.em
    w = 512
    QBi = ctx.sb["QB"][:].bitcast(I32)
    OBv = ctx.sb["OB"][:].bitcast(F32)
    SC, SC2 = ctx.sb["SC"], ctx.sb["SC2"]
    o0 = hw * 1024
    out_ap = OBv[:, o0:o0 + 1024].rearrange("p (a b2) -> p a b2", b2=2)[:, :, plane:plane + 1]
    if plane == 1:
        def fsh(eng, hw=hw):
            return eng.tensor_scalar(out=QBi[:, 3072 + hw * w:3072 + (hw + 1) * w],
                                     in0=QBi[:, hw * w:(hw + 1) * w],
                                     scalar1=4, scalar2=None,
                                     op0=ALU.logical_shift_right)
        em.op("vector", fsh, r=(b_res,), w=("dq_B",), tag="dq.sh")
        B, Bres = QBi[:, 3072 + hw * w:3072 + (hw + 1) * w], "dq_B"
    else:
        B, Bres = QBi[:, hw * w:(hw + 1) * w], b_res
    t0 = QBi[:, 1024 + hw * w:1024 + (hw + 1) * w]
    t1 = QBi[:, 2048 + hw * w:2048 + (hw + 1) * w]
    def f0(eng):
        return eng.tensor_scalar(out=t0, in0=B, scalar1=7, scalar2=22,
                                 op0=ALU.bitwise_and, op1=ALU.logical_shift_left)
    em.op("vector", f0, r=(Bres,), w=("dq_t0",), tag="dq.0")
    def f1(eng):
        return eng.tensor_scalar(out=t1, in0=B, scalar1=8, scalar2=28,
                                 op0=ALU.bitwise_and, op1=ALU.logical_shift_left)
    em.op("vector", f1, r=(Bres,), w=("dq_t1",), tag="dq.1")
    def f2(eng):
        return eng.tensor_tensor(out=t0, in0=t0, in1=t1, op=ALU.add)
    em.op("vector", f2, r=("dq_t0", "dq_t1"), w=("dq_t0",), tag="dq.2")
    def f3(eng):
        return eng.tensor_tensor(out=t0, in0=t0, in1=ctx.sb["c_e32"][:],
                                 op=ALU.add)
    em.op("vector", f3, r=("dq_t0", "c_e32"), w=("dq_t0",), tag="dq.3")
    def f4(eng):
        return eng.tensor_scalar(out=t1, in0=B, scalar1=7, scalar2=None,
                                 op0=ALU.bitwise_and)
    em.op("vector", f4, r=(Bres, "dq_t1"), w=("dq_m",), tag="dq.4")
    def f5(eng):
        return eng.copy(out=SC[:, 0:w], in_=t1)
    em.op("scalar", f5, r=("dq_m",), w=("dq_mf",), tag="dq.5")
    def f6(eng):
        return eng.activation(out=SC[:, 1024:1024 + w], in_=SC[:, 0:w],
                              func=AF.Relu, scale=-1.0,
                              bias=ctx.sb["c_mc"][:, 7:8])
    em.op("scalar", f6, r=("dq_mf", "c_mc"), w=("dq_r",), tag="dq.6")
    def f7(eng):
        return eng.tensor_scalar(out=t1, in0=B, scalar1=8, scalar2=None,
                                 op0=ALU.bitwise_and)
    em.op("vector", f7, r=(Bres, "dq_mf"), w=("dq_s",), tag="dq.7")
    def f8(eng):
        return eng.copy(out=SC2[:, 0:w], in_=t1)
    em.op("scalar", f8, r=("dq_s",), w=("dq_sf",), tag="dq.8")
    def f9(eng):
        return eng.tensor_scalar(out=SC2[:, 0:w], in0=SC2[:, 0:w],
                                 scalar1=1.0 / 16, scalar2=-0.25,
                                 op0=ALU.mult, op1=ALU.add)
    em.op("vector", f9, r=("dq_sf",), w=("dq_sf",), tag="dq.9")
    def f10(eng):
        return eng.tensor_mul(out=SC[:, 0:w], in0=SC[:, 1024:1024 + w],
                              in1=SC2[:, 0:w])
    em.op("vector", f10, r=("dq_r", "dq_sf"), w=("dq_fix",), tag="dq.10")
    def f11(eng):
        return eng.tensor_tensor(out=out_ap, in0=t0.bitcast(F32).unsqueeze(2),
                                 in1=SC[:, 0:w].unsqueeze(2), op=ALU.add)
    em.op("vector", f11, r=("dq_t0", "dq_fix"), w=("OB_P",), tag="dq.11")


def prep_matrix(ctx, name):
    em = ctx.em
    spec = WSPEC[name]
    kind = spec["kind"]
    rows, cols = spec["rows"], spec["cols"]
    dst = ctx.wc[name]
    if kind in ("cast", "cast_vw"):
        src = ctx.inp[spec["src"]]
        def f(eng):
            return eng.dma_start(out=dst[:, :], in_=src)
        em.op("gpsimd", f, w=(("wc", name),), dma=True, tag="cast." + name)
        if kind == "cast_vw":
            for hh in range(2):
                def fl(eng, hh=hh):
                    return eng.dma_start(out=ctx.sb["WB"][:, 0:D],
                                         in_=dst[hh * 128:(hh + 1) * 128, :])
                em.op("sync", fl, r=(("wc", name),), w=("WB",), dma=True,
                      tag="vw.ld")
                def fs(eng, hh=hh):
                    return eng.tensor_scalar(out=ctx.sb["WB"][:, 0:D],
                                             in0=ctx.sb["WB"][:, 0:D],
                                             scalar1=ctx.sb["SG"][:, 4 + hh:5 + hh],
                                             scalar2=None, op0=ALU.mult)
                em.op("vector", fs, r=("WB", "SG"), w=("WB",), tag="vw.sc")
                def fst(eng, hh=hh):
                    return eng.dma_start(out=dst[hh * 128:(hh + 1) * 128, :],
                                         in_=ctx.sb["WB"][:, 0:D])
                em.op("sync", fst, r=("WB",), w=(("wc", name),), dma=True,
                      tag="vw.st")
        return
    if kind == "T":
        scale = None
        if spec.get("scale") == "rms_w":
            scale = lambda rc: (ctx.sb["SG"][:, 16 + rc:17 + rc], "SG")
        _emit_T_block(ctx, name, ctx.inp[spec["src"]], dst, 0, 0, rows, cols,
                      scale)
        return
    if kind == "T3":
        src = ctx.inp[spec["src"]].rearrange("e h d -> (e h) d")
        _emit_T_block(ctx, name, src, dst, 0, 0, rows, cols)
        return
    if kind == "T3b":
        src = ctx.inp[spec["src"]].rearrange("e o h -> (e o) h")
        for e in range(8):
            _emit_T_block(ctx, name, src[e * 2048:(e + 1) * 2048, :], dst,
                          e * 1024, 0, 1024, 2048)
        return
    if kind == "l0qk":
        side = spec["side"]
        std_src = ctx.inp["hs_l0f_waq" if side == "q" else "hs_l0f_wak"]
        ta = ctx.inp["hs_l0f_waql" if side == "q" else "hs_l0f_wakl"]
        tb = ctx.inp["hs_l0f_wakl" if side == "q" else "hs_l0f_waql"]
        for rc in range(16):
            for hh in range(2):
                def fs(eng, rc=rc, hh=hh):
                    return eng.dma_start(
                        out=dst[rc * 128:(rc + 1) * 128,
                                hh * 128:hh * 128 + 120],
                        in_=std_src[rc * 128:(rc + 1) * 128,
                                    120 * hh:120 * hh + 120])
                em.op("gpsimd", fs, w=(("wc", name),), dma=True, tag="l0.std")
            def fta(eng, rc=rc):
                return eng.dma_start(out=ctx.sb["SB16"][:, 0:16],
                                     in_=ta[rc * 128:(rc + 1) * 128, :])
            em.op("gpsimd", fta, w=("blA",), dma=True, tag="l0.lda")
            def ftb(eng, rc=rc):
                return eng.dma_start(out=ctx.sb["SB16"][:, 16:32],
                                     in_=tb[rc * 128:(rc + 1) * 128, :])
            em.op("gpsimd", ftb, w=("blB",), dma=True, tag="l0.ldb")
            def fb1(eng):
                return eng.tensor_sub(out=ctx.sb["SB16"][:, 32:48],
                                      in0=ctx.sb["SB16"][:, 16:32],
                                      in1=ctx.sb["SB16"][:, 0:16])
            em.op("vector", fb1, r=("blA", "blB"), w=("blC",), tag="l0.sub")
            for hh in range(2):
                def fb2(eng, hh=hh):
                    return eng.tensor_scalar(
                        out=ctx.sb["SB16"][:, 32 + 8 * hh:40 + 8 * hh],
                        in0=ctx.sb["SB16"][:, 32 + 8 * hh:40 + 8 * hh],
                        scalar1=ctx.sb["SG"][:, 8 + hh:9 + hh],
                        scalar2=None, op0=ALU.mult)
                em.op("vector", fb2, r=("blC", "SG"), w=("blC",), tag="l0.m")
            def fb3(eng):
                return eng.tensor_add(out=ctx.sb["SB16"][:, 32:48],
                                      in0=ctx.sb["SB16"][:, 32:48],
                                      in1=ctx.sb["SB16"][:, 0:16])
            em.op("vector", fb3, r=("blC", "blA"), w=("blC",), tag="l0.add")
            for hh in range(2):
                def fst(eng, rc=rc, hh=hh):
                    return eng.dma_start(
                        out=dst[rc * 128:(rc + 1) * 128,
                                hh * 128 + 120:hh * 128 + 128],
                        in_=ctx.sb["SB16"][:, 32 + 8 * hh:40 + 8 * hh])
                em.op("sync", fst, r=("blC",), w=(("wc", name),), dma=True,
                      tag="l0.st")
        return
    if kind in ("deq_gu", "deq_d"):
        pk = ctx.inp[spec["pk"]]
        sc = ctx.inp[spec["sc"]]
        gin = ctx.dram["gin_" + name]
        QBi = ctx.sb["QB"][:].bitcast(I32)
        OBv = ctx.sb["OB"][:].bitcast(F32)
        for c in range(16):
            def fl(eng, c=c):
                return eng.dma_start(out=QBi[:, 0:1024],
                                     in_=pk[c * 128:(c + 1) * 128, :])
            em.op("sync", fl, r=("dq_t0", "dq_m", "dq_s", "dq_B"), w=("dq_pk",),
                  dma=True, tag="dq.ldpk")
            def fls(eng, c=c):
                return eng.dma_start(out=ctx.sb["DN"][:, 0:64],
                                     in_=sc[c * 128:(c + 1) * 128, :])
            em.op("sync", fls, r=("dq_sc",), w=("dq_sc",), dma=True,
                  tag="dq.ldsc")
            for hw in range(2):
                nib_decode(ctx, "dq_pk", 0, hw)
                nib_decode(ctx, "dq_pk", 1, hw)
            for b0 in range(0, 64, 8):
                def fsc(eng, b0=b0):
                    for b in range(b0, b0 + 8):
                        inst = eng.tensor_scalar(
                            out=OBv[:, 32 * b:32 * (b + 1)],
                            in0=OBv[:, 32 * b:32 * (b + 1)],
                            scalar1=ctx.sb["DN"][:, b:b + 1], scalar2=None,
                            op0=ALU.mult)
                    return inst
                em.op("vector", fsc, r=("OB_P", "dq_sc"), w=("OB_P",),
                      tag="dq.sc")
            def fcst(eng):
                return eng.tensor_copy(out=ctx.sb["WB"][:, 0:2048],
                                       in_=OBv[:, 0:2048])
            em.op("vector", fcst, r=("OB_P",), w=("WB",), tag="dq.cst")
            for j0 in range(0, 16, 8):
                for j in range(8):
                    def ft(eng, j0=j0, j=j):
                        return eng.transpose(
                            ctx.ps["tr"][:, j * 128:(j + 1) * 128],
                            ctx.sb["WB"][:, (j0 + j) * 128:(j0 + j + 1) * 128],
                            ctx.sb["c_ident"][:])
                    em.op("tensor", ft, r=("WB", "c_ident"), w=("ps_tr",),
                          tag="dq.t")
                def fcp(eng, j0=j0):
                    return eng.tensor_copy(
                        out=ctx.sb["WB"][:, 2048 + j0 * 128:2048 + (j0 + 8) * 128],
                        in_=ctx.ps["tr"][:, 0:1024])
                em.op("vector", fcp, r=("ps_tr",), w=("WB2",), tag="dq.cp")
            def fst(eng, c=c):
                return eng.dma_start(
                    out=gin[:, c * 128:(c + 1) * 128]
                        .rearrange("(a p) f -> p a f", p=128),
                    in_=ctx.sb["WB"][:, 2048:4096]
                        .rearrange("p (a f) -> p a f", f=128))
            em.op("sync", fst, r=("WB2",), w=(("gin", name),), dma=True,
                  tag="dq.st")
        def fcc(eng):
            return eng.collective_compute(
                "AllGather", ALU.bypass, replica_groups=[list(range(NCORE))],
                ins=[gin.opt()], outs=[ctx.wc[name].opt()])
        em.op("gpsimd", fcc, r=(("gin", name),), w=(("wc", name),), cc=True,
              tag="dq.ag")
        return
    raise NotImplementedError(kind)


def prep_tables(ctx, subs):
    em = ctx.em
    ST, SG = ctx.sb["ST"], ctx.sb["SG"]
    bases = {BASE.get(s, s) for s in subs}
    if "l0f" in bases:
        def f0(eng):
            return eng.dma_start(out=ctx.sb["SC"][:, 0:2],
                                 in_=ctx.inp["hs_l0f_wstd"].unsqueeze(0)
                                 .broadcast_to((128, 2)))
        em.op("gpsimd", f0, r=("SC_A",), w=("sgt",), dma=True, tag="sg.ld")
        def f0b(eng):
            return eng.dma_start(out=ctx.sb["SC"][:, 2:4],
                                 in_=ctx.inp["hs_l0f_wrec"].unsqueeze(0)
                                 .broadcast_to((128, 2)))
        em.op("gpsimd", f0b, r=("SC_A",), w=("sgt",), dma=True, tag="sg.ld2")
        def f1(eng):
            return eng.tensor_scalar(out=ctx.sb["SC"][:, 4:8],
                                     in0=ctx.sb["SC"][:, 0:4],
                                     scalar1=1e-8, scalar2=None, op0=ALU.max)
        em.op("vector", f1, r=("sgt",), w=("sgt2",), tag="sg.clip")
        def f2(eng):
            return eng.activation(out=ctx.sb["SC"][:, 8:12],
                                  in_=ctx.sb["SC"][:, 4:8], func=AF.Ln)
        em.op("scalar", f2, r=("sgt2",), w=("sgt3",), tag="sg.ln")
        def f3(eng):
            return eng.activation(out=ctx.sb["SC"][:, 12:16],
                                  in_=ctx.sb["SC"][:, 8:12], func=AF.Exp,
                                  scale=0.5)
        em.op("scalar", f3, r=("sgt3",), w=("sgt4",), tag="sg.exp")
        def f4(eng):
            return eng.tensor_scalar(out=SG[:, 8:10], in0=ctx.sb["SC"][:, 2:4],
                                     scalar1=0.1, scalar2=None, op0=ALU.is_gt)
        em.op("vector", f4, r=("sgt",), w=("SG",), tag="sg.isgt")
        def f5(eng):
            return eng.tensor_scalar(out=SG[:, 0:2], in0=ctx.sb["SC"][:, 12:14],
                                     scalar1=ctx.sb["c_mask120"][:],
                                     scalar2=None, op0=ALU.mult)
        em.op("vector", f5, r=("sgt4", "c_mask120"), w=("SG",), tag="sg.a")
        def f6(eng):
            return eng.tensor_scalar(out=ctx.sb["SC"][:, 16:18],
                                     in0=ctx.sb["SC"][:, 14:16],
                                     scalar1=ctx.sb["c_mask120"][:],
                                     scalar2=None, op0=ALU.mult)
        em.op("vector", f6, r=("sgt4", "c_mask120"), w=("sgt5",), tag="sg.b")
        def f7(eng):
            return eng.tensor_sub(out=ctx.sb["SC"][:, 18:20],
                                  in0=ctx.sb["SC"][:, 14:16],
                                  in1=ctx.sb["SC"][:, 16:18])
        em.op("vector", f7, r=("sgt4", "sgt5"), w=("sgt6",), tag="sg.c")
        def f8(eng):
            return eng.tensor_add(out=SG[:, 0:2], in0=SG[:, 0:2],
                                  in1=ctx.sb["SC"][:, 18:20])
        em.op("vector", f8, r=("SG", "sgt6"), w=("SG",), tag="sg.d")
    if "l2a" in bases:
        for hh in range(2):
            for which, col, srcn in (("k", 2, "hs_l2a_wkh"), ("v", 4, "hs_l2a_wvh")):
                def fl(eng, hh=hh, srcn=srcn):
                    return eng.dma_start(out=ctx.sb["SC"][:, 0:128],
                                         in_=ctx.inp[srcn][hh])
                em.op("sync", fl, r=("SC_A",), w=("SC_A",), dma=True, tag="kv.ld")
                def fr(eng, hh=hh, col=col):
                    return eng.reduce_sum(out=SG[:, col + hh:col + hh + 1],
                                          in_=ctx.sb["SC"][:, 0:128], axis=AXX)
                em.op("vector", fr, r=("SC_A",), w=("SG",), tag="kv.red")
    if "l3a" in bases:
        def fl(eng):
            return eng.dma_start(out=ctx.sb["SG"][:, 16:32],
                                 in_=ctx.inp["L3_attn_rms_w"]
                                 .rearrange("(c p) -> p c", p=128),
                                 allow_slow_non_contiguous=True)
        em.op("sync", fl, w=("SG",), dma=True, tag="rw.ld")
    if "l0a" in bases:
        def fl(eng):
            return eng.dma_start(out=ctx.sb["LS"][:, 0:16],
                                 in_=ctx.inp["L0_attn_ln_scale"]
                                 .rearrange("(c p) -> p c", p=128),
                                 allow_slow_non_contiguous=True)
        em.op("sync", fl, w=("LS",), dma=True, tag="ls.ld")
        def fb(eng):
            return eng.dma_start(out=ctx.sb["LS"][:, 16:32],
                                 in_=ctx.inp["L0_attn_ln_bias"]
                                 .rearrange("(c p) -> p c", p=128),
                                 allow_slow_non_contiguous=True)
        em.op("sync", fb, w=("LS",), dma=True, tag="ls.ld2")


def rope_apply(ctx, ap, res, kind, tagp=""):
    em = ctx.em
    RP, PB = ctx.sb["RP"], ctx.sb["PB"]
    groups = [(0,)] if kind in ("l2", "l3k") else [(0,), (64,)]
    for (r0,) in groups:
        b1, b2 = r0, r0 + 32
        x1 = ap[b1:b1 + 32, :]
        x2 = ap[b2:b2 + 32, :]
        c1 = RP[b1:b1 + 32, 0:512]
        c2 = RP[b2:b2 + 32, 0:512]
        sx1 = RP[b1:b1 + 32, 512:1024]
        sx2 = RP[b2:b2 + 32, 512:1024]
        t_ = PB[b1:b1 + 32, 0:512]
        u_ = PB[b1:b1 + 32, 512:1024]
        w_ = PB[b2:b2 + 32, 0:512]
        def f0(eng, x1=x1, t_=t_):
            return eng.tensor_copy(out=t_, in_=x1)
        em.op("vector", f0, r=(res,), w=("rp_t",), tag=tagp + "rp.cp")
        def f1(eng, x2=x2, sx2=sx2, u_=u_):
            return eng.tensor_mul(out=u_, in0=x2, in1=sx2)
        em.op("vector", f1, r=(res, "RP", "rp_t"), w=("rp_u",), tag=tagp + "rp.u")
        def f2(eng, x1=x1, c1=c1):
            return eng.tensor_mul(out=x1, in0=x1, in1=c1)
        em.op("vector", f2, r=(res, "RP", "rp_t"), w=(res,), tag=tagp + "rp.m1")
        def f3(eng, x1=x1, u_=u_):
            return eng.tensor_sub(out=x1, in0=x1, in1=u_)
        em.op("vector", f3, r=(res, "rp_u"), w=(res,), tag=tagp + "rp.s1")
        def f4(eng, x2=x2, c2=c2):
            return eng.tensor_mul(out=x2, in0=x2, in1=c2)
        em.op("gpsimd", f4, r=(res, "RP", "rp_u"), w=(res,), tag=tagp + "rp.m2")
        def f5(eng, t_=t_, sx1=sx1, w_=w_):
            return eng.tensor_mul(out=w_, in0=t_, in1=sx1)
        em.op("gpsimd", f5, r=("rp_t", "RP"), w=("rp_w",), tag=tagp + "rp.m3")
        def f6(eng, x2=x2, w_=w_):
            return eng.tensor_add(out=x2, in0=x2, in1=w_)
        em.op("gpsimd", f6, r=(res, "rp_w"), w=(res,), tag=tagp + "rp.a2")


def attn_site(ctx, site):
    em = ctx.em
    base = BASE.get(site, site)
    QB, OB, WB, ES, OF = (ctx.sb["QB"], ctx.sb["OB"], ctx.sb["WB"],
                          ctx.sb["ES"], ctx.sb["OF"])
    OBb = OB[:]
    cfgs = {
        "l0f": dict(hl=2, dh=128, dv=128, M=M_L0F, mcc=0, scale=128 ** -0.5,
                    wq="l0f_wq", wk="l0f_wk", wv="l0f_wv", wo="l0f_wosl",
                    ln="ln", rope=None, out_scale=0.5, add_xn=False),
        "l0a": dict(hl=2, dh=128, dv=128, M=M_L0A, mcc=1, scale=128 ** -0.5,
                    wq="l0a_wq", wk="l0a_wk", wv="l0a_wv", wo="l0a_wosl",
                    ln="ln_ln", rope=None, band=256, use_global=True,
                    out_scale=1.0, add_xn=True),
        "l1a": dict(hl=2, dh=128, dv=128, M=None, scale=128 ** -0.5,
                    wq="l1a_wq", wk="l1a_wk", wv="l1a_wv", wo="l1a_wosl",
                    ln="ln", rope=None, fmap=True, denom_eps=1e-5,
                    out_scale=1.0, add_xn=False),
        "l2a": dict(hl=2, dh=128, dv=128, M=M_L2A, mcc=2, scale=128 ** -0.5,
                    wq="l2a_wq", wk=None, wv=None, wo="l2a_wosl",
                    ln="ln", rope="l2", kv_shared=True, out_scale=1.0,
                    add_xn=False),
        "l3a": dict(hl=4, dh=64, dv=64, M=M_L3A, mcc=3, scale=64 ** -0.5,
                    wq="l3a_wq", wk="l3a_wkv", wv=None, wo="l3a_wosl",
                    ln="ln_rms", rope="l3", l3=True, out_scale=1.0,
                    add_xn=True),
    }
    cfg = cfgs[base]
    hl, dh, dv = cfg["hl"], cfg["dh"], cfg["dv"]

    def dbg_sb(name, ap, res):
        if name in TAPS and ("dbg_" + name) in ctx.dram:
            def f(eng):
                return eng.dma_start(out=ctx.dram["dbg_" + name], in_=ap)
            em.op("gpsimd", f, r=(res,), w=(("dbg", name),), dma=True,
                  tag="dbg." + name)

    def dbg_dram(name, src_ap, res):
        if name in TAPS and ("dbg_" + name) in ctx.dram:
            def f(eng):
                return eng.dma_start(out=ctx.dram["dbg_" + name], in_=src_ap)
            em.op("gpsimd", f, r=(res,), w=(("dbg", name),), dma=True,
                  tag="dbg." + name)

    ln_to_xn(ctx, kind=cfg["ln"], ln_scale=(base == "l0a"),
             r1_rows=cfg["add_xn"])
    dbg_sb("xn", ctx.sb["XN"][:], "XN")
    if base == "l1a":
        build_gb_tiles(ctx)
    xin, xg = ctx.dram["xg_in"], ctx.dram["xg"]
    def fxo(eng):
        return eng.dma_start(out=xin[:, :].rearrange("(c p) t -> p c t", p=128),
                             in_=ctx.sb["XN"][:])
    em.op("gpsimd", fxo, r=("XN",), w=("xg_in",), dma=True, tag="at.xo")
    def fcc(eng):
        return eng.collective_compute(
            "AllGather", ALU.bypass, replica_groups=[list(range(NCORE))],
            ins=[xin.opt()], outs=[xg.opt()])
    em.op("gpsimd", fcc, r=("xg_in",), w=("xg",), cc=True, tag="at.ag")
    em.claim("OB_P", "OB_A", "OB_W", "OB_Q", "OB_K", tag="at.clob")
    def ldw(eng, name, c0, width):
        return eng.dma_start(
            out=OBb[:, c0:c0 + 16 * width].rearrange("p (c f) -> p c f", f=width),
            in_=ctx.wc[name][:, :].rearrange("(c p) f -> p c f", p=128))
    def fwo(eng):
        return eng.dma_start(
            out=OBb[:, 4096:8192].rearrange("p (c f) -> p c f", f=2048),
            in_=ctx.wc[cfg["wo"]][:, :].rearrange("(c p) f -> p c f", p=128))
    em.op("sync", fwo, r=(("wc", cfg["wo"]),), w=("OB_W",), dma=True, tag="at.lwo")
    def fwq(eng):
        return eng.dma_start(
            out=OBb[:, 8192:12288].rearrange("p (c f) -> p c f", f=256),
            in_=ctx.wc[cfg["wq"]][:, :].rearrange("(c p) f -> p c f", p=128))
    em.op("sync", fwq, r=(("wc", cfg["wq"]),), w=("OB_Q",), dma=True, tag="at.lwq")
    if base == "l3a":
        def fwk(eng):
            return eng.dma_start(
                out=OBb[:, 12288:14336].rearrange("p (c f) -> p c f", f=128),
                in_=ctx.wc["l3a_wkv"][:, :].rearrange("(c p) f -> p c f", p=128))
        em.op("sync", fwk, r=(("wc", "l3a_wkv"),), w=("OB_K",), dma=True,
              tag="at.lwk")
    elif base == "l2a":
        def fwk(eng):
            return eng.dma_start(
                out=OBb[:, 12288:16384].rearrange("p (c f) -> p c f", f=256),
                in_=ctx.wc["l2a_wkv"][:, :].rearrange("(c p) f -> p c f", p=128))
        em.op("sync", fwk, r=(("wc", "l2a_wkv"),), w=("OB_K",), dma=True,
              tag="at.lwk")
    else:
        def fwk(eng):
            return eng.dma_start(
                out=OBb[:, 12288:16384].rearrange("p (c f) -> p c f", f=256),
                in_=ctx.wc[cfg["wk"]][:, :].rearrange("(c p) f -> p c f", p=128))
        em.op("sync", fwk, r=(("wc", cfg["wk"]),), w=("OB_K",), dma=True,
              tag="at.lwk")
        def fwv(eng):
            return eng.dma_start(
                out=WB[:, 4096:8192].rearrange("p (c f) -> p c f", f=256),
                in_=ctx.wc[cfg["wv"]][:, :].rearrange("(c p) f -> p c f", p=128))
        em.op("sync", fwv, r=(("wc", cfg["wv"]),), w=("WB_V",), dma=True,
              tag="at.lwv")
    if cfg.get("use_global"):
        prep_globals_l0a(ctx)
    em.claim("XN", tag="at.clxn")
    for f in range(NFAT):
        xs = f % 2
        xslot = ctx.sb["XN"][:].rearrange("p a b -> p (a b)")[:, xs * 8192:(xs + 1) * 8192]
        xslot3 = xslot.rearrange("p (c f2) -> p c f2", f2=512)
        def fxl(eng, f=f, xslot3=xslot3):
            rf, half = f // 2, f % 2
            return eng.dma_start(
                out=xslot3,
                in_=xg[2048 * rf:2048 * (rf + 1), half * 512:(half + 1) * 512]
                    .rearrange("(c p) t -> p c t", p=128))
        em.op("sync", fxl, r=("xg", "XN", f"xsl{xs}"), w=(f"xsl{xs}",), dma=True,
              tag="at.xl")
        if cfg.get("rope"):
            rr, rows = ("c_rope2", 64) if cfg["rope"] == "l2" else ("c_rope3", 64)
            def frp(eng, f=f, rr=rr, rows=rows):
                return eng.dma_start(out=ctx.sb["RP"][0:rows, 0:512],
                                     in_=ctx.inp[rr + "c"][:, f * 512:(f + 1) * 512])
            em.op("gpsimd", frp, r=("RP",), w=("RP",), dma=True, tag="at.rp")
            def frpb(eng, f=f, rr=rr, rows=rows):
                return eng.dma_start(out=ctx.sb["RP"][0:rows, 512:1024],
                                     in_=ctx.inp[rr + "s"][:, f * 512:(f + 1) * 512])
            em.op("gpsimd", frpb, r=("RP",), w=("RP",), dma=True, tag="at.rpb")
            if cfg["rope"] == "l3":
                def frp2(eng, f=f, rr=rr):
                    return eng.dma_start(out=ctx.sb["RP"][64:128, 0:512],
                                         in_=ctx.inp[rr + "c"][:, f * 512:(f + 1) * 512])
                em.op("gpsimd", frp2, r=("RP",), w=("RP",), dma=True, tag="at.rp2")
                def frpb2(eng, f=f, rr=rr):
                    return eng.dma_start(out=ctx.sb["RP"][64:128, 512:1024],
                                         in_=ctx.inp[rr + "s"][:, f * 512:(f + 1) * 512])
                em.op("gpsimd", frpb2, r=("RP",), w=("RP",), dma=True, tag="at.rpb2")
        make_qkv_fat(ctx, cfg, base, f, xslot3, xs)
    dbg_dram("xg", ctx.dram["xg"], "xg")
    if "wcq" in TAPS and base == "l0f":
        dbg_dram("wcq", ctx.wc["l0f_wq"], ("wc", "l0f_wq"))
    dbg_sb("sg", ctx.sb["SG"][:], "SG")
    dbg_dram("kloc", ctx.dram["kloc"], "kloc")
    dbg_dram("vloc", ctx.dram["vloc"], "vloc")
    dbg_sb("qb", ctx.sb["QB"][:], "QB")
    for s in range(4):
        def fo(eng, s=s, dv=dv):
            return eng.memset(
                ctx.sb["KVA"][:, 1024 + s * 264:1024 + s * 264 + 264]
                .rearrange("p (g2 f2) -> p g2 f2", f2=132)[:, :, dv:132],
                1.0)
        em.op("vector", fo, r=(f"vslot{2 * s}", f"vslot{2 * s + 1}"),
              w=(f"vslot{2 * s}", f"vslot{2 * s + 1}", "vones"),
              tag="vones")
    for f in range(NFAT):
        flash_fat(ctx, cfg, base, f)
        partial_out_fat(ctx, cfg, f)
    dbg_dram("rsin", ctx.dram["rs_in"], "rs_in")
    def frs(eng):
        return eng.collective_compute(
            "ReduceScatter", ALU.add, replica_groups=[list(range(NCORE))],
            ins=[ctx.dram["rs_in"].opt()], outs=[ctx.dram["rs_out"].opt()])
    em.op("gpsimd", frs, r=("rs_in",), w=("rs_out",), cc=True, tag="at.rs")
    dbg_dram("rsout", ctx.dram["rs_out"], "rs_out")
    add_tokmajor_dram_to_R(ctx, ctx.dram["rs_out"], cfg["out_scale"],
                           extra_r=("rs_out",))


def _evac_fm(ctx, psres, ps_ap, dst_ap, dst_res, scale_ap=None, scale_res=None,
             eng_name="vector", tag="ev"):
    em = ctx.em
    if scale_ap is not None:
        def f(eng):
            return eng.tensor_scalar(out=dst_ap, in0=ps_ap, scalar1=scale_ap,
                                     scalar2=None, op0=ALU.mult)
        em.op(eng_name, f, r=(psres, scale_res), w=(dst_res,), tag=tag)
    else:
        def f(eng):
            return eng.tensor_copy(out=dst_ap, in_=ps_ap)
        em.op(eng_name, f, r=(psres,), w=(dst_res,), tag=tag)


def make_qkv_fat(ctx, cfg, base, f, xslot3, xs):
    em = ctx.em
    QB, OB, WB, ES = ctx.sb["QB"], ctx.sb["OB"], ctx.sb["WB"], ctx.sb["ES"]
    OBb = OB[:]
    OQ = OBb[:, 8192:12288].rearrange("p (c f2) -> p c f2", f2=256)
    OK = OBb[:, 12288:16384].rearrange("p (c f2) -> p c f2", f2=256)
    OK3 = OBb[:, 12288:14336].rearrange("p (c f2) -> p c f2", f2=128)
    WV = WB[:, 4096:8192].rearrange("p (c f2) -> p c f2", f2=256)
    xres = f"xsl{xs}"
    kloc, vloc = ctx.dram["kloc"], ctx.dram["vloc"]

    def mm_acc(bank, lhs_fn, rhs_fn, n, rres, wres, nch=16, tag="mm", pdim=128):
        for c in range(nch):
            def fm(eng, c=c, bank=bank):
                return eng.matmul(ctx.ps[bank][0:pdim, 0:n], lhsT=lhs_fn(c),
                                  rhs=rhs_fn(c), start=(c == 0),
                                  stop=(c == nch - 1))
            em.op("tensor", fm, r=rres, w=(wres,), tag=tag)

    if base == "l1a":
        make_qkv_fat_l1a(ctx, cfg, f, xslot3, xs)
        return
    if base == "l3a":
        for jj in range(2):
            bank = jj % 2
            mm_acc(bank, lambda c, jj=jj: OQ[:, c, :].rearrange("p (a b) -> p a b", b=128)[:, jj, :],
                   lambda c: xslot3[:, c, :], 512, ("OB_Q", xres), f"ps{bank}",
                   tag="q3.mm")
            dst = QB[:, jj * 8192 + f * 512:jj * 8192 + (f + 1) * 512]
            _evac_fm(ctx, f"ps{bank}", ctx.ps[bank][:, 0:512], dst, "QB",
                     tag="q3.ev")
            rope_apply(ctx, dst, "QB", "l3", tagp="q3.")
        mm_acc(0, lambda c: OK3[:, c, 0:64], lambda c: xslot3[:, c, :], 512,
               ("OB_K", xres), "ps0", tag="k3.mm", pdim=64)
        _evac_fm(ctx, "ps0", ctx.ps[0][0:64, 0:512], ES[0:64, 0:512], "esK",
                 tag="k3.ev")
        rope_apply(ctx, ES[0:64, 0:512], "esK", "l3k", tagp="k3.")
        def fks(eng, f=f):
            return eng.dma_start(out=kloc[0:64, f * 512:(f + 1) * 512],
                                 in_=ES[0:64, 0:512])
        em.op("sync", fks, r=("esK",), w=("kloc",), dma=True, tag="k3.st")
        for ts_ in range(4):
            bank = ts_ % 2
            mm_acc(bank, lambda c, ts_=ts_: xslot3[:, c, ts_ * 128:(ts_ + 1) * 128],
                   lambda c: OK3[:, c, 64:128], 64, (xres, "OB_K"), f"ps{bank}",
                   tag="v3.mm")
            _evac_fm(ctx, f"ps{bank}", ctx.ps[bank][:, 0:64],
                     ES[:, 512 + ts_ * 64:512 + (ts_ + 1) * 64], "esV", tag="v3.ev")
        def fvs(eng, f=f):
            return eng.dma_start(
                out=vloc[f * 512:(f + 1) * 512, 0:64]
                    .rearrange("(a p) b -> p a b", p=128),
                in_=ES[:, 512:768].rearrange("p (a b) -> p a b", b=64))
        em.op("sync", fvs, r=("esV",), w=("vloc",), dma=True, tag="v3.st")
        return
    if base == "l2a":
        mm_acc(0, lambda c: OK[:, c, 0:128], lambda c: xslot3[:, c, :], 512,
               ("OB_K", xres), "ps0", tag="kc.mm")
        _evac_fm(ctx, "ps0", ctx.ps[0][:, 0:512], ES[:, 0:512], "esKC",
                 tag="kc.ev")
        for hh in range(2):
            def fsc(eng, hh=hh):
                return eng.tensor_scalar(out=PBK(ctx)[:, 0:512], in0=ES[:, 0:512],
                                         scalar1=ctx.sb["SG"][:, 2 + hh:3 + hh],
                                         scalar2=None, op0=ALU.mult)
            em.op("vector", fsc, r=("esKC", "SG", "rp_t", "rp_u"), w=("esKH",),
                  tag="kh.sc")
            rope_apply(ctx, PBK(ctx)[:, 0:512], "esKH", "l2", tagp="kh.")
            def fks(eng, f=f, hh=hh):
                return eng.dma_start(
                    out=kloc[hh * 128:(hh + 1) * 128, f * 512:(f + 1) * 512],
                    in_=PBK(ctx)[:, 0:512])
            em.op("sync", fks, r=("esKH",), w=("kloc",), dma=True, tag="kh.st")
        for ts_ in range(4):
            bank = ts_ % 2
            mm_acc(bank, lambda c, ts_=ts_: xslot3[:, c, ts_ * 128:(ts_ + 1) * 128],
                   lambda c: OK[:, c, 128:256], 128, (xres, "OB_K"), f"ps{bank}",
                   tag="vc.mm")
            _evac_fm(ctx, f"ps{bank}", ctx.ps[bank][:, 0:128],
                     ES[:, 512 + ts_ * 128:512 + (ts_ + 1) * 128], "esV",
                     tag="vc.ev")
        def fvs(eng, f=f):
            return eng.dma_start(
                out=vloc[f * 512:(f + 1) * 512, 0:128]
                    .rearrange("(a p) b -> p a b", p=128),
                in_=ES[:, 512:1024].rearrange("p (a b) -> p a b", b=128))
        em.op("sync", fvs, r=("esV",), w=("vloc",), dma=True, tag="vc.st")
    else:
        for hh in range(2):
            bank = hh % 2
            mm_acc(bank, lambda c, hh=hh: OK[:, c, hh * 128:(hh + 1) * 128],
                   lambda c: xslot3[:, c, :], 512, ("OB_K", xres), f"ps{bank}",
                   tag="k.mm")
            sc = (ctx.sb["SG"][:, hh:hh + 1], "SG") if base == "l0f" else (None, None)
            _evac_fm(ctx, f"ps{bank}", ctx.ps[bank][:, 0:512],
                     ES[:, hh * 512:(hh + 1) * 512], "esK", sc[0], sc[1],
                     tag="k.ev")
        def fks(eng, f=f):
            return eng.dma_start(
                out=kloc[0:256, f * 512:(f + 1) * 512]
                    .rearrange("(a p) b -> p a b", p=128),
                in_=ES[:, 0:1024].rearrange("p (a b) -> p a b", b=512))
        em.op("sync", fks, r=("esK",), w=("kloc",), dma=True, tag="k.st")
        for ts_ in range(4):
            bank = ts_ % 2
            mm_acc(bank, lambda c, ts_=ts_: xslot3[:, c, ts_ * 128:(ts_ + 1) * 128],
                   lambda c: WV[:, c, :], 256, (xres, "WB_V"), f"ps{bank}",
                   tag="v.mm")
            _evac_fm(ctx, f"ps{bank}", ctx.ps[bank][:, 0:256],
                     ctx.sb["RT"][:, ts_ * 256:(ts_ + 1) * 256], "esV", tag="v.ev")
        def fvs(eng, f=f):
            return eng.dma_start(
                out=vloc[f * 512:(f + 1) * 512, 0:256]
                    .rearrange("(a p) b -> p a b", p=128),
                in_=ctx.sb["RT"][:, 0:1024].rearrange("p (a b) -> p a b", b=256))
        em.op("sync", fvs, r=("esV",), w=("vloc",), dma=True, tag="v.st")
    for hh in range(2):
        bank = hh % 2
        mm_acc(bank, lambda c, hh=hh: OQ[:, c, hh * 128:(hh + 1) * 128],
               lambda c: xslot3[:, c, :], 512, ("OB_Q", xres), f"ps{bank}",
               tag="q.mm")
        dst = QB[:, hh * 8192 + f * 512:hh * 8192 + (f + 1) * 512]
        sc = (ctx.sb["SG"][:, hh:hh + 1], "SG") if base == "l0f" else (None, None)
        _evac_fm(ctx, f"ps{bank}", ctx.ps[bank][:, 0:512], dst, "QB",
                 sc[0], sc[1], tag="q.ev")
        if base == "l2a":
            rope_apply(ctx, dst, "QB", "l2", tagp="q2.")


def PBK(ctx):
    return ctx.sb["KVA"][:, 0:512]


def make_qkv_fat_l1a(ctx, cfg, f, xslot3, xs):
    em = ctx.em
    QB, OB, WB, ES, SC, ST = (ctx.sb["QB"], ctx.sb["OB"], ctx.sb["WB"],
                              ctx.sb["ES"], ctx.sb["SC"], ctx.sb["ST"])
    OBb = OB[:]
    OQ = OBb[:, 8192:12288].rearrange("p (c f2) -> p c f2", f2=256)
    OK = OBb[:, 12288:16384].rearrange("p (c f2) -> p c f2", f2=256)
    WV = WB[:, 4096:8192].rearrange("p (c f2) -> p c f2", f2=256)
    xres = f"xsl{xs}"
    kloc, vloc = ctx.dram["kloc"], ctx.dram["vloc"]
    for ts_ in range(4):
        for side, WS, wres, b0 in (("q", OQ, "OB_Q", 0), ("k", OK, "OB_K", 1)):
            bank = b0
            for c in range(NCH):
                def fm(eng, c=c, bank=bank, WS=WS, ts_=ts_):
                    return eng.matmul(ctx.ps[bank][:, 0:256],
                                      lhsT=xslot3[:, c, ts_ * 128:(ts_ + 1) * 128],
                                      rhs=WS[:, c, :], start=(c == 0),
                                      stop=(c == NCH - 1))
                em.op("tensor", fm, r=(xres, wres), w=(f"ps{bank}",), tag="fm.mm")
            def fy(eng, bank=bank):
                return eng.tensor_mul(out=SC[:, 0:256], in0=ctx.ps[bank][:, 0:256],
                                      in1=ctx.sb["SC2"][:, 0:256])
            em.op("vector", fy, r=(f"ps{bank}", "gbt"), w=("SC_A",), tag="fm.g")
            def fy2(eng):
                return eng.tensor_add(out=SC[:, 0:256], in0=SC[:, 0:256],
                                      in1=ctx.sb["SC2"][:, 256:512])
            em.op("vector", fy2, r=("SC_A", "gbt"), w=("SC_A",), tag="fm.b")
            def fs(eng):
                return eng.reduce_sum(out=ST[:, 0:2],
                                      in_=SC[:, 0:256].rearrange("p (a b) -> p a b", b=128),
                                      axis=AXX)
            em.op("vector", fs, r=("SC_A",), w=("fm_s",), tag="fm.s")
            def fq(eng):
                return eng.activation(out=SC[:, 256:512], in_=SC[:, 0:256],
                                      func=AF.Square)
            em.op("scalar", fq, r=("SC_A",), w=("fm_q",), tag="fm.q")
            def fs2(eng):
                return eng.reduce_sum(out=ST[:, 2:4],
                                      in_=SC[:, 256:512].rearrange("p (a b) -> p a b", b=128),
                                      axis=AXX)
            em.op("vector", fs2, r=("fm_q",), w=("fm_s2",), tag="fm.s2")
            def fmu(eng):
                return eng.tensor_scalar(out=ST[:, 4:6], in0=ST[:, 0:2],
                                         scalar1=1.0 / 128, scalar2=None,
                                         op0=ALU.mult)
            em.op("vector", fmu, r=("fm_s",), w=("fm_mu",), tag="fm.mu")
            def fvar(eng):
                return eng.tensor_scalar(out=ST[:, 6:8], in0=ST[:, 2:4],
                                         scalar1=1.0 / 128, scalar2=None,
                                         op0=ALU.mult)
            em.op("vector", fvar, r=("fm_s2",), w=("fm_va",), tag="fm.va")
            def fvb(eng):
                return eng.scalar_tensor_tensor(out=ST[:, 8:10], in0=ST[:, 4:6],
                                                scalar=-1.0, in1=ST[:, 4:6],
                                                op0=ALU.mult, op1=ALU.mult)
            em.op("vector", fvb, r=("fm_mu",), w=("fm_m2",), tag="fm.m2")
            def fvc(eng):
                return eng.tensor_add(out=ST[:, 6:8], in0=ST[:, 6:8],
                                      in1=ST[:, 8:10])
            em.op("vector", fvc, r=("fm_va", "fm_m2"), w=("fm_va",), tag="fm.vc")
            def fl_(eng):
                return eng.activation(out=ST[:, 10:12], in_=ST[:, 6:8],
                                      func=AF.Ln,
                                      bias=ctx.sb["c_mc"][:, 4:5])
            em.op("scalar", fl_, r=("fm_va", "c_mc"), w=("fm_l",), tag="fm.l")
            def fe(eng):
                return eng.activation(out=ST[:, 12:14], in_=ST[:, 10:12],
                                      func=AF.Exp, scale=-0.5)
            em.op("scalar", fe, r=("fm_l",), w=("fm_r",), tag="fm.e")
            for hh in range(2):
                def fa(eng, hh=hh):
                    return eng.tensor_scalar(
                        out=ctx.sb["SB16"][:, 0:128], in0=SC[:, hh * 128:(hh + 1) * 128],
                        scalar1=ST[:, 4 + hh:5 + hh], scalar2=ST[:, 12 + hh:13 + hh],
                        op0=ALU.subtract, op1=ALU.mult)
                em.op("vector", fa, r=("SC_A", "fm_mu", "fm_r"), w=("fm_y",),
                      tag="fm.a")
                def ft(eng):
                    return eng.transpose(ctx.ps["tr"][:, 0:128],
                                         ctx.sb["SB16"][:, 0:128],
                                         ctx.sb["c_ident"][:])
                em.op("tensor", ft, r=("fm_y", "c_ident"), w=("ps_tr",), tag="fm.t")
                if side == "q":
                    def fc(eng, hh=hh, ts_=ts_, f=f):
                        return eng.tensor_copy(
                            out=QB[:, hh * 8192 + f * 512 + ts_ * 128:
                                   hh * 8192 + f * 512 + (ts_ + 1) * 128],
                            in_=ctx.ps["tr"][:, 0:128])
                    em.op("vector", fc, r=("ps_tr",), w=("QB",), tag="fm.cq")
                else:
                    def fc(eng, hh=hh, ts_=ts_):
                        return eng.tensor_copy(
                            out=ES[:, hh * 512 + ts_ * 128:hh * 512 + (ts_ + 1) * 128],
                            in_=ctx.ps["tr"][:, 0:128])
                    em.op("vector", fc, r=("ps_tr",), w=("esK",), tag="fm.ck")
    def fks(eng, f=f):
        return eng.dma_start(
            out=kloc[0:256, f * 512:(f + 1) * 512]
                .rearrange("(a p) b -> p a b", p=128),
            in_=ES[:, 0:1024].rearrange("p (a b) -> p a b", b=512))
    em.op("sync", fks, r=("esK",), w=("kloc",), dma=True, tag="fm.kst")
    for ts_ in range(4):
        bank = ts_ % 2
        for c in range(NCH):
            def fm(eng, c=c, bank=bank, ts_=ts_):
                return eng.matmul(ctx.ps[bank][:, 0:256],
                                  lhsT=xslot3[:, c, ts_ * 128:(ts_ + 1) * 128],
                                  rhs=WV[:, c, :], start=(c == 0),
                                  stop=(c == NCH - 1))
            em.op("tensor", fm, r=(xres, "WB_V"), w=(f"ps{bank}",), tag="fmv.mm")
        _evac_fm(ctx, f"ps{bank}", ctx.ps[bank][:, 0:256],
                 ctx.sb["RT"][:, ts_ * 256:(ts_ + 1) * 256], "esV", tag="fmv.ev")
    def fvs(eng, f=f):
        return eng.dma_start(
            out=vloc[f * 512:(f + 1) * 512, 0:256]
                .rearrange("(a p) b -> p a b", p=128),
            in_=ctx.sb["RT"][:, 0:1024].rearrange("p (a b) -> p a b", b=256))
    em.op("sync", fvs, r=("esV",), w=("vloc",), dma=True, tag="fmv.st")


def build_gb_tiles(ctx):
    em = ctx.em
    SC2 = ctx.sb["SC2"]
    for which, srcn, c0 in (("g", "L1_attn_gamma", 0), ("b", "L1_attn_beta", 256)):
        def fl(eng, srcn=srcn, c0=c0):
            return eng.dma_start(
                out=SC2[:, c0:c0 + 256].rearrange("p (a b) -> p a b", b=128),
                in_=ctx.inp[srcn].unsqueeze(0).unsqueeze(0)
                .broadcast_to((128, 2, 128)))
        em.op("gpsimd", fl, r=("gbt",), w=("gbt",), dma=True, tag="gb.ld")



def prep_globals_l0a(ctx):
    em = ctx.em
    OB, WB, KVA, SB16 = ctx.sb["OB"], ctx.sb["WB"], ctx.sb["KVA"], ctx.sb["SB16"]
    OK = OB[:][:, 12288:16384].rearrange("p (c f2) -> p c f2", f2=256)
    WV = WB[:, 4096:8192].rearrange("p (c f2) -> p c f2", f2=256)
    g = ctx.inp["L0_attn_global_tokens"]
    for c0 in range(0, 16, 8):
        def fl(eng, c0=c0):
            return eng.dma_start(out=ctx.sb["RT"][0:16, 0:1024],
                                 in_=g[0][:, c0 * 128:(c0 + 8) * 128])
        em.op("gpsimd", fl, r=("RT",), w=("RT",), dma=True, tag="gl.ld")
        for j in range(8):
            def ft(eng, j=j):
                return eng.transpose(ctx.ps["tr"][:, j * 16:(j + 1) * 16],
                                     ctx.sb["RT"][0:16, j * 128:(j + 1) * 128],
                                     ctx.sb["c_ident"][0:16, 0:16])
            em.op("tensor", ft, r=("RT", "c_ident"), w=("ps_tr",), tag="gl.t")
        def fcp(eng, c0=c0):
            return eng.tensor_copy(out=SB16[:, c0 * 16:(c0 + 8) * 16],
                                   in_=ctx.ps["tr"][:, 0:128])
        em.op("vector", fcp, r=("ps_tr",), w=("gT",), tag="gl.cp")
    for hh in range(2):
        for c in range(NCH):
            def fm(eng, c=c, hh=hh):
                return eng.matmul(ctx.ps[0][:, 0:16],
                                  lhsT=OK[:, c, hh * 128:(hh + 1) * 128],
                                  rhs=SB16[:, c * 16:(c + 1) * 16],
                                  start=(c == 0), stop=(c == NCH - 1))
            em.op("tensor", fm, r=("OB_K", "gT"), w=("ps0",), tag="gl.kmm")
        def fe(eng, hh=hh):
            return eng.tensor_copy(out=KVA[:, 2080 + hh * 16:2096 + hh * 16],
                                   in_=ctx.ps[0][:, 0:16])
        em.op("vector", fe, r=("ps0",), w=("kglob",), tag="gl.kev")
    def fo(eng):
        return eng.memset(KVA[0:16, 2120:2384].rearrange("p (g2 f2) -> p g2 f2", f2=132)[:, :, 128:129], 1.0)
    em.op("vector", fo, w=("vglob",), tag="gl.vo")
    for c in range(NCH):
        def fm(eng, c=c):
            return eng.matmul(ctx.ps[1][0:16, 0:256],
                              lhsT=SB16[:, c * 16:(c + 1) * 16],
                              rhs=WV[:, c, :], start=(c == 0),
                              stop=(c == NCH - 1))
        em.op("tensor", fm, r=("gT", "WB_V"), w=("ps1",), tag="gl.vmm")
    def fe2(eng):
        return eng.tensor_copy(
            out=KVA[0:16, 2120:2384].rearrange("p (g2 f2) -> p g2 f2", f2=132)[:, :, 0:128],
            in_=ctx.ps[1][0:16, 0:256].rearrange("p (g2 f2) -> p g2 f2", f2=128))
    em.op("vector", fe2, r=("ps1", "vglob"), w=("vglob",), tag="gl.vev")


def flash_fat(ctx, cfg, base, f):
    em = ctx.em
    hl, dh, dv = cfg["hl"], cfg["dh"], cfg["dv"]
    M, scale = cfg["M"], cfg["scale"]
    band = cfg.get("band")
    kv_shared = cfg.get("kv_shared", False) or base == "l3a"
    KVA, PB, QB, ES = ctx.sb["KVA"], ctx.sb["PB"], ctx.sb["QB"], ctx.sb["ES"]
    OF = ctx.sb["OF"]
    kloc, vloc = ctx.dram["kloc"], ctx.dram["vloc"]
    koff = [s * 128 for s in range(8)]
    voff = [1024 + s * 132 for s in range(8)]
    b = AB[f]
    PSL = [(PB, 0, "pb0"), (PB, 512, "pb1"), (ES, 0, "pb2"), (ES, 512, "pb3")]
    SBK = [0, 1, 6]
    use_g = bool(cfg.get("use_global"))
    OBANK = [2 + i for i in range(hl)] if base == "l3a" else [2, 3]
    DBANK = [4, 5]

    def qsrc(h):
        if base == "l3a":
            return QB[(h % 2) * 64:(h % 2) * 64 + 64,
                      (h // 2) * 8192:(h // 2 + 1) * 8192]
        return QB[:, h * 8192:(h + 1) * 8192]

    def jrange(kt):
        q0a = b * BLK
        ktr = kt - q0a // 128
        jlo = max(0, ktr)
        jhi = min(3, ktr + 2) if band else 3
        if jlo > jhi:
            return None
        diag_j = ktr if 0 <= ktr <= 3 else None
        uedge_j = (ktr + 2) if (band and 0 <= ktr + 2 <= 3) else None
        return jlo, jhi, diag_j, uedge_j

    kt0 = max(0, (b * BLK) // 128 - 2) if band else 0
    kt_list = [kt for kt in range(kt0, (b + 1) * 4) if jrange(kt) is not None]
    n = len(kt_list)
    touch = {}
    for kt in kt_list:
        jlo, jhi, _, _ = jrange(kt)
        for j in range(jlo, jhi + 1):
            touch.setdefault(j, []).append(kt)

    func = AF.Exp if M is not None else AF.Square
    akw = (dict(scale=scale) if M is None else
           dict(scale=scale, bias=ctx.sb["c_mc"][:, cfg["mcc"]:cfg["mcc"] + 1]))

    for h in range(hl):
        hk0 = 0 if kv_shared else h * dh
        hv0 = 0 if kv_shared else h * dv
        qb_ = 64 * (h % 2) if base == "l3a" else 0
        goff = 1 if use_g else 0
        n_it = n + goff

        def emit_loads(ridx, hk0=hk0, hv0=hv0):
            for rx in range(ridx, min(ridx + 4, n)):
                kt = kt_list[rx]
                si = rx % 8
                blkk = kt * 128 // BLK
                gc = FAT_OF_BLOCK[blkk] * BLK + (kt * 128) % BLK
                def flk(eng, si=si, gc=gc, hk0=hk0):
                    return eng.dma_start(
                        out=KVA[0:dh, koff[si]:koff[si] + 128],
                        in_=kloc[hk0:hk0 + dh, gc:gc + 128])
                em.op("sync", flk, r=("kloc", f"kslot{si}"), w=(f"kslot{si}",),
                      dma=True, tag="fl.ldk")
                if base == "l3a":
                    def flk2(eng, si=si, gc=gc):
                        return eng.dma_start(
                            out=KVA[64:128, koff[si]:koff[si] + 128],
                            in_=kloc[0:64, gc:gc + 128])
                    em.op("sync", flk2, r=("kloc", f"kslot{si}"),
                          w=(f"kslot{si}",), dma=True, tag="fl.ldk2")
                def flv(eng, si=si, gc=gc, hv0=hv0):
                    return eng.dma_start(
                        out=KVA[:, voff[si]:voff[si] + dv],
                        in_=vloc[gc:gc + 128, hv0:hv0 + dv])
                em.op("sync", flv, r=("vloc", f"vslot{si}"), w=(f"vslot{si}",),
                      dma=True, tag="fl.ldv")

        def stage_av(it, h=h, goff=goff):
            buf, poff, pres = PSL[it % 4]
            ob = OBANK[h]
            st = (it == 0)
            sp = (it == n_it - 1)
            if use_g and it == 0:
                def favg(eng, buf=buf, poff=poff, ob=ob, sp=sp):
                    return eng.matmul(
                        ctx.ps[ob][0:dv, 0:512],
                        lhsT=KVA[0:16, 2120 + h * 132:2120 + h * 132 + dv],
                        rhs=buf[0:16, poff:poff + 512],
                        start=True, stop=sp)
                em.op("tensor", favg, r=(pres, "vglob"), w=(f"ps{ob}",),
                      tag="gf.av")
                def fdng(eng, buf=buf, poff=poff, sp=sp, db=DBANK[h]):
                    return eng.matmul(
                        ctx.ps[db][0:1, 0:512],
                        lhsT=ctx.sb["c_ones_col"][0:16, 0:1],
                        rhs=buf[0:16, poff:poff + 512],
                        start=True, stop=sp)
                em.op("tensor", fdng, r=(pres, "c_ones_col"),
                      w=(f"ps{DBANK[h]}",), tag="gf.dn")
                return
            rx = it - goff
            kt = kt_list[rx]
            si = rx % 8
            jlo, jhi, _, _ = jrange(kt)
            nq = jhi - jlo + 1
            vw = dv + 1 if base == "l3a" else dv
            def fav(eng, jlo=jlo, nq=nq, buf=buf, poff=poff, si=si,
                    ob=ob, st=st, sp=sp, vw=vw):
                return eng.matmul(
                    ctx.ps[ob][0:vw, jlo * 128:(jlo + nq) * 128],
                    lhsT=KVA[:, voff[si]:voff[si] + vw],
                    rhs=buf[:, poff:poff + nq * 128],
                    start=st, stop=sp)
            em.op("tensor", fav, r=(pres, f"vslot{si}", "vones"),
                  w=(f"ps{ob}",), tag="fl.av")
            if base != "l3a":
                def fdn(eng, jlo=jlo, nq=nq, buf=buf, poff=poff,
                        st=st, sp=sp, db=DBANK[h]):
                    return eng.matmul(
                        ctx.ps[db][0:1, jlo * 128:(jlo + nq) * 128],
                        lhsT=ctx.sb["c_ones_col"][:],
                        rhs=buf[:, poff:poff + nq * 128],
                        start=st, stop=sp)
                em.op("tensor", fdn, r=(pres, "c_ones_col"),
                      w=(f"ps{DBANK[h]}",), tag="fl.dn")

        for it in range(n_it):
            buf, poff, pres = PSL[it % 4]
            sb_ = SBK[it % 3]
            if use_g and it == 0:
                def fqkg(eng, h=h, f=f, sb_=sb_):
                    return eng.matmul(
                        ctx.ps[sb_][0:16, 0:512],
                        lhsT=KVA[:, 2080 + h * 16:2096 + h * 16],
                        rhs=qsrc(h)[:, f * 512:(f + 1) * 512],
                        start=True, stop=True)
                em.op("tensor", fqkg, r=("kglob", "QB"), w=(f"ps{sb_}",),
                      tag="gf.qk")
                def fexg(eng, sb_=sb_, buf=buf, poff=poff):
                    return eng.activation(
                        out=buf[0:16, poff:poff + 512],
                        in_=ctx.ps[sb_][0:16, 0:512], func=AF.Exp, scale=scale,
                        bias=ctx.sb["c_mc"][0:16, cfg["mcc"]:cfg["mcc"] + 1])
                em.op("scalar", fexg, r=(f"ps{sb_}", "c_mc"), w=(pres,),
                      tag="gf.exp")
            else:
                rx = it - goff
                if rx % 4 == 0:
                    emit_loads(rx)
                kt = kt_list[rx]
                si = rx % 8
                jlo, jhi, diag_j, uedge_j = jrange(kt)
                nq = jhi - jlo + 1
                def fqk(eng, h=h, si=si, f=f, jlo=jlo, nq=nq, sb_=sb_,
                        qb_=qb_):
                    return eng.matmul(
                        ctx.ps[sb_][:, 0:nq * 128],
                        lhsT=KVA[qb_:qb_ + dh, koff[si]:koff[si] + 128],
                        rhs=qsrc(h)[:, f * 512 + jlo * 128:f * 512 + (jlo + nq) * 128],
                        start=True, stop=True)
                em.op("tensor", fqk, r=(f"kslot{si}", "QB"), w=(f"ps{sb_}",),
                      tag="fl.qk")
                def fex(eng, sb_=sb_, buf=buf, poff=poff, nq=nq):
                    return eng.activation(out=buf[:, poff:poff + nq * 128],
                                          in_=ctx.ps[sb_][:, 0:nq * 128],
                                          func=func, **akw)
                em.op("scalar", fex, r=(f"ps{sb_}", "c_mc"), w=(pres,),
                      tag="fl.ex")
                for (mj, mask) in ((diag_j, "c_maskL"), (uedge_j, "c_maskU")):
                    if mj is None:
                        continue
                    def fmk(eng, buf=buf, poff=poff, mj=mj, jlo=jlo, mask=mask):
                        return eng.tensor_mul(
                            out=buf[:, poff + (mj - jlo) * 128:poff + (mj - jlo + 1) * 128],
                            in0=buf[:, poff + (mj - jlo) * 128:poff + (mj - jlo + 1) * 128],
                            in1=ctx.sb[mask][:])
                    em.op("vector", fmk, r=(pres, mask), w=(pres,),
                          tag="fl.mk")
            if it >= 3:
                stage_av(it - 3)
        for it in range(max(0, n_it - 3), n_it):
            stage_av(it)
        ob = OBANK[h]
        rsl = 1024 + (h % 2) * 512
        rres = "SC_C" if h % 2 == 0 else "SC_D"
        esl = (h % 2) * 512
        eres = "pb2" if h % 2 == 0 else "pb3"
        if base == "l3a":
            den_in = (lambda ob=ob: ctx.ps[ob][64:65, 0:512])
            den_res = f"ps{ob}"
        else:
            den_in = (lambda db=DBANK[h]: ctx.ps[db][0:1, 0:512])
            den_res = f"ps{DBANK[h]}"
        if cfg.get("denom_eps"):
            def fre(eng, rsl=rsl, den_in=den_in):
                return eng.tensor_scalar(
                    out=ctx.sb["SC"][0:1, rsl:rsl + 512], in0=den_in(),
                    scalar1=cfg["denom_eps"], scalar2=None, op0=ALU.add)
            em.op("vector", fre, r=(den_res, rres), w=(rres,), tag="fl.rce")
            def fr(eng, rsl=rsl):
                return eng.reciprocal(out=ctx.sb["SC"][0:1, rsl:rsl + 512],
                                      in_=ctx.sb["SC"][0:1, rsl:rsl + 512])
            em.op("vector", fr, r=(rres,), w=(rres,), tag="fl.rcp")
        else:
            def fr(eng, rsl=rsl, den_in=den_in):
                return eng.reciprocal(out=ctx.sb["SC"][0:1, rsl:rsl + 512],
                                      in_=den_in())
            em.op("vector", fr, r=(den_res, rres), w=(rres,), tag="fl.rcp")
        def fbc(eng, rsl=rsl):
            return eng.matmul(
                ctx.ps["tr"][:].bitcast(F32)[:, 0:512],
                lhsT=ctx.sb["c_onesrf"][:],
                rhs=ctx.sb["SC"][0:1, rsl:rsl + 512],
                start=True, stop=True)
        em.op("tensor", fbc, r=(rres, "c_onesrf"), w=("ps_tr",), tag="fl.bc")
        def fev(eng, esl=esl):
            return eng.copy(out=ES[:, esl:esl + 512],
                            in_=ctx.ps["tr"][:].bitcast(F32)[:, 0:512])
        em.op("scalar", fev, r=("ps_tr", eres), w=(eres,), tag="fl.be")
        if base == "l3a":
            def fmu(eng, h=h, ob=ob, esl=esl):
                return eng.tensor_mul(
                    out=OF[(h % 2) * 64:(h % 2) * 64 + 64,
                           (h // 2) * 512:(h // 2) * 512 + 512],
                    in0=ctx.ps[ob][0:dv, 0:512], in1=ES[0:dv, esl:esl + 512])
            em.op("vector", fmu, r=(f"ps{ob}", eres), w=("OFr",), tag="fl.om")
        else:
            def fmu(eng, h=h, ob=ob, esl=esl):
                return eng.tensor_mul(
                    out=OF[:, h * 512:(h + 1) * 512],
                    in0=ctx.ps[ob][0:dv, 0:512], in1=ES[:, esl:esl + 512])
            em.op("vector", fmu, r=(f"ps{ob}", eres), w=("OFr",), tag="fl.om")


def partial_out_fat(ctx, cfg, f):
    em = ctx.em
    OF, RT = ctx.sb["OF"], ctx.sb["RT"]
    OW = ctx.sb["OB"][:][:, 4096:8192].rearrange("p (c f2) -> p c f2", f2=2048)
    for ts_ in range(4):
        for half in range(2):
            for mf in (2 * half, 2 * half + 1):
                bank = mf % 2 + 2
                for c in range(2):
                    def fm(eng, ts_=ts_, mf=mf, c=c, bank=bank):
                        return eng.matmul(
                            ctx.ps[bank][:, 0:512],
                            lhsT=OF[:, c * 512 + ts_ * 128:c * 512 + (ts_ + 1) * 128],
                            rhs=OW[:, c, mf * 512:(mf + 1) * 512],
                            start=(c == 0), stop=(c == 1))
                    em.op("tensor", fm, r=("OFr", "OB_W"), w=(f"ps{bank}",),
                          tag="po.mm")
                def fe(eng, mf=mf, bank=bank):
                    return eng.copy(out=RT[:, (mf % 2) * 512:(mf % 2) * 512 + 512],
                                    in_=ctx.ps[bank][:, 0:512])
                em.op("scalar", fe, r=(f"ps{bank}", "RTpo"), w=("RTpo",),
                      tag="po.ev")
            def fst(eng, f=f, ts_=ts_, half=half):
                return eng.dma_start(
                    out=ctx.dram["rs_in"][f * 512 + ts_ * 128:f * 512 + (ts_ + 1) * 128,
                                          half * 1024:(half + 1) * 1024],
                    in_=RT[:, 0:1024])
            em.op("sync", fst, r=("RTpo",), w=("rs_in",), dma=True, tag="po.st")


USE_GELU_TANH_LUT = True


def gelu_tanh_chain(ctx, gate_ps, psres, up_ps, upres, out_ap, out_res):
    em = ctx.em
    SC, SC2 = ctx.sb["SC"], ctx.sb["SC2"]
    if USE_GELU_TANH_LUT:
        def f1(eng):
            return eng.activation(out=SC[:, 0:512], in_=gate_ps,
                                  func=AF.Gelu_apprx_tanh)
        em.op("scalar", f1, r=(psres,), w=("SC_A",), tag="gl.g")
        def f2(eng):
            return eng.tensor_tensor(out=out_ap, in0=SC[:, 0:512], in1=up_ps,
                                     op=ALU.mult)
        em.op("vector", f2, r=("SC_A", upres), w=(out_res,), tag="gl.m")
        return
    def f1(eng):
        return eng.activation(out=SC[:, 0:512], in_=gate_ps, func=AF.Square)
    em.op("scalar", f1, r=(psres,), w=("SC_A",), tag="gl.sq")
    def f2(eng):
        return eng.tensor_scalar(out=SC[:, 0:512], in0=SC[:, 0:512],
                                 scalar1=0.044715 * 0.7978845608028654,
                                 scalar2=0.7978845608028654,
                                 op0=ALU.mult, op1=ALU.add)
    em.op("vector", f2, r=("SC_A",), w=("SC_A",), tag="gl.af")
    def f3(eng):
        return eng.tensor_tensor(out=SC[:, 512:1024], in0=SC[:, 0:512],
                                 in1=gate_ps, op=ALU.mult)
    em.op("vector", f3, r=("SC_A", psres), w=("SC_B",), tag="gl.p")
    def f4(eng):
        return eng.activation(out=SC2[:, 0:512], in_=SC[:, 512:1024],
                              func=AF.Tanh)
    em.op("scalar", f4, r=("SC_B",), w=("gl_t",), tag="gl.t")
    def f5(eng):
        return eng.tensor_scalar(out=SC[:, 512:1024], in0=SC2[:, 0:512],
                                 scalar1=1.0, scalar2=0.5,
                                 op0=ALU.add, op1=ALU.mult)
    em.op("vector", f5, r=("gl_t",), w=("SC_B",), tag="gl.h")
    def f6(eng):
        return eng.tensor_tensor(out=SC[:, 0:512], in0=SC[:, 512:1024],
                                 in1=gate_ps, op=ALU.mult)
    em.op("vector", f6, r=("SC_B", psres), w=("SC_A",), tag="gl.xg")
    def f7(eng):
        return eng.tensor_tensor(out=out_ap, in0=SC[:, 0:512], in1=up_ps,
                                 op=ALU.mult)
    em.op("vector", f7, r=("SC_A", upres), w=(out_res,), tag="gl.up")


def ffn_down_group(ctx, w2name, gi, w2row0, dacc, first):
    em = ctx.em
    QB, WB, SC = ctx.sb["QB"], ctx.sb["WB"], ctx.sb["SC"]
    for dmf in range(4):
        def fw(eng, dmf=dmf, w2row0=w2row0):
            return eng.dma_start(
                out=WB[:, 0:8192].rearrange("p (c f2) -> p c f2", f2=512),
                in_=ctx.wc[w2name][w2row0:w2row0 + 2048,
                                   dmf * 512:(dmf + 1) * 512]
                    .rearrange("(c p) f2 -> p c f2", p=128))
        em.op("sync", fw, r=(("wc", w2name), "WB"), w=("WB",), dma=True,
              tag="fd.w")
        for ts_ in range(8):
            bank = 2 + ts_ % 2
            for cg in range(16):
                def fm(eng, ts_=ts_, cg=cg, bank=bank):
                    return eng.matmul(
                        ctx.ps[bank][:, 0:512],
                        lhsT=QB[:, cg * 1024 + ts_ * 128:cg * 1024 + (ts_ + 1) * 128],
                        rhs=WB[:, cg * 512:(cg + 1) * 512],
                        start=(cg == 0), stop=(cg == 15))
                em.op("tensor", fm, r=("QB", "WB"), w=(f"ps{bank}",), tag="fd.mm")
            sl = (ts_ % 2) * 512
            def fe(eng, bank=bank, sl=sl):
                return eng.tensor_copy(out=SC[:, sl:sl + 512],
                                       in_=ctx.ps[bank][:, 0:512])
            em.op("vector", fe, r=(f"ps{bank}",), w=(f"fd{sl}",), tag="fd.ev")
            def fst(eng, ts_=ts_, dmf=dmf, sl=sl, first=first):
                return eng.dma_start(
                    out=dacc[ts_ * 128:(ts_ + 1) * 128,
                             dmf * 512:(dmf + 1) * 512],
                    in_=SC[:, sl:sl + 512],
                    accum_op=(ALU.bypass if first else ALU.add))
            em.op("gpsimd", fst, r=(f"fd{sl}", "dacc"), w=("dacc",), dma=True,
                  tag="fd.st")


def ffn_site(ctx, site):
    em = ctx.em
    base = BASE.get(site, site)
    QB, WB, SC, ST, DN = (ctx.sb["QB"], ctx.sb["WB"], ctx.sb["SC"],
                          ctx.sb["ST"], ctx.sb["DN"])
    XN = ctx.sb["XN"]
    dacc = ctx.dram["dacc"]
    ln_to_xn(ctx, kind="ln")
    em.claim("QB", "WB", tag="ffn.cl")
    if base == "l1f":
        for c0 in range(0, 16, 8):
            def fwr(eng, c0=c0):
                return eng.dma_start(out=ctx.sb["RT"][0:16, 0:1024],
                                     in_=ctx.inp["L1_ffn_Wr"][:, c0 * 128:(c0 + 8) * 128])
            em.op("gpsimd", fwr, r=("RT",), w=("RT",), dma=True, tag="rt.ld")
            for j in range(8):
                def ft(eng, j=j):
                    return eng.transpose(ctx.ps["tr"][:, j * 16:(j + 1) * 16],
                                         ctx.sb["RT"][0:16, j * 128:(j + 1) * 128],
                                         ctx.sb["c_ident"][0:16, 0:16])
                em.op("tensor", ft, r=("RT", "c_ident"), w=("ps_tr",), tag="rt.t")
            def fcp(eng, c0=c0):
                return eng.tensor_copy(out=ctx.sb["SB16"][:, c0 * 16:(c0 + 8) * 16],
                                       in_=ctx.ps["tr"][:, 0:128])
            em.op("vector", fcp, r=("ps_tr",), w=("wrT",), tag="rt.cp")
        def fbr(eng):
            return eng.dma_start(out=ctx.sb["SG"][:, 32:48],
                                 in_=ctx.inp["L1_ffn_br"].unsqueeze(0)
                                 .broadcast_to((128, 16)))
        em.op("gpsimd", fbr, r=("SG",), w=("SG",), dma=True, tag="br.ld")
        n_exp = 16
    if base == "l3f":
        def fwg(eng):
            return eng.dma_start(out=WB[:, 0:128].rearrange("p (c f2) -> p c f2", f2=8),
                                 in_=ctx.inp["L3_ffn_wg"]
                                 .rearrange("(c p) e -> p c e", p=128))
        em.op("gpsimd", fwg, r=("WB",), w=("WB",), dma=True, tag="wg.ld")
        n_exp = 8
    if base in ("l1f", "l3f"):
        for ts_ in range(8):
            fat, tt0 = divmod(ts_, 4)
            for c in range(NCH):
                def fm(eng, c=c, ts_=ts_, fat=fat, tt0=tt0):
                    rhs = (ctx.sb["SB16"][:, c * 16:(c + 1) * 16] if base == "l1f"
                           else WB[:, c * 8:(c + 1) * 8])
                    return eng.matmul(ctx.ps[6][:, 0:n_exp],
                                      lhsT=XN[:, c, ts_ * 128:(ts_ + 1) * 128],
                                      rhs=rhs, start=(c == 0),
                                      stop=(c == NCH - 1))
                em.op("tensor", fm,
                      r=("XN", "wrT" if base == "l1f" else "WB"),
                      w=("ps6",), tag="gt.mm")
            if base == "l1f":
                def fl_(eng, ts_=ts_):
                    return eng.tensor_tensor(out=SC[:, 0:n_exp],
                                             in0=ctx.ps[6][:, 0:n_exp],
                                             in1=ctx.sb["SG"][:, 32:48],
                                             op=ALU.add)
                em.op("vector", fl_, r=("ps6", "SG"), w=("SC_A",), tag="gt.br")
            else:
                def fl_(eng, ts_=ts_):
                    return eng.tensor_copy(out=SC[:, 0:n_exp],
                                           in_=ctx.ps[6][:, 0:n_exp])
                em.op("vector", fl_, r=("ps6",), w=("SC_A",), tag="gt.cp")
            def fz(eng, ts_=ts_):
                return eng.memset(DN[:, ts_ * 16:ts_ * 16 + n_exp], 0.0)
            em.op("vector", fz, w=("DNa",), tag="gt.z")
            for it in range(4):
                def f1(eng):
                    return eng.reduce_max(out=ST[:, 0:1], in_=SC[:, 0:n_exp],
                                          axis=AXX)
                em.op("vector", f1, r=("SC_A",), w=("gtm",), tag="gt.mx")
                def f2(eng):
                    return eng.tensor_scalar(out=SC[:, 64:64 + n_exp],
                                             in0=SC[:, 0:n_exp],
                                             scalar1=ST[:, 0:1], scalar2=None,
                                             op0=ALU.is_ge)
                em.op("vector", f2, r=("SC_A", "gtm"), w=("gtge",), tag="gt.ge")
                def f3(eng, ts_=ts_):
                    return eng.tensor_tensor(out=DN[:, ts_ * 16:ts_ * 16 + n_exp],
                                             in0=DN[:, ts_ * 16:ts_ * 16 + n_exp],
                                             in1=SC[:, 64:64 + n_exp], op=ALU.add)
                em.op("vector", f3, r=("gtge", "DNa"), w=("DNa",), tag="gt.acc")
                def f4a(eng):
                    return eng.tensor_scalar(out=SC[:, 96:96 + n_exp],
                                             in0=SC[:, 64:64 + n_exp],
                                             scalar1=-1e30, scalar2=None,
                                             op0=ALU.mult)
                em.op("vector", f4a, r=("gtge",), w=("gtsu",), tag="gt.sa")
                def f4(eng):
                    return eng.tensor_tensor(out=SC[:, 0:n_exp],
                                             in0=SC[:, 96:96 + n_exp],
                                             in1=SC[:, 0:n_exp], op=ALU.add)
                em.op("vector", f4, r=("gtsu", "SC_A"), w=("SC_A",), tag="gt.su")
    if base == "l2f":
        n_chunks, w2name = 128, "l2f_d"
        def gu_tile(name, c, ng):
            r = ng // 16
            return ctx.wc[name][2048 * r + c * 128:2048 * r + (c + 1) * 128,
                               (ng % 16) * 128:(ng % 16 + 1) * 128]
        for gi in range(8):
            for cg in range(16):
                ng = gi * 16 + cg
                def fwg_(eng, ng=ng):
                    return eng.dma_start(
                        out=WB[:, 0:2048].rearrange("p (c f2) -> p c f2", f2=128),
                        in_=ctx.wc["l2f_g"][2048 * (ng // 16):2048 * (ng // 16) + 2048,
                                            (ng % 16) * 128:(ng % 16 + 1) * 128]
                            .rearrange("(c p) f2 -> p c f2", p=128))
                em.op("sync", fwg_, r=(("wc", "l2f_g"), "WB"), w=("WB",),
                      dma=True, tag="l2.w")
                def fwu_(eng, ng=ng):
                    return eng.dma_start(
                        out=WB[:, 2048:4096].rearrange("p (c f2) -> p c f2", f2=128),
                        in_=ctx.wc["l2f_u"][2048 * (ng // 16):2048 * (ng // 16) + 2048,
                                            (ng % 16) * 128:(ng % 16 + 1) * 128]
                            .rearrange("(c p) f2 -> p c f2", p=128))
                em.op("sync", fwu_, r=(("wc", "l2f_u"), "WB"), w=("WB",),
                      dma=True, tag="l2.wu")
                for fat in range(2):
                    for c in range(NCH):
                        def fg(eng, c=c, fat=fat):
                            return eng.matmul(ctx.ps[0][:, 0:512],
                                              lhsT=WB[:, c * 128:(c + 1) * 128],
                                              rhs=XN[:, c, fat * 512:(fat + 1) * 512],
                                              start=(c == 0), stop=(c == NCH - 1))
                        em.op("tensor", fg, r=("WB", "XN"), w=("ps0",), tag="l2.g")
                        def fu(eng, c=c, fat=fat):
                            return eng.matmul(ctx.ps[1][:, 0:512],
                                              lhsT=WB[:, 2048 + c * 128:2048 + (c + 1) * 128],
                                              rhs=XN[:, c, fat * 512:(fat + 1) * 512],
                                              start=(c == 0), stop=(c == NCH - 1))
                        em.op("tensor", fu, r=("WB", "XN"), w=("ps1",), tag="l2.u")
                    out_ap = QB[:, cg * 1024 + fat * 512:cg * 1024 + (fat + 1) * 512]
                    gelu_tanh_chain(ctx, ctx.ps[0][:, 0:512], "ps0",
                                    ctx.ps[1][:, 0:512], "ps1", out_ap, "QB")
            ffn_down_group(ctx, w2name, gi, gi * 2048, dacc, gi == 0)
    else:
        w1name = "l1f_w1" if base == "l1f" else "l3f_w1"
        w2name = "l1f_w2" if base == "l1f" else "l3f_w2"
        n_groups = 4
        for gi in range(n_groups):
            for ii in range(4):
                ncol0 = gi * 2048 + ii * 512
                def fw1(eng, ncol0=ncol0):
                    return eng.dma_start(
                        out=WB[:, 0:8192].rearrange("p (c f2) -> p c f2", f2=512),
                        in_=ctx.wc[w1name][:, ncol0:ncol0 + 512]
                            .rearrange("(c p) f2 -> p c f2", p=128))
                em.op("sync", fw1, r=(("wc", w1name), "WB"), w=("WB",),
                      dma=True, tag="f13.w1")
                for ts_ in range(8):
                    fat, tt0 = divmod(ts_, 4)
                    bank = ts_ % 2
                    for c in range(NCH):
                        def fm(eng, c=c, ts_=ts_, bank=bank, fat=fat, tt0=tt0):
                            return eng.matmul(
                                ctx.ps[bank][:, 0:512],
                                lhsT=XN[:, c, fat * 512 + tt0 * 128:fat * 512 + (tt0 + 1) * 128],
                                rhs=WB[:, c * 512:(c + 1) * 512],
                                start=(c == 0), stop=(c == NCH - 1))
                        em.op("tensor", fm, r=("XN", "WB"), w=(f"ps{bank}",),
                              tag="f13.h")
                    ncol_abs = ncol0
                    if base == "l1f":
                        blk_i = ncol_abs // 512
                        def fe1(eng, bank=bank):
                            return eng.activation(out=SC[:, 0:512],
                                                  in_=ctx.ps[bank][:, 0:512],
                                                  func=AF.Erf,
                                                  scale=0.7071067811865476)
                        em.op("scalar", fe1, r=(f"ps{bank}",), w=("SC_A",),
                              tag="f13.erf")
                        def fe2(eng, bank=bank):
                            return eng.scalar_tensor_tensor(
                                out=SC[:, 512:1024], in0=SC[:, 0:512],
                                scalar=1.0, in1=ctx.ps[bank][:, 0:512],
                                op0=ALU.add, op1=ALU.mult)
                        em.op("vector", fe2, r=("SC_A", f"ps{bank}"),
                              w=("SC_B",), tag="f13.g")
                        def fe3(eng, ts_=ts_, blk_i=blk_i):
                            return eng.tensor_scalar(
                                out=ctx.sb["RT"][:, 0:512], in0=SC[:, 512:1024],
                                scalar1=DN[:, ts_ * 16 + blk_i:ts_ * 16 + blk_i + 1],
                                scalar2=0.5, op0=ALU.mult, op1=ALU.mult)
                        em.op("vector", fe3, r=("SC_B", "DNa"), w=("RTff",),
                              tag="f13.m")
                    else:
                        e_i = ncol_abs // 1024
                        def fe3(eng, bank=bank, ts_=ts_, e_i=e_i):
                            return eng.tensor_scalar(
                                out=ctx.sb["RT"][:, 0:512],
                                in0=ctx.ps[bank][:, 0:512],
                                scalar1=0.0,
                                scalar2=DN[:, ts_ * 16 + e_i:ts_ * 16 + e_i + 1],
                                op0=ALU.max, op1=ALU.mult)
                        em.op("vector", fe3, r=(f"ps{bank}", "DNa"), w=("RTff",),
                              tag="f13.rm")
                    for nc_ in range(4):
                        def ft(eng, nc_=nc_):
                            return eng.transpose(
                                ctx.ps["tr"][:, nc_ * 128:(nc_ + 1) * 128],
                                ctx.sb["RT"][:, nc_ * 128:(nc_ + 1) * 128],
                                ctx.sb["c_ident"][:])
                        em.op("tensor", ft, r=("RTff", "c_ident"), w=("ps_tr",),
                              tag="f13.t")
                    def fcp2(eng, ii=ii, ts_=ts_):
                        return eng.tensor_copy(
                            out=QB[:].rearrange("p (a b2) -> p a b2", b2=1024)
                                [:, ii * 4:(ii + 1) * 4, ts_ * 128:(ts_ + 1) * 128],
                            in_=ctx.ps["tr"][:, 0:512]
                                .rearrange("p (a b2) -> p a b2", b2=128))
                    em.op("vector", fcp2, r=("ps_tr",), w=("QB",), tag="f13.cp")
            ffn_down_group(ctx, w2name, gi, gi * 2048, dacc, gi == 0)
    scale = 0.5
    add_tokmajor_dram_to_R(ctx, dacc, scale, extra_r=("dacc",))


def build_program(subs):
    nc = bass.Bass("TRN2", target_bir_lowering=False, debug=False)
    em = Emitter()
    ctx = Ctx(nc, em, subs)
    bases = [BASE.get(s, s) for s in subs]
    needed_in = set()
    needed_w = []
    for b in bases:
        for x in SUB_IN[b]:
            needed_in.add(x)
        for wname in SUB_W[b]:
            if wname not in needed_w:
                needed_w.append(wname)
    ctx.inp["res_in"] = nc.dram_tensor("res_in", [D, TPC], F32,
                                       kind="ExternalInput").ap()
    for name in sorted(needed_in):
        shp, dt = INPUT_SHAPES[name]
        ctx.inp[name] = nc.dram_tensor(name, list(shp), dt,
                                       kind="ExternalInput").ap()
    for name, (shape, dt) in SBUF_CONSTS.items():
        ctx.inp[name] = nc.dram_tensor(name, shape, dt, kind="ExternalInput").ap()
    any_rope = any(b in ("l2a", "l3a") for b in bases)
    for name, (shape, dt) in DRAM_CONSTS.items():
        if any_rope:
            ctx.inp[name] = nc.dram_tensor(name, shape, dt,
                                           kind="ExternalInput").ap()
    ctx.out_res = nc.dram_tensor("res_out", [D, TPC], F32,
                                 kind="ExternalOutput").ap()
    for wname in needed_w:
        s = WSPEC[wname]
        ctx.wc[wname] = nc.dram_tensor("wc_" + wname, [s["rows"], s["cols"]],
                                       BF16, kind="Internal").ap()
        if s["kind"] in ("deq_gu", "deq_d"):
            ctx.dram["gin_" + wname] = nc.dram_tensor(
                "gin_" + wname, [2048, 2048], BF16, kind="Internal").ap()
    has_attn = any(b in ("l0f", "l0a", "l1a", "l2a", "l3a") for b in bases)
    if has_attn:
        ctx.dram["xg_in"] = nc.dram_tensor("xg_in", [D, TPC], BF16,
                                           kind="Internal").ap()
        ctx.dram["xg"] = nc.dram_tensor("xg", [NCORE * D, TPC], BF16,
                                        kind="Internal").ap()
        ctx.dram["kloc"] = nc.dram_tensor("kloc", [256, T + 16], BF16,
                                          kind="Internal").ap()
        ctx.dram["vloc"] = nc.dram_tensor("vloc", [T + 16, 256], BF16,
                                          kind="Internal").ap()
        ctx.dram["rs_in"] = nc.dram_tensor("rs_in", [T, D], BF16,
                                           kind="Internal").ap()
        ctx.dram["rs_out"] = nc.dram_tensor("rs_out", [TPC, D], BF16,
                                            kind="Internal").ap()
    if any(b in ("l1f", "l2f", "l3f") for b in bases):
        ctx.dram["dacc"] = nc.dram_tensor("dacc", [TPC, D], F32,
                                          kind="Internal").ap()
    dbg_shapes = {"xn": ([128, NCH, TPC], BF16), "xg": ([NCORE * D, TPC], BF16),
                  "kloc": ([256, T + 16], BF16), "vloc": ([T + 16, 256], BF16),
                  "rsin": ([T, D], BF16), "rsout": ([TPC, D], BF16),
                  "wcq": ([2048, 256], BF16), "sg": ([128, 16], F32),
                  "qb": ([128, 16384], BF16)}
    for t in TAPS:
        shp, dt = dbg_shapes[t]
        ctx.dram["dbg_" + t] = nc.dram_tensor("dbg_" + t, list(shp), dt,
                                              kind="ExternalOutput").ap()

    with nc.Block() as block:
        alloc_common(ctx)
        def fwarm0(eng):
            return eng.memset(ctx.sb["LR"][32:33, 504:506], 0.0)
        em.op("vector", fwarm0, w=("warm",), tag="warmup0")
        def fsgz(eng):
            return eng.memset(ctx.sb["SG"][:], 0.0)
        em.op("vector", fsgz, w=("SG",), tag="sgz")
        def fwarm(eng):
            return eng.activation(out=ctx.sb["LR"][32:33, 506:508],
                                  in_=ctx.sb["LR"][32:33, 504:506], func=AF.Exp)
        em.op("scalar", fwarm, r=("warm",), w=("warm",), tag="warmup")
        def fres(eng):
            return eng.dma_start(
                out=ctx.sb["R"][:],
                in_=ctx.inp["res_in"].rearrange("(c p) t -> p c t", p=128))
        em.op("sync", fres, w=("R",), dma=True, tag="ld.res")
        prep_tables(ctx, subs)
        for wname in needed_w:
            prep_matrix(ctx, wname)
        for site in subs:
            base = BASE.get(site, site)
            if base in ("l0f", "l0a", "l1a", "l2a", "l3a"):
                attn_site(ctx, site)
            else:
                ffn_site(ctx, site)
        def fout(eng):
            return eng.dma_start(
                out=ctx.out_res.rearrange("(c p) t -> p c t", p=128),
                in_=ctx.sb["R"][:])
        em.op("sync", fout, r=("R",), w=("res_out",), dma=True, tag="st.res")
        em.emit(nc, block)
    return nc


INPUT_SHAPES = {}


def _init_input_shapes():
    sp = {
        "L0_ffn_Wp": ((2048, 2048), F32),
        "hs_l0f_wstd": ((2,), F32), "hs_l0f_wrec": ((2,), F32),
        "hs_l0f_waq": ((2048, 240), F32), "hs_l0f_wak": ((2048, 240), F32),
        "hs_l0f_waql": ((2048, 16), F32), "hs_l0f_wakl": ((2048, 16), F32),
        "hs_l0f_wv": ((2048, 256), F32), "hs_l0f_wosl": ((256, 2048), F32),
        "L0_attn_global_tokens": ((1, 16, 2048), F32),
        "L0_attn_ln_scale": ((2048,), F32), "L0_attn_ln_bias": ((2048,), F32),
        "L1_ffn_W1": ((8192, 2048), F32), "L1_ffn_W2": ((2048, 8192), F32),
        "L1_ffn_Wr": ((16, 2048), F32), "L1_ffn_br": ((16,), F32),
        "L1_attn_gamma": ((128,), F32), "L1_attn_beta": ((128,), F32),
        "L2_attn_W_kv": ((256, 2048), F32),
        "hs_l2a_wkh": ((2, 128, 128), F32), "hs_l2a_wvh": ((2, 128, 128), F32),
        "hs_l2f_gpk": ((2048, 1024), I32), "hs_l2f_gsc": ((2048, 64), F32),
        "hs_l2f_upk": ((2048, 1024), I32), "hs_l2f_usc": ((2048, 64), F32),
        "hs_l2f_dpk": ((2048, 1024), I32), "hs_l2f_dsc": ((2048, 64), F32),
        "L3_ffn_wg": ((2048, 8), F32),
        "L3_ffn_w1": ((8, 1024, 2048), F32), "L3_ffn_w2": ((8, 2048, 1024), F32),
        "L3_attn_rms_w": ((2048,), F32),
        "hs_l3a_wq": ((256, 2048), F32), "hs_l3a_wkv": ((128, 2048), F32),
        "hs_l3a_wosl": ((256, 2048), F32),
    }
    for tag in ("l0a", "l1a"):
        for w in ("wq", "wk", "wv", "wosl"):
            sp[f"hs_{tag}_{w}"] = ((256, 2048), F32)
    sp["hs_l2a_wq"] = ((256, 2048), F32)
    sp["hs_l2a_wosl"] = ((256, 2048), F32)
    INPUT_SHAPES.update(sp)


_init_input_shapes()


def kernel(**inputs):
    consts = _host_consts()
    x = np.asarray(inputs["L0_attn_x"])[0]
    res_shards = []
    for c in range(NCORE):
        bA, bB = OWN_BLOCKS[c]
        sl = np.concatenate([x[bA * BLK:(bA + 1) * BLK],
                             x[bB * BLK:(bB + 1) * BLK]], 0)
        res_shards.append(np.ascontiguousarray(sl.T))
    slices = [make_host_slices(inputs, c) for c in range(NCORE)]
    for group in LAUNCH_GROUPS:
        nc = build_program(group)
        bases = [BASE.get(s, s) for s in group]
        needed = set()
        for b in bases:
            needed.update(SUB_IN[b])
        any_rope = any(b in ("l2a", "l3a") for b in bases)
        in_maps = []
        for c in range(NCORE):
            m = {"res_in": res_shards[c]}
            for name in needed:
                m[name] = (slices[c][name] if name.startswith("hs_")
                           else np.ascontiguousarray(inputs[name]))
            for cname in SBUF_CONSTS:
                m[cname] = consts[cname]
            if any_rope:
                for cname in DRAM_CONSTS:
                    m[cname] = consts[cname]
            in_maps.append(m)
        res = run_bass_kernel_spmd(nc, in_maps, core_ids=list(range(NCORE)))
        if TAPS:
            TAP_RESULTS.append([{k: np.asarray(v) for k, v in r.items()
                                 if k.startswith("dbg_")} for r in res.results])
        res_shards = [np.asarray(r["res_out"]) for r in res.results]
    out = np.empty((T, D), np.float32)
    for c in range(NCORE):
        bA, bB = OWN_BLOCKS[c]
        sh = res_shards[c].T
        out[bA * BLK:(bA + 1) * BLK] = sh[0:BLK]
        out[bB * BLK:(bB + 1) * BLK] = sh[BLK:]
    return out[None]


# version 06
# speedup vs baseline: 1.0043x; 1.0043x over previous
import math
import numpy as np
import ml_dtypes
import concourse.bass as bass
import concourse.mybir as mybir
from concourse.bass_utils import run_bass_kernel_spmd

F32 = mybir.dt.float32
BF16 = mybir.dt.bfloat16
I32 = mybir.dt.int32
AF = mybir.ActivationFunctionType
ALU = mybir.AluOpType
AXX = mybir.AxisListType.X

NCORE = 8
D = 2048
T = 8192
NCH = 16
TPC = 1024
NBLK = 16
BLK = 512
NFAT = 16

AB = []
for _r in range(NCORE):
    AB += [_r, NBLK - 1 - _r]
FAT_OF_BLOCK = {b: f for f, b in enumerate(AB)}
OWN_BLOCKS = [(c, NBLK - 1 - c) for c in range(NCORE)]

def gcol_of_abs(tok):
    b = tok // BLK
    return FAT_OF_BLOCK[b] * BLK + tok % BLK

M_L0F = 10.0
M_L0A = 12.0
M_L2A = 40.0
M_L3A = 12.0

LN_EPS = 1e-5
SUBLAYERS = ["l0f", "l0a", "l0f2", "l1f", "l1a", "l1f2",
             "l2f", "l2a", "l2f2", "l3f", "l3a", "l3f2"]
BASE = {"l0f2": "l0f", "l1f2": "l1f", "l2f2": "l2f", "l3f2": "l3f"}
LAUNCH_GROUPS = [list(SUBLAYERS)]
TAPS = []
TAP_RESULTS = []


def _host_consts():
    pos = np.array([AB[f] * BLK + i for f in range(NFAT) for i in range(BLK)],
                   np.float64)
    fr2 = 1.0 / (10000.0 ** (np.arange(32, dtype=np.float64) / 32))
    ang2 = pos[:, None] * fr2[None, :]
    r2c = np.repeat(np.cos(ang2), 2, -1).T
    r2s = np.repeat(np.sin(ang2), 2, -1).T
    r2s = np.concatenate([r2s[32:64], r2s[0:32]], 0)
    r2c = r2c.astype(ml_dtypes.bfloat16)
    r2s = r2s.astype(ml_dtypes.bfloat16)
    fr3 = 1.0 / (10000.0 ** (np.arange(0, 64, 2, dtype=np.float64) / 64))
    ang3 = pos[:, None] * fr3[None, :]
    r3c = np.cos(ang3).T
    r3s = np.sin(ang3).T
    r3c = np.concatenate([r3c, r3c], 0)
    r3s = np.concatenate([r3s, r3s], 0)
    r3c = r3c.astype(ml_dtypes.bfloat16)
    r3s = r3s.astype(ml_dtypes.bfloat16)
    iL = np.arange(128)[:, None]
    jL = np.arange(128)[None, :]
    e8 = np.zeros((16, 128), ml_dtypes.bfloat16)
    for h in range(16):
        e8[h, 8 * h:8 * h + 8] = 1
    return {
        "c_rope2c": r2c, "c_rope2s": r2s, "c_rope3c": r3c, "c_rope3s": r3s,
        "c_ident": np.eye(128, dtype=ml_dtypes.bfloat16),
        "c_maskL": (iL <= jL).astype(ml_dtypes.bfloat16),
        "c_maskU": (iL > jL).astype(ml_dtypes.bfloat16),
        "c_mask120": (np.arange(128) < 120).astype(np.float32)[:, None],
        "c_ones_col": np.ones((128, 1), ml_dtypes.bfloat16),
        "c_ones_row": np.ones((1, 128), ml_dtypes.bfloat16),
        "c_i16": np.eye(16, dtype=ml_dtypes.bfloat16),
        "c_onesrf": np.ones((1, 128), np.float32),
        "c_e32": np.full((128, 512), 0x3F000000, np.int32),
        "c_mc": np.tile(np.array([[-M_L0F, -M_L0A, -M_L2A, -M_L3A,
                                   1e-5, 1e-10, 1e-11, 2.0]], np.float32),
                        (128, 1)),
    }


DRAM_CONSTS = {"c_rope2c": ([64, T], BF16), "c_rope2s": ([64, T], BF16),
               "c_rope3c": ([64, T], BF16), "c_rope3s": ([64, T], BF16)}
SBUF_CONSTS = {"c_ident": ([128, 128], BF16), "c_e32": ([128, 512], I32),
               "c_maskL": ([128, 128], BF16), "c_maskU": ([128, 128], BF16),
               "c_mask120": ([128, 1], F32),
               "c_ones_col": ([128, 1], BF16), "c_ones_row": ([1, 128], BF16),
               "c_onesrf": ([1, 128], F32), "c_i16": ([16, 16], BF16),
               "c_mc": ([128, 8], F32)}


class Op:
    __slots__ = ("engine", "fn", "deps", "dma", "cc", "inc", "idx", "tag")

    def __init__(self, engine, fn, deps, dma, cc, idx, tag):
        self.engine, self.fn, self.deps = engine, fn, deps
        self.dma, self.cc, self.idx, self.tag = dma, cc, idx, tag
        self.inc = False


RES_ALIAS = {}
MULTI_ALIAS = {"esKH": ("kslot0", "kslot1", "kslot2", "kslot3")}
for _b, _names in {
    "SB16": ["rep_ones", "rep_bc", "blA", "blB", "blC", "gT", "brb", "wrT",
             "fm_y"],
    "SG": ["rp_m"],
    "ES": ["lnb0", "lnb512", "esK", "esV", "esd", "ra_t", "esKC", "pb2",
           "pb3"],
    "PB": ["pb0", "pb1", "rp_t", "rp_u", "rp_w"],
    "ST": ["st_w", "st_c", "st_l", "st_s", "st_m", "br", "gtm", "fm_s",
           "fm_s2", "fm_mu", "fm_va", "fm_m2", "fm_l", "fm_r"],
    "SC_A": ["dq_mf", "dq_fix", "gtge", "gtsu", "fm_q", "fd0", "sgt", "sgt2", "sgt3", "sgt4", "sgt5", "sgt6"],
    "SC_B": ["fd512"],
    "SC_C": ["dq_r", "apl0"],
    "SC_D": ["apl512", "rp_std", "rp_rec", "rp_im"],
    "SC2": ["gbt", "st_gb", "r1A", "r1B", "lrb", "lr32", "gl_t", "dq_sf"],
    "RT": ["RTpo", "RTff", "lnq0", "lnq512"],
    "QB": ["dq_pk", "dq_t0", "dq_t1", "dq_m", "dq_s", "dq_B"],
    "LR": ["lr0"],
}.items():
    for _n in _names:
        RES_ALIAS[_n] = _b


class Emitter:
    ENGINES = ("tensor", "scalar", "vector", "gpsimd", "sync")

    def __init__(self):
        self.ops = []
        self.last_writer = {}
        self.readers = {}

    def op(self, engine, fn, r=(), w=(), dma=False, cc=False, tag=""):
        r = tuple(r) + tuple(RES_ALIAS[x] for x in r if x in RES_ALIAS)
        w = tuple(w) + tuple(RES_ALIAS[x] for x in w if x in RES_ALIAS)
        for x in tuple(r):
            if x in MULTI_ALIAS:
                r = r + MULTI_ALIAS[x]
        for x in tuple(w):
            if x in MULTI_ALIAS:
                w = w + MULTI_ALIAS[x]
        deps = set()
        for x in r:
            lw = self.last_writer.get(x)
            if lw is not None:
                deps.add(lw)
        for x in w:
            lw = self.last_writer.get(x)
            if lw is not None:
                deps.add(lw)
            for rd in self.readers.get(x, ()):
                deps.add(rd)
        i = len(self.ops)
        self.ops.append(Op(engine, fn, deps, dma, cc, i, tag))
        for x in r:
            self.readers.setdefault(x, []).append(i)
        for x in w:
            self.last_writer[x] = i
            self.readers[x] = []
        return i

    def claim(self, *res, tag="claim"):
        self.op("vector", lambda eng: eng.engine_nop(), w=res, tag=tag)

    def emit(self, nc, block):
        NPAR = 6
        ops_by_engine = {e: [o for o in self.ops if o.engine == e]
                         for e in self.ENGINES}
        consumers = {i: [] for i in range(len(self.ops))}
        for o in self.ops:
            for d in o.deps:
                consumers[d].append(o.idx)
        for o in self.ops:
            if o.dma or o.cc:
                o.inc = True
            else:
                for cidx in consumers[o.idx]:
                    c = self.ops[cidx]
                    if (c.engine != o.engine or c.dma or o.dma
                            or o.engine in ("scalar", "vector", "gpsimd")):
                        o.inc = True
                        break

        semkey = {}
        semval = {}
        qcount = {}
        for o in self.ops:
            if not o.inc:
                continue
            if o.dma or o.cc:
                q = o.engine + ("_cc" if o.cc else "_dma")
                n = qcount.get(q, 0)
                qcount[q] = n + 1
                k = f"{q}{n % NPAR}"
                semkey[o.idx] = k
                semval[o.idx] = ((n // NPAR) + 1) * (16 if o.dma else 1)
            else:
                k = o.engine
                n = qcount.get(k, 0) + 1
                qcount[k] = n
                semkey[o.idx] = k
                semval[o.idx] = n
        sems = {}
        for idx, k in semkey.items():
            if k not in sems:
                sems[k] = nc.semaphore("sem_" + k).__enter__()
        emitter = self

        def run_engine(eng, ename):
            waited = {}

            def wait(k, v):
                if waited.get(k, -1) >= v:
                    return
                eng.wait_ge(sems[k], v)
                waited[k] = v

            for o in ops_by_engine[ename]:
                need = {}
                for d in o.deps:
                    p = emitter.ops[d]
                    if (p.engine == ename and not p.dma and not p.cc
                            and not o.dma and ename == "tensor"):
                        continue
                    if not p.inc:
                        raise RuntimeError(f"dep without inc: {p.tag} <- {o.tag}")
                    k, v = semkey[p.idx], semval[p.idx]
                    if v > need.get(k, -1):
                        need[k] = v
                for k, v in sorted(need.items()):
                    wait(k, v)
                if o.fn is None:
                    continue
                if o.inc and (o.dma or o.cc):
                    k, v = semkey[o.idx], semval[o.idx]
                    step = 16 if o.dma else 1
                    if v - step > 0:
                        wait(k, v - step)
                inst = o.fn(eng)
                if o.inc:
                    if inst is None:
                        inst = eng.nop()
                    inst.then_inc(sems[semkey[o.idx]], 16 if o.dma else 1)
            if ename in ("gpsimd", "sync"):
                totals = {}
                for o in ops_by_engine[ename]:
                    if o.inc and (o.dma or o.cc):
                        k, v = semkey[o.idx], semval[o.idx]
                        totals[k] = max(totals.get(k, 0), v)
                for k, v in sorted(totals.items()):
                    wait(k, v)

        for ename, meth in (("sync", block.sync), ("tensor", block.tensor),
                            ("scalar", block.scalar), ("vector", block.vector),
                            ("gpsimd", block.gpsimd)):
            if ops_by_engine[ename]:
                meth(lambda eng, _e=ename: run_engine(eng, _e))


def wspec_all():
    S = {}
    def add(name, rows, cols, kind, **kw):
        S[name] = dict(rows=rows, cols=cols, kind=kind, **kw)
    add("l0f_wq", D, 256, "l0qk", side="q")
    add("l0f_wk", D, 256, "l0qk", side="k")
    add("l0f_wv", D, 256, "cast", src="hs_l0f_wv")
    add("l0f_wp", D, D, "cast", src="L0_ffn_Wp")
    add("l0f_wosl", 256, D, "cast", src="hs_l0f_wosl")
    add("l0a_wq", D, 256, "T", src="hs_l0a_wq")
    add("l0a_wk", D, 256, "T", src="hs_l0a_wk")
    add("l0a_wv", D, 256, "T", src="hs_l0a_wv")
    add("l0a_wosl", 256, D, "cast", src="hs_l0a_wosl")
    add("l1a_wq", D, 256, "T", src="hs_l1a_wq")
    add("l1a_wk", D, 256, "T", src="hs_l1a_wk")
    add("l1a_wv", D, 256, "T", src="hs_l1a_wv")
    add("l1a_wosl", 256, D, "cast", src="hs_l1a_wosl")
    add("l1f_w1", D, 8192, "T", src="L1_ffn_W1")
    add("l1f_w2", 8192, D, "T", src="L1_ffn_W2")
    add("l2a_wq", D, 256, "T", src="hs_l2a_wq")
    add("l2a_wkv", D, 256, "T", src="L2_attn_W_kv")
    add("l2a_wosl", 256, D, "cast_vw", src="hs_l2a_wosl")
    add("l2f_g", 16384, D, "deq_gu", pk="hs_l2f_gpk", sc="hs_l2f_gsc")
    add("l2f_u", 16384, D, "deq_gu", pk="hs_l2f_upk", sc="hs_l2f_usc")
    add("l2f_d", 16384, D, "deq_d", pk="hs_l2f_dpk", sc="hs_l2f_dsc")
    add("l3a_wq", D, 256, "T", src="hs_l3a_wq", scale="rms_w")
    add("l3a_wkv", D, 128, "T", src="hs_l3a_wkv", scale="rms_w")
    add("l3a_wosl", 256, D, "cast", src="hs_l3a_wosl")
    add("l3f_w1", D, 8192, "T3", src="L3_ffn_w1")
    add("l3f_w2", 8192, D, "T3b", src="L3_ffn_w2")
    return S


WSPEC = wspec_all()

SUB_W = {
    "l0f": ["l0f_wq", "l0f_wk", "l0f_wv", "l0f_wp", "l0f_wosl"],
    "l0a": ["l0a_wq", "l0a_wk", "l0a_wv", "l0a_wosl"],
    "l1f": ["l1f_w1", "l1f_w2"],
    "l1a": ["l1a_wq", "l1a_wk", "l1a_wv", "l1a_wosl"],
    "l2f": ["l2f_g", "l2f_u", "l2f_d"],
    "l2a": ["l2a_wq", "l2a_wkv", "l2a_wosl"],
    "l3f": ["l3f_w1", "l3f_w2"],
    "l3a": ["l3a_wq", "l3a_wkv", "l3a_wosl"],
}

SUB_IN = {
    "l0f": ["L0_ffn_Wp", "hs_l0f_wstd", "hs_l0f_wrec", "hs_l0f_waq",
            "hs_l0f_wak", "hs_l0f_waql", "hs_l0f_wakl", "hs_l0f_wv",
            "hs_l0f_wosl"],
    "l0a": ["hs_l0a_wq", "hs_l0a_wk", "hs_l0a_wv", "hs_l0a_wosl",
            "L0_attn_global_tokens", "L0_attn_ln_scale", "L0_attn_ln_bias"],
    "l1f": ["L1_ffn_W1", "L1_ffn_W2", "L1_ffn_Wr", "L1_ffn_br"],
    "l1a": ["hs_l1a_wq", "hs_l1a_wk", "hs_l1a_wv", "hs_l1a_wosl",
            "L1_attn_gamma", "L1_attn_beta"],
    "l2f": ["hs_l2f_gpk", "hs_l2f_gsc", "hs_l2f_upk", "hs_l2f_usc",
            "hs_l2f_dpk", "hs_l2f_dsc"],
    "l2a": ["hs_l2a_wq", "L2_attn_W_kv", "hs_l2a_wkh", "hs_l2a_wvh",
            "hs_l2a_wosl"],
    "l3f": ["L3_ffn_wg", "L3_ffn_w1", "L3_ffn_w2"],
    "l3a": ["hs_l3a_wq", "hs_l3a_wkv", "hs_l3a_wosl", "L3_attn_rms_w"],
}


def make_host_slices(inp, core):
    h0 = 2 * core
    out = {}
    wa = inp["L0_ffn_Wa"]
    out["hs_l0f_waq"] = np.ascontiguousarray(wa[:, 120 * h0:120 * (h0 + 2)])
    out["hs_l0f_wak"] = np.ascontiguousarray(wa[:, 1920 + 120 * h0:1920 + 120 * (h0 + 2)])
    out["hs_l0f_waql"] = np.ascontiguousarray(wa[:, 5888 + 8 * h0:5888 + 8 * (h0 + 2)])
    out["hs_l0f_wakl"] = np.ascontiguousarray(wa[:, 6016 + 8 * h0:6016 + 8 * (h0 + 2)])
    out["hs_l0f_wv"] = np.ascontiguousarray(wa[:, 3840 + 128 * h0:3840 + 128 * (h0 + 2)])
    out["hs_l0f_wstd"] = np.ascontiguousarray(inp["L0_ffn_w_std"][h0:h0 + 2])
    out["hs_l0f_wrec"] = np.ascontiguousarray(inp["L0_ffn_w_rec"][h0:h0 + 2])
    out["hs_l0f_wosl"] = np.ascontiguousarray(inp["L0_ffn_Wp"][128 * h0:128 * (h0 + 2), :])
    for tag, pre in (("l0a", "L0_attn_"), ("l1a", "L1_attn_")):
        out[f"hs_{tag}_wq"] = np.ascontiguousarray(inp[pre + "Wq"][128 * h0:128 * (h0 + 2), :])
        out[f"hs_{tag}_wk"] = np.ascontiguousarray(inp[pre + "Wk"][128 * h0:128 * (h0 + 2), :])
        out[f"hs_{tag}_wv"] = np.ascontiguousarray(inp[pre + "Wv"][128 * h0:128 * (h0 + 2), :])
        out[f"hs_{tag}_wosl"] = np.ascontiguousarray(
            inp[pre + "Wo"][:, 128 * h0:128 * (h0 + 2)].T)
    out["hs_l2a_wq"] = np.ascontiguousarray(inp["L2_attn_W_q"][128 * h0:128 * (h0 + 2), :])
    out["hs_l2a_wkh"] = np.ascontiguousarray(inp["L2_attn_W_k_heads"][h0:h0 + 2])
    out["hs_l2a_wvh"] = np.ascontiguousarray(inp["L2_attn_W_v_heads"][h0:h0 + 2])
    out["hs_l2a_wosl"] = np.ascontiguousarray(
        inp["L2_attn_W_o"][:, 128 * h0:128 * (h0 + 2)].T)
    n0 = 2048 * core
    out["hs_l2f_gpk"] = np.ascontiguousarray(inp["L2_ffn_gate_W_packed"][n0:n0 + 2048, :])
    out["hs_l2f_gsc"] = np.ascontiguousarray(inp["L2_ffn_gate_scales"][n0:n0 + 2048, :])
    out["hs_l2f_upk"] = np.ascontiguousarray(inp["L2_ffn_up_W_packed"][n0:n0 + 2048, :])
    out["hs_l2f_usc"] = np.ascontiguousarray(inp["L2_ffn_up_scales"][n0:n0 + 2048, :])
    out["hs_l2f_dpk"] = np.ascontiguousarray(
        inp["L2_ffn_down_W_packed"][:, 1024 * core:1024 * (core + 1)])
    out["hs_l2f_dsc"] = np.ascontiguousarray(
        inp["L2_ffn_down_scales"][:, 64 * core:64 * (core + 1)])
    win = inp["L3_attn_w_in"]
    g = core
    out["hs_l3a_wq"] = np.ascontiguousarray(win[256 * g:256 * (g + 1), :])
    out["hs_l3a_wkv"] = np.ascontiguousarray(np.concatenate(
        [win[2048 + 64 * g:2048 + 64 * (g + 1), :],
         win[2560 + 64 * g:2560 + 64 * (g + 1), :]], 0))
    out["hs_l3a_wosl"] = np.ascontiguousarray(
        inp["L3_attn_w_out"][:, 256 * g:256 * (g + 1)].T)
    return out


class Ctx:
    def __init__(self, nc, em, subs):
        self.nc, self.em, self.subs = nc, em, subs
        self.inp = {}
        self.wc = {}
        self.sb = {}
        self.ps = {}
        self.dram = {}


def alloc_common(ctx):
    nc = ctx.nc
    A = nc.alloc_sbuf_tensor
    ctx.sb["R"] = A("R", [128, NCH, TPC], F32)
    ctx.sb["XN"] = A("XN", [128, NCH, TPC], BF16)
    ctx.sb["QB"] = A("QB", [128, 16384], BF16)
    ctx.sb["OB"] = A("OB", [128, 16384], BF16)
    ctx.sb["WB"] = A("WB", [128, 8192], BF16)
    ctx.sb["SC"] = A("SC", [128, 2048], F32)
    ctx.sb["SC2"] = A("SC2", [128, 1024], F32)
    ctx.sb["ES"] = A("ES", [128, 1024], BF16)
    ctx.sb["PB"] = A("PB", [128, 1024], BF16)
    ctx.sb["KVA"] = A("KVA", [128, 2384], BF16)
    ctx.sb["RT"] = A("RT", [128, 1024], BF16)
    ctx.sb["DN"] = A("DN", [128, 144], F32)
    ctx.sb["SG"] = A("SG", [128, 48], F32)
    ctx.sb["SB16"] = A("SB16", [128, 256], BF16)
    ctx.sb["ST"] = A("ST", [128, 32], F32)
    ctx.sb["LR"] = A("LR", [64, 512], F32)
    ctx.sb["RP"] = A("RP", [128, 1024], BF16)
    ctx.sb["OF"] = ctx.sb["RP"]
    ctx.sb["LS"] = A("LS", [128, 32], F32)
    for i in range(7):
        ctx.ps[i] = nc.alloc_psum_tensor(f"ps{i}", [128, 512], F32)
    ctx.ps["tr"] = nc.alloc_psum_tensor("ps_tr", [128, 1024], BF16)
    for name, (shape, dt) in SBUF_CONSTS.items():
        ctx.sb[name] = A("s_" + name, shape, dt)
        def f(eng, n=name):
            return eng.dma_start(out=ctx.sb[n][:], in_=ctx.inp[n])
        ctx.em.op("sync", f, w=(name,), dma=True, tag="ldc")


def ln_to_xn(ctx, kind="ln", ln_scale=False, r1_rows=False):
    em = ctx.em
    R, XN, SC, LR = ctx.sb["R"], ctx.sb["XN"], ctx.sb["SC"], ctx.sb["LR"]
    ones_col = ctx.sb["c_ones_col"]
    onesrf = ctx.sb["c_onesrf"]
    vscale = {"ln": 1.0, "ln_ln": 1.0 + 1e-5, "ln_rms": 1.0 + 1e-6}[kind]
    mccol = {"ln": 4, "ln_ln": 5, "ln_rms": 6}[kind]
    for fat in range(2):
        t0 = fat * 512
        for c in range(NCH):
            sl = (c % 2) * 512
            def fcp2(eng, c=c, t0=t0, sl=sl):
                return eng.tensor_copy(out=ctx.sb["ES"][:, sl:sl + 512],
                                       in_=R[:, c, t0:t0 + 512])
            em.op("vector", fcp2, r=("R",), w=(f"lnb{sl}",), tag="ln.cp")
            def fsq(eng, c=c, sl=sl):
                return eng.activation(out=ctx.sb["RT"][:, sl:sl + 512],
                                      in_=ctx.sb["ES"][:, sl:sl + 512],
                                      func=AF.Square)
            em.op("scalar", fsq, r=(f"lnb{sl}",), w=(f"lnq{sl}",), tag="ln.sq")
            def fs1(eng, c=c, sl=sl):
                return eng.matmul(ctx.ps[5][0:1, 0:512], lhsT=ones_col[:],
                                  rhs=ctx.sb["ES"][:, sl:sl + 512],
                                  start=(c == 0), stop=(c == NCH - 1))
            em.op("tensor", fs1, r=(f"lnb{sl}", "c_ones_col"), w=("ps5",),
                  tag="ln.s1")
            def fs2(eng, c=c, sl=sl):
                return eng.matmul(ctx.ps[6][0:1, 0:512], lhsT=ones_col[:],
                                  rhs=ctx.sb["RT"][:, sl:sl + 512],
                                  start=(c == 0), stop=(c == NCH - 1))
            em.op("tensor", fs2, r=(f"lnq{sl}", "c_ones_col"), w=("ps6",),
                  tag="ln.s2")
        def fmu(eng):
            return eng.tensor_scalar(out=ctx.ps[5][0:1, 0:512],
                                     in0=ctx.ps[5][0:1, 0:512],
                                     scalar1=1.0 / D, scalar2=None, op0=ALU.mult)
        em.op("vector", fmu, r=("ps5",), w=("ps5",), tag="ln.mu")
        def fes(eng):
            return eng.tensor_scalar(out=ctx.ps[4][0:1, 0:512],
                                     in0=ctx.ps[6][0:1, 0:512],
                                     scalar1=1.0 / D, scalar2=None, op0=ALU.mult)
        em.op("vector", fes, r=("ps6",), w=("ps4",), tag="ln.es")
        def fcesq(eng):
            return eng.tensor_copy(out=LR[0:1, :], in_=ctx.ps[4][0:1, 0:512])
        em.op("vector", fcesq, r=("ps4",), w=("lr0",), tag="ln.ce")
        def fmu2(eng):
            return eng.activation(out=ctx.ps[6][0:1, 0:512],
                                  in_=ctx.ps[5][0:1, 0:512], func=AF.Square)
        em.op("scalar", fmu2, r=("ps5",), w=("ps6",), tag="ln.mu2")
        def fvar(eng):
            return eng.tensor_tensor(out=ctx.ps[6][0:1, 0:512], in0=LR[0:1, :],
                                     in1=ctx.ps[6][0:1, 0:512], op=ALU.subtract)
        em.op("vector", fvar, r=("lr0", "ps6"), w=("ps6",), tag="ln.var")
        def fln(eng):
            return eng.activation(out=ctx.ps[4][0:1, 0:512],
                                  in_=ctx.ps[6][0:1, 0:512], func=AF.Ln,
                                  bias=ctx.sb["c_mc"][0:1, mccol:mccol + 1],
                                  scale=vscale)
        em.op("scalar", fln, r=("ps6", "c_mc"), w=("ps4",), tag="ln.ln")
        def fexp(eng):
            return eng.activation(out=ctx.ps[3][0:1, 0:512],
                                  in_=ctx.ps[4][0:1, 0:512], func=AF.Exp,
                                  scale=-0.5)
        em.op("scalar", fexp, r=("ps4",), w=("ps3",), tag="ln.exp")
        def fcr(eng):
            return eng.tensor_copy(out=LR[0:1, :], in_=ctx.ps[3][0:1, 0:512])
        em.op("vector", fcr, r=("ps3", "lr0"), w=("lr0",), tag="ln.cr")
        def fbA(eng):
            return eng.matmul(ctx.ps[2][:, 0:512], lhsT=onesrf[:],
                              rhs=LR[0:1, :], start=True, stop=True)
        em.op("tensor", fbA, r=("lr0", "c_onesrf"), w=("ps2",), tag="ln.bA")
        def fcA(eng):
            return eng.tensor_copy(out=SC[:, 0:512], in_=ctx.ps[2][:, 0:512])
        em.op("vector", fcA, r=("ps2",), w=("SC_A",), tag="ln.cA")
        def fs2r(eng):
            return eng.tensor_scalar(out=ctx.sb["SC2"][0:1, 512:1024], in0=ctx.ps[5][0:1, 0:512],
                                     scalar1=-1.0, scalar2=None, op0=ALU.mult)
        em.op("vector", fs2r, r=("ps5",), w=("lr32",), tag="ln.s2a")
        def fs2b(eng):
            return eng.tensor_mul(out=ctx.sb["SC2"][0:1, 512:1024],
                                  in0=ctx.sb["SC2"][0:1, 512:1024],
                                  in1=LR[0:1, :])
        em.op("vector", fs2b, r=("lr32", "lr0"), w=("lr32",), tag="ln.s2b")
        def fbB(eng):
            return eng.matmul(ctx.ps[2][:, 0:512], lhsT=onesrf[:],
                              rhs=ctx.sb["SC2"][0:1, 512:1024], start=True, stop=True)
        em.op("tensor", fbB, r=("lr32", "c_onesrf"), w=("ps2",), tag="ln.bB")
        def fcB(eng):
            return eng.tensor_copy(out=SC[:, 512:1024], in_=ctx.ps[2][:, 0:512])
        em.op("vector", fcB, r=("ps2",), w=("SC_B",), tag="ln.cB")
        if r1_rows:
            def fln1(eng):
                return eng.activation(out=ctx.ps[4][0:1, 0:512],
                                      in_=ctx.ps[6][0:1, 0:512], func=AF.Ln,
                                      bias=ctx.sb["c_mc"][0:1, 4:5])
            em.op("scalar", fln1, r=("ps6", "c_mc"), w=("ps4",), tag="ln.l1")
            def fex1(eng):
                return eng.activation(out=ctx.ps[3][0:1, 0:512],
                                      in_=ctx.ps[4][0:1, 0:512], func=AF.Exp,
                                      scale=-0.5)
            em.op("scalar", fex1, r=("ps4",), w=("ps3",), tag="ln.e1")
            def fp1(eng):
                return eng.tensor_scalar(out=LR[0:1, :], in0=ctx.ps[3][0:1, 0:512],
                                         scalar1=1.0, scalar2=None, op0=ALU.add)
            em.op("vector", fp1, r=("ps3", "lr0"), w=("lr0",), tag="ln.p1")
            def fbA1(eng):
                return eng.matmul(ctx.ps[2][:, 0:512], lhsT=onesrf[:],
                                  rhs=LR[0:1, :], start=True, stop=True)
            em.op("tensor", fbA1, r=("lr0", "c_onesrf"), w=("ps2",), tag="ln.b1")
            def fcA1(eng):
                return eng.tensor_copy(out=ctx.sb["SC2"][:, 0:512],
                                       in_=ctx.ps[2][:, 0:512])
            em.op("vector", fcA1, r=("ps2",), w=("r1A",), tag="ln.c1")
            def fs7a(eng):
                return eng.tensor_scalar(out=ctx.sb["SC2"][0:1, 512:1024],
                                         in0=ctx.ps[5][0:1, 0:512],
                                         scalar1=-1.0, scalar2=None, op0=ALU.mult)
            em.op("vector", fs7a, r=("ps5", "lr32"), w=("lr32",), tag="ln.s7a")
            def fs7b(eng):
                return eng.tensor_tensor(out=ctx.sb["SC2"][0:1, 512:1024],
                                         in0=ctx.sb["SC2"][0:1, 512:1024],
                                         in1=ctx.ps[3][0:1, 0:512], op=ALU.mult)
            em.op("vector", fs7b, r=("lr32", "ps3"), w=("lr32",), tag="ln.s7b")
            def fbB1(eng):
                return eng.matmul(ctx.ps[2][:, 0:512], lhsT=onesrf[:],
                                  rhs=ctx.sb["SC2"][0:1, 512:1024], start=True, stop=True)
            em.op("tensor", fbB1, r=("lr32", "c_onesrf"), w=("ps2",), tag="ln.b2")
            def fcB1(eng):
                return eng.tensor_copy(out=ctx.sb["SC2"][:, 512:1024],
                                       in_=ctx.ps[2][:, 0:512])
            em.op("vector", fcB1, r=("ps2",), w=("r1B",), tag="ln.c2")
        for c in range(NCH):
            ename = "vector" if c % 2 == 0 else "gpsimd"
            slot = 0 if ename == "vector" else 512
            def fa1(eng, c=c, t0=t0, slot=slot):
                return eng.tensor_mul(out=ctx.sb["SC"][:, 1024 + slot:1536 + slot],
                                      in0=R[:, c, t0:t0 + 512],
                                      in1=SC[:, 0:512])
            em.op(ename, fa1, r=("R", "SC_A"), w=(f"apl{slot}",), tag="ln.a1")
            if not ln_scale:
                def fa2(eng, c=c, t0=t0, slot=slot):
                    return eng.tensor_add(out=XN[:, c, t0:t0 + 512],
                                          in0=ctx.sb["SC"][:, 1024 + slot:1536 + slot],
                                          in1=SC[:, 512:1024])
                em.op(ename, fa2, r=(f"apl{slot}", "SC_B"), w=("XN",), tag="ln.a2")
            else:
                def fa2(eng, c=c, t0=t0, slot=slot):
                    return eng.tensor_add(out=ctx.sb["SC"][:, 1024 + slot:1536 + slot],
                                          in0=ctx.sb["SC"][:, 1024 + slot:1536 + slot],
                                          in1=SC[:, 512:1024])
                em.op(ename, fa2, r=(f"apl{slot}", "SC_B"), w=(f"apl{slot}",),
                      tag="ln.a2s")
                def fa3(eng, c=c, t0=t0, slot=slot):
                    return eng.tensor_scalar(out=XN[:, c, t0:t0 + 512],
                                             in0=ctx.sb["SC"][:, 1024 + slot:1536 + slot],
                                             scalar1=ctx.sb["LS"][:, c:c + 1],
                                             scalar2=ctx.sb["LS"][:, 16 + c:17 + c],
                                             op0=ALU.mult, op1=ALU.add)
                em.op(ename, fa3, r=(f"apl{slot}", "LS"), w=("XN",), tag="ln.a3")
        if r1_rows:
            for c in range(NCH):
                ename = "vector" if c % 2 == 0 else "gpsimd"
                def fu1(eng, c=c, t0=t0):
                    return eng.tensor_mul(out=R[:, c, t0:t0 + 512],
                                          in0=R[:, c, t0:t0 + 512],
                                          in1=ctx.sb["SC2"][:, 0:512])
                em.op(ename, fu1, r=("R", "r1A", "XN"), w=("R",), tag="ru.1")
                def fu2(eng, c=c, t0=t0):
                    return eng.tensor_add(out=R[:, c, t0:t0 + 512],
                                          in0=R[:, c, t0:t0 + 512],
                                          in1=ctx.sb["SC2"][:, 512:1024])
                em.op(ename, fu2, r=("R", "r1B"), w=("R",), tag="ru.2")


def add_tokmajor_dram_to_R(ctx, dram_ap, scale, dt_=F32, extra_r=()):
    em = ctx.em
    R = ctx.sb["R"]
    for ts_ in range(8):
        for g0 in range(0, 16, 8):
            def fl(eng, ts_=ts_, g0=g0):
                return eng.dma_start(out=ctx.sb["RT"][:, 0:1024],
                                     in_=dram_ap[ts_ * 128:(ts_ + 1) * 128,
                                                 g0 * 128:(g0 + 8) * 128])
            em.op("gpsimd", fl, r=("RT",) + extra_r, w=("RT",), dma=True,
                  tag="ra.ld")
            for j in range(8):
                def ft(eng, j=j):
                    return eng.transpose(ctx.ps["tr"][:, j * 128:(j + 1) * 128],
                                         ctx.sb["RT"][:, j * 128:(j + 1) * 128],
                                         ctx.sb["c_ident"][:])
                em.op("tensor", ft, r=("RT", "c_ident"), w=("ps_tr",), tag="ra.t")
            for j in range(8):
                c = g0 + j
                fat, t0 = (0, 0) if ts_ < 4 else (1, 512)
                tt0 = (ts_ % 4) * 128
                if scale == 1.0:
                    def fa(eng, c=c, t0=t0, tt0=tt0, j=j):
                        return eng.tensor_tensor(
                            out=R[:, c, t0 + tt0:t0 + tt0 + 128],
                            in0=ctx.ps["tr"][:, j * 128:(j + 1) * 128],
                            in1=R[:, c, t0 + tt0:t0 + tt0 + 128], op=ALU.add)
                    em.op("vector", fa, r=("ps_tr", "R"), w=("R",), tag="ra.a")
                else:
                    def fa0(eng, j=j):
                        return eng.tensor_scalar(
                            out=ctx.sb["ES"][:, 0:128],
                            in0=ctx.ps["tr"][:, j * 128:(j + 1) * 128],
                            scalar1=scale, scalar2=None, op0=ALU.mult)
                    em.op("vector", fa0, r=("ps_tr",), w=("ra_t",), tag="ra.s")
                    def fa(eng, c=c, t0=t0, tt0=tt0):
                        return eng.tensor_tensor(
                            out=R[:, c, t0 + tt0:t0 + tt0 + 128],
                            in0=ctx.sb["ES"][:, 0:128],
                            in1=R[:, c, t0 + tt0:t0 + tt0 + 128], op=ALU.add)
                    em.op("vector", fa, r=("ra_t", "R"), w=("R",), tag="ra.a")




def _emit_T_block(ctx, name, src_block, dst, dst_r0, dst_c0, n_in, n_out,
                  scale=None):
    em = ctx.em
    OBf = ctx.sb["OB"]
    for rc in range(n_in // 128):
        def fl(eng, rc=rc):
            return eng.dma_start(
                out=ctx.sb["WB"][:, 0:n_out].rearrange("p (c f) -> p c f", f=128),
                in_=src_block[:, rc * 128:(rc + 1) * 128]
                    .rearrange("(c p) f -> p c f", p=128))
        em.op("gpsimd", fl, w=("WB",), dma=True, tag="pT.ld")
        for fc0 in range(0, n_out // 128, 8):
            g = min(8, n_out // 128 - fc0)
            for j in range(g):
                def ft(eng, fc=fc0 + j, j=j):
                    return eng.transpose(
                        ctx.ps["tr"][:, j * 128:(j + 1) * 128],
                        ctx.sb["WB"][:, fc * 128:(fc + 1) * 128],
                        ctx.sb["c_ident"][:])
                em.op("tensor", ft, r=("WB", "c_ident"), w=("ps_tr",), tag="pT.t")
            def fcp(eng, fc0=fc0, g=g):
                return eng.tensor_copy(out=OBf[:, fc0 * 128:(fc0 + g) * 128],
                                       in_=ctx.ps["tr"][:, 0:g * 128])
            em.op("vector", fcp, r=("ps_tr",), w=("OB_P",), tag="pT.cp")
        if scale is not None:
            sc_ap, sc_res = scale(dst_r0 // 128 + rc)
            def fsc(eng, sc_ap=sc_ap):
                return eng.tensor_scalar(out=OBf[:, 0:n_out], in0=OBf[:, 0:n_out],
                                         scalar1=sc_ap, scalar2=None, op0=ALU.mult)
            em.op("vector", fsc, r=("OB_P", sc_res), w=("OB_P",), tag="pT.sc")
        def fst(eng, rc=rc):
            return eng.dma_start(
                out=dst[dst_r0 + rc * 128:dst_r0 + (rc + 1) * 128,
                        dst_c0:dst_c0 + n_out],
                in_=OBf[:, 0:n_out])
        em.op("sync", fst, r=("OB_P",), w=(("wc", name),), dma=True, tag="pT.st")


def nib_decode(ctx, b_res, plane, hw):
    em = ctx.em
    w = 512
    QBi = ctx.sb["QB"][:].bitcast(I32)
    OBv = ctx.sb["OB"][:].bitcast(F32)
    SC, SC2 = ctx.sb["SC"], ctx.sb["SC2"]
    o0 = hw * 1024
    out_ap = OBv[:, o0:o0 + 1024].rearrange("p (a b2) -> p a b2", b2=2)[:, :, plane:plane + 1]
    if plane == 1:
        def fsh(eng, hw=hw):
            return eng.tensor_scalar(out=QBi[:, 3072 + hw * w:3072 + (hw + 1) * w],
                                     in0=QBi[:, hw * w:(hw + 1) * w],
                                     scalar1=4, scalar2=None,
                                     op0=ALU.logical_shift_right)
        em.op("vector", fsh, r=(b_res,), w=("dq_B",), tag="dq.sh")
        B, Bres = QBi[:, 3072 + hw * w:3072 + (hw + 1) * w], "dq_B"
    else:
        B, Bres = QBi[:, hw * w:(hw + 1) * w], b_res
    t0 = QBi[:, 1024 + hw * w:1024 + (hw + 1) * w]
    t1 = QBi[:, 2048 + hw * w:2048 + (hw + 1) * w]
    def f0(eng):
        return eng.tensor_scalar(out=t0, in0=B, scalar1=7, scalar2=22,
                                 op0=ALU.bitwise_and, op1=ALU.logical_shift_left)
    em.op("vector", f0, r=(Bres,), w=("dq_t0",), tag="dq.0")
    def f1(eng):
        return eng.tensor_scalar(out=t1, in0=B, scalar1=8, scalar2=28,
                                 op0=ALU.bitwise_and, op1=ALU.logical_shift_left)
    em.op("vector", f1, r=(Bres,), w=("dq_t1",), tag="dq.1")
    def f2(eng):
        return eng.tensor_tensor(out=t0, in0=t0, in1=t1, op=ALU.add)
    em.op("vector", f2, r=("dq_t0", "dq_t1"), w=("dq_t0",), tag="dq.2")
    def f3(eng):
        return eng.tensor_tensor(out=t0, in0=t0, in1=ctx.sb["c_e32"][:],
                                 op=ALU.add)
    em.op("vector", f3, r=("dq_t0", "c_e32"), w=("dq_t0",), tag="dq.3")
    def f4(eng):
        return eng.tensor_scalar(out=t1, in0=B, scalar1=7, scalar2=None,
                                 op0=ALU.bitwise_and)
    em.op("vector", f4, r=(Bres, "dq_t1"), w=("dq_m",), tag="dq.4")
    def f5(eng):
        return eng.copy(out=SC[:, 0:w], in_=t1)
    em.op("scalar", f5, r=("dq_m",), w=("dq_mf",), tag="dq.5")
    def f6(eng):
        return eng.activation(out=SC[:, 1024:1024 + w], in_=SC[:, 0:w],
                              func=AF.Relu, scale=-1.0,
                              bias=ctx.sb["c_mc"][:, 7:8])
    em.op("scalar", f6, r=("dq_mf", "c_mc"), w=("dq_r",), tag="dq.6")
    def f7(eng):
        return eng.tensor_scalar(out=t1, in0=B, scalar1=8, scalar2=None,
                                 op0=ALU.bitwise_and)
    em.op("vector", f7, r=(Bres, "dq_mf"), w=("dq_s",), tag="dq.7")
    def f8(eng):
        return eng.copy(out=SC2[:, 0:w], in_=t1)
    em.op("scalar", f8, r=("dq_s",), w=("dq_sf",), tag="dq.8")
    def f9(eng):
        return eng.tensor_scalar(out=SC2[:, 0:w], in0=SC2[:, 0:w],
                                 scalar1=1.0 / 16, scalar2=-0.25,
                                 op0=ALU.mult, op1=ALU.add)
    em.op("vector", f9, r=("dq_sf",), w=("dq_sf",), tag="dq.9")
    def f10(eng):
        return eng.tensor_mul(out=SC[:, 0:w], in0=SC[:, 1024:1024 + w],
                              in1=SC2[:, 0:w])
    em.op("vector", f10, r=("dq_r", "dq_sf"), w=("dq_fix",), tag="dq.10")
    def f11(eng):
        return eng.tensor_tensor(out=out_ap, in0=t0.bitcast(F32).unsqueeze(2),
                                 in1=SC[:, 0:w].unsqueeze(2), op=ALU.add)
    em.op("vector", f11, r=("dq_t0", "dq_fix"), w=("OB_P",), tag="dq.11")


def prep_matrix(ctx, name):
    em = ctx.em
    spec = WSPEC[name]
    kind = spec["kind"]
    rows, cols = spec["rows"], spec["cols"]
    dst = ctx.wc[name]
    if kind in ("cast", "cast_vw"):
        src = ctx.inp[spec["src"]]
        def f(eng):
            return eng.dma_start(out=dst[:, :], in_=src)
        em.op("gpsimd", f, w=(("wc", name),), dma=True, tag="cast." + name)
        if kind == "cast_vw":
            for hh in range(2):
                def fl(eng, hh=hh):
                    return eng.dma_start(out=ctx.sb["WB"][:, 0:D],
                                         in_=dst[hh * 128:(hh + 1) * 128, :])
                em.op("sync", fl, r=(("wc", name),), w=("WB",), dma=True,
                      tag="vw.ld")
                def fs(eng, hh=hh):
                    return eng.tensor_scalar(out=ctx.sb["WB"][:, 0:D],
                                             in0=ctx.sb["WB"][:, 0:D],
                                             scalar1=ctx.sb["SG"][:, 4 + hh:5 + hh],
                                             scalar2=None, op0=ALU.mult)
                em.op("vector", fs, r=("WB", "SG"), w=("WB",), tag="vw.sc")
                def fst(eng, hh=hh):
                    return eng.dma_start(out=dst[hh * 128:(hh + 1) * 128, :],
                                         in_=ctx.sb["WB"][:, 0:D])
                em.op("sync", fst, r=("WB",), w=(("wc", name),), dma=True,
                      tag="vw.st")
        return
    if kind == "T":
        scale = None
        if spec.get("scale") == "rms_w":
            scale = lambda rc: (ctx.sb["SG"][:, 16 + rc:17 + rc], "SG")
        _emit_T_block(ctx, name, ctx.inp[spec["src"]], dst, 0, 0, rows, cols,
                      scale)
        return
    if kind == "T3":
        src = ctx.inp[spec["src"]].rearrange("e h d -> (e h) d")
        _emit_T_block(ctx, name, src, dst, 0, 0, rows, cols)
        return
    if kind == "T3b":
        src = ctx.inp[spec["src"]].rearrange("e o h -> (e o) h")
        for e in range(8):
            _emit_T_block(ctx, name, src[e * 2048:(e + 1) * 2048, :], dst,
                          e * 1024, 0, 1024, 2048)
        return
    if kind == "l0qk":
        side = spec["side"]
        std_src = ctx.inp["hs_l0f_waq" if side == "q" else "hs_l0f_wak"]
        ta = ctx.inp["hs_l0f_waql" if side == "q" else "hs_l0f_wakl"]
        tb = ctx.inp["hs_l0f_wakl" if side == "q" else "hs_l0f_waql"]
        for rc in range(16):
            for hh in range(2):
                def fs(eng, rc=rc, hh=hh):
                    return eng.dma_start(
                        out=dst[rc * 128:(rc + 1) * 128,
                                hh * 128:hh * 128 + 120],
                        in_=std_src[rc * 128:(rc + 1) * 128,
                                    120 * hh:120 * hh + 120])
                em.op("gpsimd", fs, w=(("wc", name),), dma=True, tag="l0.std")
            def fta(eng, rc=rc):
                return eng.dma_start(out=ctx.sb["SB16"][:, 0:16],
                                     in_=ta[rc * 128:(rc + 1) * 128, :])
            em.op("gpsimd", fta, w=("blA",), dma=True, tag="l0.lda")
            def ftb(eng, rc=rc):
                return eng.dma_start(out=ctx.sb["SB16"][:, 16:32],
                                     in_=tb[rc * 128:(rc + 1) * 128, :])
            em.op("gpsimd", ftb, w=("blB",), dma=True, tag="l0.ldb")
            def fb1(eng):
                return eng.tensor_sub(out=ctx.sb["SB16"][:, 32:48],
                                      in0=ctx.sb["SB16"][:, 16:32],
                                      in1=ctx.sb["SB16"][:, 0:16])
            em.op("vector", fb1, r=("blA", "blB"), w=("blC",), tag="l0.sub")
            for hh in range(2):
                def fb2(eng, hh=hh):
                    return eng.tensor_scalar(
                        out=ctx.sb["SB16"][:, 32 + 8 * hh:40 + 8 * hh],
                        in0=ctx.sb["SB16"][:, 32 + 8 * hh:40 + 8 * hh],
                        scalar1=ctx.sb["SG"][:, 8 + hh:9 + hh],
                        scalar2=None, op0=ALU.mult)
                em.op("vector", fb2, r=("blC", "SG"), w=("blC",), tag="l0.m")
            def fb3(eng):
                return eng.tensor_add(out=ctx.sb["SB16"][:, 32:48],
                                      in0=ctx.sb["SB16"][:, 32:48],
                                      in1=ctx.sb["SB16"][:, 0:16])
            em.op("vector", fb3, r=("blC", "blA"), w=("blC",), tag="l0.add")
            for hh in range(2):
                def fst(eng, rc=rc, hh=hh):
                    return eng.dma_start(
                        out=dst[rc * 128:(rc + 1) * 128,
                                hh * 128 + 120:hh * 128 + 128],
                        in_=ctx.sb["SB16"][:, 32 + 8 * hh:40 + 8 * hh])
                em.op("sync", fst, r=("blC",), w=(("wc", name),), dma=True,
                      tag="l0.st")
        return
    if kind in ("deq_gu", "deq_d"):
        pk = ctx.inp[spec["pk"]]
        sc = ctx.inp[spec["sc"]]
        gin = ctx.dram["gin_" + name]
        QBi = ctx.sb["QB"][:].bitcast(I32)
        OBv = ctx.sb["OB"][:].bitcast(F32)
        for c in range(16):
            def fl(eng, c=c):
                return eng.dma_start(out=QBi[:, 0:1024],
                                     in_=pk[c * 128:(c + 1) * 128, :])
            em.op("sync", fl, r=("dq_t0", "dq_m", "dq_s", "dq_B"), w=("dq_pk",),
                  dma=True, tag="dq.ldpk")
            def fls(eng, c=c):
                return eng.dma_start(out=ctx.sb["DN"][:, 0:64],
                                     in_=sc[c * 128:(c + 1) * 128, :])
            em.op("sync", fls, r=("dq_sc",), w=("dq_sc",), dma=True,
                  tag="dq.ldsc")
            for hw in range(2):
                nib_decode(ctx, "dq_pk", 0, hw)
                nib_decode(ctx, "dq_pk", 1, hw)
            for b0 in range(0, 64, 8):
                def fsc(eng, b0=b0):
                    for b in range(b0, b0 + 8):
                        inst = eng.tensor_scalar(
                            out=OBv[:, 32 * b:32 * (b + 1)],
                            in0=OBv[:, 32 * b:32 * (b + 1)],
                            scalar1=ctx.sb["DN"][:, b:b + 1], scalar2=None,
                            op0=ALU.mult)
                    return inst
                em.op("vector", fsc, r=("OB_P", "dq_sc"), w=("OB_P",),
                      tag="dq.sc")
            def fcst(eng):
                return eng.tensor_copy(out=ctx.sb["WB"][:, 0:2048],
                                       in_=OBv[:, 0:2048])
            em.op("vector", fcst, r=("OB_P",), w=("WB",), tag="dq.cst")
            for j0 in range(0, 16, 8):
                for j in range(8):
                    def ft(eng, j0=j0, j=j):
                        return eng.transpose(
                            ctx.ps["tr"][:, j * 128:(j + 1) * 128],
                            ctx.sb["WB"][:, (j0 + j) * 128:(j0 + j + 1) * 128],
                            ctx.sb["c_ident"][:])
                    em.op("tensor", ft, r=("WB", "c_ident"), w=("ps_tr",),
                          tag="dq.t")
                def fcp(eng, j0=j0):
                    return eng.tensor_copy(
                        out=ctx.sb["WB"][:, 2048 + j0 * 128:2048 + (j0 + 8) * 128],
                        in_=ctx.ps["tr"][:, 0:1024])
                em.op("vector", fcp, r=("ps_tr",), w=("WB2",), tag="dq.cp")
            def fst(eng, c=c):
                return eng.dma_start(
                    out=gin[:, c * 128:(c + 1) * 128]
                        .rearrange("(a p) f -> p a f", p=128),
                    in_=ctx.sb["WB"][:, 2048:4096]
                        .rearrange("p (a f) -> p a f", f=128))
            em.op("sync", fst, r=("WB2",), w=(("gin", name),), dma=True,
                  tag="dq.st")
        def fcc(eng):
            return eng.collective_compute(
                "AllGather", ALU.bypass, replica_groups=[list(range(NCORE))],
                ins=[gin.opt()], outs=[ctx.wc[name].opt()])
        em.op("gpsimd", fcc, r=(("gin", name),), w=(("wc", name),), cc=True,
              tag="dq.ag")
        return
    raise NotImplementedError(kind)


def prep_tables(ctx, subs):
    em = ctx.em
    ST, SG = ctx.sb["ST"], ctx.sb["SG"]
    bases = {BASE.get(s, s) for s in subs}
    if "l0f" in bases:
        def f0(eng):
            return eng.dma_start(out=ctx.sb["SC"][:, 0:2],
                                 in_=ctx.inp["hs_l0f_wstd"].unsqueeze(0)
                                 .broadcast_to((128, 2)))
        em.op("gpsimd", f0, r=("SC_A",), w=("sgt",), dma=True, tag="sg.ld")
        def f0b(eng):
            return eng.dma_start(out=ctx.sb["SC"][:, 2:4],
                                 in_=ctx.inp["hs_l0f_wrec"].unsqueeze(0)
                                 .broadcast_to((128, 2)))
        em.op("gpsimd", f0b, r=("SC_A",), w=("sgt",), dma=True, tag="sg.ld2")
        def f1(eng):
            return eng.tensor_scalar(out=ctx.sb["SC"][:, 4:8],
                                     in0=ctx.sb["SC"][:, 0:4],
                                     scalar1=1e-8, scalar2=None, op0=ALU.max)
        em.op("vector", f1, r=("sgt",), w=("sgt2",), tag="sg.clip")
        def f2(eng):
            return eng.activation(out=ctx.sb["SC"][:, 8:12],
                                  in_=ctx.sb["SC"][:, 4:8], func=AF.Ln)
        em.op("scalar", f2, r=("sgt2",), w=("sgt3",), tag="sg.ln")
        def f3(eng):
            return eng.activation(out=ctx.sb["SC"][:, 12:16],
                                  in_=ctx.sb["SC"][:, 8:12], func=AF.Exp,
                                  scale=0.5)
        em.op("scalar", f3, r=("sgt3",), w=("sgt4",), tag="sg.exp")
        def f4(eng):
            return eng.tensor_scalar(out=SG[:, 8:10], in0=ctx.sb["SC"][:, 2:4],
                                     scalar1=0.1, scalar2=None, op0=ALU.is_gt)
        em.op("vector", f4, r=("sgt",), w=("SG",), tag="sg.isgt")
        def f5(eng):
            return eng.tensor_scalar(out=SG[:, 0:2], in0=ctx.sb["SC"][:, 12:14],
                                     scalar1=ctx.sb["c_mask120"][:],
                                     scalar2=None, op0=ALU.mult)
        em.op("vector", f5, r=("sgt4", "c_mask120"), w=("SG",), tag="sg.a")
        def f6(eng):
            return eng.tensor_scalar(out=ctx.sb["SC"][:, 16:18],
                                     in0=ctx.sb["SC"][:, 14:16],
                                     scalar1=ctx.sb["c_mask120"][:],
                                     scalar2=None, op0=ALU.mult)
        em.op("vector", f6, r=("sgt4", "c_mask120"), w=("sgt5",), tag="sg.b")
        def f7(eng):
            return eng.tensor_sub(out=ctx.sb["SC"][:, 18:20],
                                  in0=ctx.sb["SC"][:, 14:16],
                                  in1=ctx.sb["SC"][:, 16:18])
        em.op("vector", f7, r=("sgt4", "sgt5"), w=("sgt6",), tag="sg.c")
        def f8(eng):
            return eng.tensor_add(out=SG[:, 0:2], in0=SG[:, 0:2],
                                  in1=ctx.sb["SC"][:, 18:20])
        em.op("vector", f8, r=("SG", "sgt6"), w=("SG",), tag="sg.d")
    if "l2a" in bases:
        for hh in range(2):
            for which, col, srcn in (("k", 2, "hs_l2a_wkh"), ("v", 4, "hs_l2a_wvh")):
                def fl(eng, hh=hh, srcn=srcn):
                    return eng.dma_start(out=ctx.sb["SC"][:, 0:128],
                                         in_=ctx.inp[srcn][hh])
                em.op("sync", fl, r=("SC_A",), w=("SC_A",), dma=True, tag="kv.ld")
                def fr(eng, hh=hh, col=col):
                    return eng.reduce_sum(out=SG[:, col + hh:col + hh + 1],
                                          in_=ctx.sb["SC"][:, 0:128], axis=AXX)
                em.op("vector", fr, r=("SC_A",), w=("SG",), tag="kv.red")
    if "l3a" in bases:
        def fl(eng):
            return eng.dma_start(out=ctx.sb["SG"][:, 16:32],
                                 in_=ctx.inp["L3_attn_rms_w"]
                                 .rearrange("(c p) -> p c", p=128),
                                 allow_slow_non_contiguous=True)
        em.op("sync", fl, w=("SG",), dma=True, tag="rw.ld")
    if "l0a" in bases:
        def fl(eng):
            return eng.dma_start(out=ctx.sb["LS"][:, 0:16],
                                 in_=ctx.inp["L0_attn_ln_scale"]
                                 .rearrange("(c p) -> p c", p=128),
                                 allow_slow_non_contiguous=True)
        em.op("sync", fl, w=("LS",), dma=True, tag="ls.ld")
        def fb(eng):
            return eng.dma_start(out=ctx.sb["LS"][:, 16:32],
                                 in_=ctx.inp["L0_attn_ln_bias"]
                                 .rearrange("(c p) -> p c", p=128),
                                 allow_slow_non_contiguous=True)
        em.op("sync", fb, w=("LS",), dma=True, tag="ls.ld2")


def rope_apply(ctx, ap, res, kind, tagp=""):
    em = ctx.em
    RP, PB = ctx.sb["RP"], ctx.sb["PB"]
    groups = [(0,)] if kind in ("l2", "l3k") else [(0,), (64,)]
    for (r0,) in groups:
        b1, b2 = r0, r0 + 32
        x1 = ap[b1:b1 + 32, :]
        x2 = ap[b2:b2 + 32, :]
        c1 = RP[b1:b1 + 32, 0:512]
        c2 = RP[b2:b2 + 32, 0:512]
        sx1 = RP[b1:b1 + 32, 512:1024]
        sx2 = RP[b2:b2 + 32, 512:1024]
        t_ = PB[b1:b1 + 32, 0:512]
        u_ = PB[b1:b1 + 32, 512:1024]
        w_ = PB[b2:b2 + 32, 0:512]
        def f0(eng, x1=x1, t_=t_):
            return eng.tensor_copy(out=t_, in_=x1)
        em.op("vector", f0, r=(res,), w=("rp_t",), tag=tagp + "rp.cp")
        def f1(eng, x2=x2, sx2=sx2, u_=u_):
            return eng.tensor_mul(out=u_, in0=x2, in1=sx2)
        em.op("vector", f1, r=(res, "RP", "rp_t"), w=("rp_u",), tag=tagp + "rp.u")
        def f2(eng, x1=x1, c1=c1):
            return eng.tensor_mul(out=x1, in0=x1, in1=c1)
        em.op("vector", f2, r=(res, "RP", "rp_t"), w=(res,), tag=tagp + "rp.m1")
        def f3(eng, x1=x1, u_=u_):
            return eng.tensor_sub(out=x1, in0=x1, in1=u_)
        em.op("vector", f3, r=(res, "rp_u"), w=(res,), tag=tagp + "rp.s1")
        def f4(eng, x2=x2, c2=c2):
            return eng.tensor_mul(out=x2, in0=x2, in1=c2)
        em.op("gpsimd", f4, r=(res, "RP", "rp_u"), w=(res,), tag=tagp + "rp.m2")
        def f5(eng, t_=t_, sx1=sx1, w_=w_):
            return eng.tensor_mul(out=w_, in0=t_, in1=sx1)
        em.op("gpsimd", f5, r=("rp_t", "RP"), w=("rp_w",), tag=tagp + "rp.m3")
        def f6(eng, x2=x2, w_=w_):
            return eng.tensor_add(out=x2, in0=x2, in1=w_)
        em.op("gpsimd", f6, r=(res, "rp_w"), w=(res,), tag=tagp + "rp.a2")


def attn_site(ctx, site):
    em = ctx.em
    base = BASE.get(site, site)
    QB, OB, WB, ES, OF = (ctx.sb["QB"], ctx.sb["OB"], ctx.sb["WB"],
                          ctx.sb["ES"], ctx.sb["OF"])
    OBb = OB[:]
    cfgs = {
        "l0f": dict(hl=2, dh=128, dv=128, M=M_L0F, mcc=0, scale=128 ** -0.5,
                    wq="l0f_wq", wk="l0f_wk", wv="l0f_wv", wo="l0f_wosl",
                    ln="ln", rope=None, out_scale=0.5, add_xn=False),
        "l0a": dict(hl=2, dh=128, dv=128, M=M_L0A, mcc=1, scale=128 ** -0.5,
                    wq="l0a_wq", wk="l0a_wk", wv="l0a_wv", wo="l0a_wosl",
                    ln="ln_ln", rope=None, band=256, use_global=True,
                    out_scale=1.0, add_xn=True),
        "l1a": dict(hl=2, dh=128, dv=128, M=None, scale=128 ** -0.5,
                    wq="l1a_wq", wk="l1a_wk", wv="l1a_wv", wo="l1a_wosl",
                    ln="ln", rope=None, fmap=True, denom_eps=1e-5,
                    out_scale=1.0, add_xn=False),
        "l2a": dict(hl=2, dh=128, dv=128, M=M_L2A, mcc=2, scale=128 ** -0.5,
                    wq="l2a_wq", wk=None, wv=None, wo="l2a_wosl",
                    ln="ln", rope="l2", kv_shared=True, out_scale=1.0,
                    add_xn=False),
        "l3a": dict(hl=4, dh=64, dv=64, M=M_L3A, mcc=3, scale=64 ** -0.5,
                    wq="l3a_wq", wk="l3a_wkv", wv=None, wo="l3a_wosl",
                    ln="ln_rms", rope="l3", l3=True, out_scale=1.0,
                    add_xn=True),
    }
    cfg = cfgs[base]
    hl, dh, dv = cfg["hl"], cfg["dh"], cfg["dv"]

    def dbg_sb(name, ap, res):
        if name in TAPS and ("dbg_" + name) in ctx.dram:
            def f(eng):
                return eng.dma_start(out=ctx.dram["dbg_" + name], in_=ap)
            em.op("gpsimd", f, r=(res,), w=(("dbg", name),), dma=True,
                  tag="dbg." + name)

    def dbg_dram(name, src_ap, res):
        if name in TAPS and ("dbg_" + name) in ctx.dram:
            def f(eng):
                return eng.dma_start(out=ctx.dram["dbg_" + name], in_=src_ap)
            em.op("gpsimd", f, r=(res,), w=(("dbg", name),), dma=True,
                  tag="dbg." + name)

    ln_to_xn(ctx, kind=cfg["ln"], ln_scale=(base == "l0a"),
             r1_rows=cfg["add_xn"])
    dbg_sb("xn", ctx.sb["XN"][:], "XN")
    if base == "l1a":
        build_gb_tiles(ctx)
    xin, xg = ctx.dram["xg_in"], ctx.dram["xg"]
    def fxo(eng):
        return eng.dma_start(out=xin[:, :].rearrange("(c p) t -> p c t", p=128),
                             in_=ctx.sb["XN"][:])
    em.op("gpsimd", fxo, r=("XN",), w=("xg_in",), dma=True, tag="at.xo")
    def fcc(eng):
        return eng.collective_compute(
            "AllGather", ALU.bypass, replica_groups=[list(range(NCORE))],
            ins=[xin.opt()], outs=[xg.opt()])
    em.op("gpsimd", fcc, r=("xg_in",), w=("xg",), cc=True, tag="at.ag")
    em.claim("OB_P", "OB_A", "OB_W", "OB_Q", "OB_K", tag="at.clob")
    def ldw(eng, name, c0, width):
        return eng.dma_start(
            out=OBb[:, c0:c0 + 16 * width].rearrange("p (c f) -> p c f", f=width),
            in_=ctx.wc[name][:, :].rearrange("(c p) f -> p c f", p=128))
    def fwo(eng):
        return eng.dma_start(
            out=OBb[:, 4096:8192].rearrange("p (c f) -> p c f", f=2048),
            in_=ctx.wc[cfg["wo"]][:, :].rearrange("(c p) f -> p c f", p=128))
    em.op("sync", fwo, r=(("wc", cfg["wo"]),), w=("OB_W",), dma=True, tag="at.lwo")
    def fwq(eng):
        return eng.dma_start(
            out=OBb[:, 8192:12288].rearrange("p (c f) -> p c f", f=256),
            in_=ctx.wc[cfg["wq"]][:, :].rearrange("(c p) f -> p c f", p=128))
    em.op("sync", fwq, r=(("wc", cfg["wq"]),), w=("OB_Q",), dma=True, tag="at.lwq")
    if base == "l3a":
        def fwk(eng):
            return eng.dma_start(
                out=OBb[:, 12288:14336].rearrange("p (c f) -> p c f", f=128),
                in_=ctx.wc["l3a_wkv"][:, :].rearrange("(c p) f -> p c f", p=128))
        em.op("sync", fwk, r=(("wc", "l3a_wkv"),), w=("OB_K",), dma=True,
              tag="at.lwk")
    elif base == "l2a":
        def fwk(eng):
            return eng.dma_start(
                out=OBb[:, 12288:16384].rearrange("p (c f) -> p c f", f=256),
                in_=ctx.wc["l2a_wkv"][:, :].rearrange("(c p) f -> p c f", p=128))
        em.op("sync", fwk, r=(("wc", "l2a_wkv"),), w=("OB_K",), dma=True,
              tag="at.lwk")
    else:
        def fwk(eng):
            return eng.dma_start(
                out=OBb[:, 12288:16384].rearrange("p (c f) -> p c f", f=256),
                in_=ctx.wc[cfg["wk"]][:, :].rearrange("(c p) f -> p c f", p=128))
        em.op("sync", fwk, r=(("wc", cfg["wk"]),), w=("OB_K",), dma=True,
              tag="at.lwk")
        def fwv(eng):
            return eng.dma_start(
                out=WB[:, 4096:8192].rearrange("p (c f) -> p c f", f=256),
                in_=ctx.wc[cfg["wv"]][:, :].rearrange("(c p) f -> p c f", p=128))
        em.op("sync", fwv, r=(("wc", cfg["wv"]),), w=("WB_V",), dma=True,
              tag="at.lwv")
    if cfg.get("use_global"):
        prep_globals_l0a(ctx)
    em.claim("XN", tag="at.clxn")
    for f in range(NFAT):
        xs = f % 2
        xslot = ctx.sb["XN"][:].rearrange("p a b -> p (a b)")[:, xs * 8192:(xs + 1) * 8192]
        xslot3 = xslot.rearrange("p (c f2) -> p c f2", f2=512)
        def fxl(eng, f=f, xslot3=xslot3):
            rf, half = f // 2, f % 2
            return eng.dma_start(
                out=xslot3,
                in_=xg[2048 * rf:2048 * (rf + 1), half * 512:(half + 1) * 512]
                    .rearrange("(c p) t -> p c t", p=128))
        em.op("sync", fxl, r=("xg", "XN", f"xsl{xs}"), w=(f"xsl{xs}",), dma=True,
              tag="at.xl")
        if cfg.get("rope"):
            rr, rows = ("c_rope2", 64) if cfg["rope"] == "l2" else ("c_rope3", 64)
            def frp(eng, f=f, rr=rr, rows=rows):
                return eng.dma_start(out=ctx.sb["RP"][0:rows, 0:512],
                                     in_=ctx.inp[rr + "c"][:, f * 512:(f + 1) * 512])
            em.op("gpsimd", frp, r=("RP",), w=("RP",), dma=True, tag="at.rp")
            def frpb(eng, f=f, rr=rr, rows=rows):
                return eng.dma_start(out=ctx.sb["RP"][0:rows, 512:1024],
                                     in_=ctx.inp[rr + "s"][:, f * 512:(f + 1) * 512])
            em.op("gpsimd", frpb, r=("RP",), w=("RP",), dma=True, tag="at.rpb")
            if cfg["rope"] == "l3":
                def frp2(eng, f=f, rr=rr):
                    return eng.dma_start(out=ctx.sb["RP"][64:128, 0:512],
                                         in_=ctx.inp[rr + "c"][:, f * 512:(f + 1) * 512])
                em.op("gpsimd", frp2, r=("RP",), w=("RP",), dma=True, tag="at.rp2")
                def frpb2(eng, f=f, rr=rr):
                    return eng.dma_start(out=ctx.sb["RP"][64:128, 512:1024],
                                         in_=ctx.inp[rr + "s"][:, f * 512:(f + 1) * 512])
                em.op("gpsimd", frpb2, r=("RP",), w=("RP",), dma=True, tag="at.rpb2")
        make_qkv_fat(ctx, cfg, base, f, xslot3, xs)
    dbg_dram("xg", ctx.dram["xg"], "xg")
    if "wcq" in TAPS and base == "l0f":
        dbg_dram("wcq", ctx.wc["l0f_wq"], ("wc", "l0f_wq"))
    dbg_sb("sg", ctx.sb["SG"][:], "SG")
    dbg_dram("kloc", ctx.dram["kloc"], "kloc")
    dbg_dram("vloc", ctx.dram["vloc"], "vloc")
    dbg_sb("qb", ctx.sb["QB"][:], "QB")
    for s in range(4):
        def fo(eng, s=s, dv=dv):
            return eng.memset(
                ctx.sb["KVA"][:, 1024 + s * 264:1024 + s * 264 + 264]
                .rearrange("p (g2 f2) -> p g2 f2", f2=132)[:, :, dv:132],
                1.0)
        em.op("vector", fo, r=(f"vslot{2 * s}", f"vslot{2 * s + 1}"),
              w=(f"vslot{2 * s}", f"vslot{2 * s + 1}", "vones"),
              tag="vones")
    for f in range(NFAT):
        flash_fat(ctx, cfg, base, f)
        partial_out_fat(ctx, cfg, f)
    dbg_dram("rsin", ctx.dram["rs_in"], "rs_in")
    def frs(eng):
        return eng.collective_compute(
            "ReduceScatter", ALU.add, replica_groups=[list(range(NCORE))],
            ins=[ctx.dram["rs_in"].opt()], outs=[ctx.dram["rs_out"].opt()])
    em.op("gpsimd", frs, r=("rs_in",), w=("rs_out",), cc=True, tag="at.rs")
    dbg_dram("rsout", ctx.dram["rs_out"], "rs_out")
    add_tokmajor_dram_to_R(ctx, ctx.dram["rs_out"], cfg["out_scale"],
                           extra_r=("rs_out",))


def _evac_fm(ctx, psres, ps_ap, dst_ap, dst_res, scale_ap=None, scale_res=None,
             eng_name="vector", tag="ev"):
    em = ctx.em
    if scale_ap is not None:
        def f(eng):
            return eng.tensor_scalar(out=dst_ap, in0=ps_ap, scalar1=scale_ap,
                                     scalar2=None, op0=ALU.mult)
        em.op(eng_name, f, r=(psres, scale_res), w=(dst_res,), tag=tag)
    else:
        def f(eng):
            return eng.tensor_copy(out=dst_ap, in_=ps_ap)
        em.op(eng_name, f, r=(psres,), w=(dst_res,), tag=tag)


def make_qkv_fat(ctx, cfg, base, f, xslot3, xs):
    em = ctx.em
    QB, OB, WB, ES = ctx.sb["QB"], ctx.sb["OB"], ctx.sb["WB"], ctx.sb["ES"]
    OBb = OB[:]
    OQ = OBb[:, 8192:12288].rearrange("p (c f2) -> p c f2", f2=256)
    OK = OBb[:, 12288:16384].rearrange("p (c f2) -> p c f2", f2=256)
    OK3 = OBb[:, 12288:14336].rearrange("p (c f2) -> p c f2", f2=128)
    WV = WB[:, 4096:8192].rearrange("p (c f2) -> p c f2", f2=256)
    xres = f"xsl{xs}"
    kloc, vloc = ctx.dram["kloc"], ctx.dram["vloc"]

    def mm_acc(bank, lhs_fn, rhs_fn, n, rres, wres, nch=16, tag="mm", pdim=128):
        for c in range(nch):
            def fm(eng, c=c, bank=bank):
                return eng.matmul(ctx.ps[bank][0:pdim, 0:n], lhsT=lhs_fn(c),
                                  rhs=rhs_fn(c), start=(c == 0),
                                  stop=(c == nch - 1))
            em.op("tensor", fm, r=rres, w=(wres,), tag=tag)

    if base == "l1a":
        make_qkv_fat_l1a(ctx, cfg, f, xslot3, xs)
        return
    if base == "l3a":
        for jj in range(2):
            bank = jj % 2
            mm_acc(bank, lambda c, jj=jj: OQ[:, c, :].rearrange("p (a b) -> p a b", b=128)[:, jj, :],
                   lambda c: xslot3[:, c, :], 512, ("OB_Q", xres), f"ps{bank}",
                   tag="q3.mm")
            dst = QB[:, jj * 8192 + f * 512:jj * 8192 + (f + 1) * 512]
            _evac_fm(ctx, f"ps{bank}", ctx.ps[bank][:, 0:512], dst, "QB",
                     tag="q3.ev")
            rope_apply(ctx, dst, "QB", "l3", tagp="q3.")
        mm_acc(0, lambda c: OK3[:, c, 0:64], lambda c: xslot3[:, c, :], 512,
               ("OB_K", xres), "ps0", tag="k3.mm", pdim=64)
        _evac_fm(ctx, "ps0", ctx.ps[0][0:64, 0:512], ES[0:64, 0:512], "esK",
                 tag="k3.ev")
        rope_apply(ctx, ES[0:64, 0:512], "esK", "l3k", tagp="k3.")
        def fks(eng, f=f):
            return eng.dma_start(out=kloc[0:64, f * 512:(f + 1) * 512],
                                 in_=ES[0:64, 0:512])
        em.op("sync", fks, r=("esK",), w=("kloc",), dma=True, tag="k3.st")
        for ts_ in range(4):
            bank = ts_ % 2
            mm_acc(bank, lambda c, ts_=ts_: xslot3[:, c, ts_ * 128:(ts_ + 1) * 128],
                   lambda c: OK3[:, c, 64:128], 64, (xres, "OB_K"), f"ps{bank}",
                   tag="v3.mm")
            _evac_fm(ctx, f"ps{bank}", ctx.ps[bank][:, 0:64],
                     ES[:, 512 + ts_ * 64:512 + (ts_ + 1) * 64], "esV", tag="v3.ev")
        def fvs(eng, f=f):
            return eng.dma_start(
                out=vloc[f * 512:(f + 1) * 512, 0:64]
                    .rearrange("(a p) b -> p a b", p=128),
                in_=ES[:, 512:768].rearrange("p (a b) -> p a b", b=64))
        em.op("sync", fvs, r=("esV",), w=("vloc",), dma=True, tag="v3.st")
        return
    if base == "l2a":
        mm_acc(0, lambda c: OK[:, c, 0:128], lambda c: xslot3[:, c, :], 512,
               ("OB_K", xres), "ps0", tag="kc.mm")
        _evac_fm(ctx, "ps0", ctx.ps[0][:, 0:512], ES[:, 0:512], "esKC",
                 tag="kc.ev")
        for hh in range(2):
            def fsc(eng, hh=hh):
                return eng.tensor_scalar(out=PBK(ctx)[:, 0:512], in0=ES[:, 0:512],
                                         scalar1=ctx.sb["SG"][:, 2 + hh:3 + hh],
                                         scalar2=None, op0=ALU.mult)
            em.op("vector", fsc, r=("esKC", "SG", "rp_t", "rp_u"), w=("esKH",),
                  tag="kh.sc")
            rope_apply(ctx, PBK(ctx)[:, 0:512], "esKH", "l2", tagp="kh.")
            def fks(eng, f=f, hh=hh):
                return eng.dma_start(
                    out=kloc[hh * 128:(hh + 1) * 128, f * 512:(f + 1) * 512],
                    in_=PBK(ctx)[:, 0:512])
            em.op("sync", fks, r=("esKH",), w=("kloc",), dma=True, tag="kh.st")
        for ts_ in range(4):
            bank = ts_ % 2
            mm_acc(bank, lambda c, ts_=ts_: xslot3[:, c, ts_ * 128:(ts_ + 1) * 128],
                   lambda c: OK[:, c, 128:256], 128, (xres, "OB_K"), f"ps{bank}",
                   tag="vc.mm")
            _evac_fm(ctx, f"ps{bank}", ctx.ps[bank][:, 0:128],
                     ES[:, 512 + ts_ * 128:512 + (ts_ + 1) * 128], "esV",
                     tag="vc.ev")
        def fvs(eng, f=f):
            return eng.dma_start(
                out=vloc[f * 512:(f + 1) * 512, 0:128]
                    .rearrange("(a p) b -> p a b", p=128),
                in_=ES[:, 512:1024].rearrange("p (a b) -> p a b", b=128))
        em.op("sync", fvs, r=("esV",), w=("vloc",), dma=True, tag="vc.st")
    else:
        for hh in range(2):
            bank = hh % 2
            mm_acc(bank, lambda c, hh=hh: OK[:, c, hh * 128:(hh + 1) * 128],
                   lambda c: xslot3[:, c, :], 512, ("OB_K", xres), f"ps{bank}",
                   tag="k.mm")
            sc = (ctx.sb["SG"][:, hh:hh + 1], "SG") if base == "l0f" else (None, None)
            _evac_fm(ctx, f"ps{bank}", ctx.ps[bank][:, 0:512],
                     ES[:, hh * 512:(hh + 1) * 512], "esK", sc[0], sc[1],
                     tag="k.ev")
        def fks(eng, f=f):
            return eng.dma_start(
                out=kloc[0:256, f * 512:(f + 1) * 512]
                    .rearrange("(a p) b -> p a b", p=128),
                in_=ES[:, 0:1024].rearrange("p (a b) -> p a b", b=512))
        em.op("sync", fks, r=("esK",), w=("kloc",), dma=True, tag="k.st")
        for ts_ in range(4):
            bank = ts_ % 2
            mm_acc(bank, lambda c, ts_=ts_: xslot3[:, c, ts_ * 128:(ts_ + 1) * 128],
                   lambda c: WV[:, c, :], 256, (xres, "WB_V"), f"ps{bank}",
                   tag="v.mm")
            _evac_fm(ctx, f"ps{bank}", ctx.ps[bank][:, 0:256],
                     ctx.sb["RT"][:, ts_ * 256:(ts_ + 1) * 256], "esV", tag="v.ev")
        def fvs(eng, f=f):
            return eng.dma_start(
                out=vloc[f * 512:(f + 1) * 512, 0:256]
                    .rearrange("(a p) b -> p a b", p=128),
                in_=ctx.sb["RT"][:, 0:1024].rearrange("p (a b) -> p a b", b=256))
        em.op("sync", fvs, r=("esV",), w=("vloc",), dma=True, tag="v.st")
    for hh in range(2):
        bank = hh % 2
        mm_acc(bank, lambda c, hh=hh: OQ[:, c, hh * 128:(hh + 1) * 128],
               lambda c: xslot3[:, c, :], 512, ("OB_Q", xres), f"ps{bank}",
               tag="q.mm")
        dst = QB[:, hh * 8192 + f * 512:hh * 8192 + (f + 1) * 512]
        sc = (ctx.sb["SG"][:, hh:hh + 1], "SG") if base == "l0f" else (None, None)
        _evac_fm(ctx, f"ps{bank}", ctx.ps[bank][:, 0:512], dst, "QB",
                 sc[0], sc[1], tag="q.ev")
        if base == "l2a":
            rope_apply(ctx, dst, "QB", "l2", tagp="q2.")


def PBK(ctx):
    return ctx.sb["KVA"][:, 0:512]


def make_qkv_fat_l1a(ctx, cfg, f, xslot3, xs):
    em = ctx.em
    QB, OB, WB, ES, SC, ST = (ctx.sb["QB"], ctx.sb["OB"], ctx.sb["WB"],
                              ctx.sb["ES"], ctx.sb["SC"], ctx.sb["ST"])
    OBb = OB[:]
    OQ = OBb[:, 8192:12288].rearrange("p (c f2) -> p c f2", f2=256)
    OK = OBb[:, 12288:16384].rearrange("p (c f2) -> p c f2", f2=256)
    WV = WB[:, 4096:8192].rearrange("p (c f2) -> p c f2", f2=256)
    xres = f"xsl{xs}"
    kloc, vloc = ctx.dram["kloc"], ctx.dram["vloc"]
    for ts_ in range(4):
        for side, WS, wres, b0 in (("q", OQ, "OB_Q", 0), ("k", OK, "OB_K", 1)):
            bank = b0
            for c in range(NCH):
                def fm(eng, c=c, bank=bank, WS=WS, ts_=ts_):
                    return eng.matmul(ctx.ps[bank][:, 0:256],
                                      lhsT=xslot3[:, c, ts_ * 128:(ts_ + 1) * 128],
                                      rhs=WS[:, c, :], start=(c == 0),
                                      stop=(c == NCH - 1))
                em.op("tensor", fm, r=(xres, wres), w=(f"ps{bank}",), tag="fm.mm")
            def fy(eng, bank=bank):
                return eng.tensor_mul(out=SC[:, 0:256], in0=ctx.ps[bank][:, 0:256],
                                      in1=ctx.sb["SC2"][:, 0:256])
            em.op("vector", fy, r=(f"ps{bank}", "gbt"), w=("SC_A",), tag="fm.g")
            def fy2(eng):
                return eng.tensor_add(out=SC[:, 0:256], in0=SC[:, 0:256],
                                      in1=ctx.sb["SC2"][:, 256:512])
            em.op("vector", fy2, r=("SC_A", "gbt"), w=("SC_A",), tag="fm.b")
            def fs(eng):
                return eng.reduce_sum(out=ST[:, 0:2],
                                      in_=SC[:, 0:256].rearrange("p (a b) -> p a b", b=128),
                                      axis=AXX)
            em.op("vector", fs, r=("SC_A",), w=("fm_s",), tag="fm.s")
            def fq(eng):
                return eng.activation(out=SC[:, 256:512], in_=SC[:, 0:256],
                                      func=AF.Square)
            em.op("scalar", fq, r=("SC_A",), w=("fm_q",), tag="fm.q")
            def fs2(eng):
                return eng.reduce_sum(out=ST[:, 2:4],
                                      in_=SC[:, 256:512].rearrange("p (a b) -> p a b", b=128),
                                      axis=AXX)
            em.op("vector", fs2, r=("fm_q",), w=("fm_s2",), tag="fm.s2")
            def fmu(eng):
                return eng.tensor_scalar(out=ST[:, 4:6], in0=ST[:, 0:2],
                                         scalar1=1.0 / 128, scalar2=None,
                                         op0=ALU.mult)
            em.op("vector", fmu, r=("fm_s",), w=("fm_mu",), tag="fm.mu")
            def fvar(eng):
                return eng.tensor_scalar(out=ST[:, 6:8], in0=ST[:, 2:4],
                                         scalar1=1.0 / 128, scalar2=None,
                                         op0=ALU.mult)
            em.op("vector", fvar, r=("fm_s2",), w=("fm_va",), tag="fm.va")
            def fvb(eng):
                return eng.scalar_tensor_tensor(out=ST[:, 8:10], in0=ST[:, 4:6],
                                                scalar=-1.0, in1=ST[:, 4:6],
                                                op0=ALU.mult, op1=ALU.mult)
            em.op("vector", fvb, r=("fm_mu",), w=("fm_m2",), tag="fm.m2")
            def fvc(eng):
                return eng.tensor_add(out=ST[:, 6:8], in0=ST[:, 6:8],
                                      in1=ST[:, 8:10])
            em.op("vector", fvc, r=("fm_va", "fm_m2"), w=("fm_va",), tag="fm.vc")
            def fl_(eng):
                return eng.activation(out=ST[:, 10:12], in_=ST[:, 6:8],
                                      func=AF.Ln,
                                      bias=ctx.sb["c_mc"][:, 4:5])
            em.op("scalar", fl_, r=("fm_va", "c_mc"), w=("fm_l",), tag="fm.l")
            def fe(eng):
                return eng.activation(out=ST[:, 12:14], in_=ST[:, 10:12],
                                      func=AF.Exp, scale=-0.5)
            em.op("scalar", fe, r=("fm_l",), w=("fm_r",), tag="fm.e")
            for hh in range(2):
                def fa(eng, hh=hh):
                    return eng.tensor_scalar(
                        out=ctx.sb["SB16"][:, 0:128], in0=SC[:, hh * 128:(hh + 1) * 128],
                        scalar1=ST[:, 4 + hh:5 + hh], scalar2=ST[:, 12 + hh:13 + hh],
                        op0=ALU.subtract, op1=ALU.mult)
                em.op("vector", fa, r=("SC_A", "fm_mu", "fm_r"), w=("fm_y",),
                      tag="fm.a")
                def ft(eng):
                    return eng.transpose(ctx.ps["tr"][:, 0:128],
                                         ctx.sb["SB16"][:, 0:128],
                                         ctx.sb["c_ident"][:])
                em.op("tensor", ft, r=("fm_y", "c_ident"), w=("ps_tr",), tag="fm.t")
                if side == "q":
                    def fc(eng, hh=hh, ts_=ts_, f=f):
                        return eng.tensor_copy(
                            out=QB[:, hh * 8192 + f * 512 + ts_ * 128:
                                   hh * 8192 + f * 512 + (ts_ + 1) * 128],
                            in_=ctx.ps["tr"][:, 0:128])
                    em.op("vector", fc, r=("ps_tr",), w=("QB",), tag="fm.cq")
                else:
                    def fc(eng, hh=hh, ts_=ts_):
                        return eng.tensor_copy(
                            out=ES[:, hh * 512 + ts_ * 128:hh * 512 + (ts_ + 1) * 128],
                            in_=ctx.ps["tr"][:, 0:128])
                    em.op("vector", fc, r=("ps_tr",), w=("esK",), tag="fm.ck")
    def fks(eng, f=f):
        return eng.dma_start(
            out=kloc[0:256, f * 512:(f + 1) * 512]
                .rearrange("(a p) b -> p a b", p=128),
            in_=ES[:, 0:1024].rearrange("p (a b) -> p a b", b=512))
    em.op("sync", fks, r=("esK",), w=("kloc",), dma=True, tag="fm.kst")
    for ts_ in range(4):
        bank = ts_ % 2
        for c in range(NCH):
            def fm(eng, c=c, bank=bank, ts_=ts_):
                return eng.matmul(ctx.ps[bank][:, 0:256],
                                  lhsT=xslot3[:, c, ts_ * 128:(ts_ + 1) * 128],
                                  rhs=WV[:, c, :], start=(c == 0),
                                  stop=(c == NCH - 1))
            em.op("tensor", fm, r=(xres, "WB_V"), w=(f"ps{bank}",), tag="fmv.mm")
        _evac_fm(ctx, f"ps{bank}", ctx.ps[bank][:, 0:256],
                 ctx.sb["RT"][:, ts_ * 256:(ts_ + 1) * 256], "esV", tag="fmv.ev")
    def fvs(eng, f=f):
        return eng.dma_start(
            out=vloc[f * 512:(f + 1) * 512, 0:256]
                .rearrange("(a p) b -> p a b", p=128),
            in_=ctx.sb["RT"][:, 0:1024].rearrange("p (a b) -> p a b", b=256))
    em.op("sync", fvs, r=("esV",), w=("vloc",), dma=True, tag="fmv.st")


def build_gb_tiles(ctx):
    em = ctx.em
    SC2 = ctx.sb["SC2"]
    for which, srcn, c0 in (("g", "L1_attn_gamma", 0), ("b", "L1_attn_beta", 256)):
        def fl(eng, srcn=srcn, c0=c0):
            return eng.dma_start(
                out=SC2[:, c0:c0 + 256].rearrange("p (a b) -> p a b", b=128),
                in_=ctx.inp[srcn].unsqueeze(0).unsqueeze(0)
                .broadcast_to((128, 2, 128)))
        em.op("gpsimd", fl, r=("gbt",), w=("gbt",), dma=True, tag="gb.ld")



def prep_globals_l0a(ctx):
    em = ctx.em
    OB, WB, KVA, SB16 = ctx.sb["OB"], ctx.sb["WB"], ctx.sb["KVA"], ctx.sb["SB16"]
    OK = OB[:][:, 12288:16384].rearrange("p (c f2) -> p c f2", f2=256)
    WV = WB[:, 4096:8192].rearrange("p (c f2) -> p c f2", f2=256)
    g = ctx.inp["L0_attn_global_tokens"]
    for c0 in range(0, 16, 8):
        def fl(eng, c0=c0):
            return eng.dma_start(out=ctx.sb["RT"][0:16, 0:1024],
                                 in_=g[0][:, c0 * 128:(c0 + 8) * 128])
        em.op("gpsimd", fl, r=("RT",), w=("RT",), dma=True, tag="gl.ld")
        for j in range(8):
            def ft(eng, j=j):
                return eng.transpose(ctx.ps["tr"][:, j * 16:(j + 1) * 16],
                                     ctx.sb["RT"][0:16, j * 128:(j + 1) * 128],
                                     ctx.sb["c_ident"][0:16, 0:16])
            em.op("tensor", ft, r=("RT", "c_ident"), w=("ps_tr",), tag="gl.t")
        def fcp(eng, c0=c0):
            return eng.tensor_copy(out=SB16[:, c0 * 16:(c0 + 8) * 16],
                                   in_=ctx.ps["tr"][:, 0:128])
        em.op("vector", fcp, r=("ps_tr",), w=("gT",), tag="gl.cp")
    for hh in range(2):
        for c in range(NCH):
            def fm(eng, c=c, hh=hh):
                return eng.matmul(ctx.ps[0][:, 0:16],
                                  lhsT=OK[:, c, hh * 128:(hh + 1) * 128],
                                  rhs=SB16[:, c * 16:(c + 1) * 16],
                                  start=(c == 0), stop=(c == NCH - 1))
            em.op("tensor", fm, r=("OB_K", "gT"), w=("ps0",), tag="gl.kmm")
        def fe(eng, hh=hh):
            return eng.tensor_copy(out=KVA[:, 2080 + hh * 16:2096 + hh * 16],
                                   in_=ctx.ps[0][:, 0:16])
        em.op("vector", fe, r=("ps0",), w=("kglob",), tag="gl.kev")
    def fo(eng):
        return eng.memset(KVA[0:16, 2120:2384].rearrange("p (g2 f2) -> p g2 f2", f2=132)[:, :, 128:129], 1.0)
    em.op("vector", fo, w=("vglob",), tag="gl.vo")
    for c in range(NCH):
        def fm(eng, c=c):
            return eng.matmul(ctx.ps[1][0:16, 0:256],
                              lhsT=SB16[:, c * 16:(c + 1) * 16],
                              rhs=WV[:, c, :], start=(c == 0),
                              stop=(c == NCH - 1))
        em.op("tensor", fm, r=("gT", "WB_V"), w=("ps1",), tag="gl.vmm")
    def fe2(eng):
        return eng.tensor_copy(
            out=KVA[0:16, 2120:2384].rearrange("p (g2 f2) -> p g2 f2", f2=132)[:, :, 0:128],
            in_=ctx.ps[1][0:16, 0:256].rearrange("p (g2 f2) -> p g2 f2", f2=128))
    em.op("vector", fe2, r=("ps1", "vglob"), w=("vglob",), tag="gl.vev")


def flash_fat(ctx, cfg, base, f):
    em = ctx.em
    hl, dh, dv = cfg["hl"], cfg["dh"], cfg["dv"]
    M, scale = cfg["M"], cfg["scale"]
    band = cfg.get("band")
    kv_shared = cfg.get("kv_shared", False) or base == "l3a"
    KVA, PB, QB, ES = ctx.sb["KVA"], ctx.sb["PB"], ctx.sb["QB"], ctx.sb["ES"]
    OF = ctx.sb["OF"]
    kloc, vloc = ctx.dram["kloc"], ctx.dram["vloc"]
    koff = [s * 128 for s in range(8)]
    voff = [1024 + s * 132 for s in range(8)]
    b = AB[f]
    PSL = [(PB, 0, "pb0"), (PB, 512, "pb1"), (ES, 0, "pb2"), (ES, 512, "pb3")]
    SBK = [0, 1, 6]
    use_g = bool(cfg.get("use_global"))
    OBANK = [2 + i for i in range(hl)] if base == "l3a" else [2, 3]
    DBANK = [4, 5]

    def qsrc(h):
        if base == "l3a":
            return QB[(h % 2) * 64:(h % 2) * 64 + 64,
                      (h // 2) * 8192:(h // 2 + 1) * 8192]
        return QB[:, h * 8192:(h + 1) * 8192]

    def jrange(kt):
        q0a = b * BLK
        ktr = kt - q0a // 128
        jlo = max(0, ktr)
        jhi = min(3, ktr + 2) if band else 3
        if jlo > jhi:
            return None
        diag_j = ktr if 0 <= ktr <= 3 else None
        uedge_j = (ktr + 2) if (band and 0 <= ktr + 2 <= 3) else None
        return jlo, jhi, diag_j, uedge_j

    kt0 = max(0, (b * BLK) // 128 - 2) if band else 0
    kt_list = [kt for kt in range(kt0, (b + 1) * 4) if jrange(kt) is not None]
    n = len(kt_list)
    touch = {}
    for kt in kt_list:
        jlo, jhi, _, _ = jrange(kt)
        for j in range(jlo, jhi + 1):
            touch.setdefault(j, []).append(kt)

    func = AF.Exp if M is not None else AF.Square
    akw = (dict(scale=scale) if M is None else
           dict(scale=scale, bias=ctx.sb["c_mc"][:, cfg["mcc"]:cfg["mcc"] + 1]))

    for h in range(hl):
        hk0 = 0 if kv_shared else h * dh
        hv0 = 0 if kv_shared else h * dv
        qb_ = 64 * (h % 2) if base == "l3a" else 0
        goff = 1 if use_g else 0
        n_it = n + goff

        def emit_loads(ridx, hk0=hk0, hv0=hv0):
            for rx in range(ridx, min(ridx + 4, n)):
                kt = kt_list[rx]
                si = rx % 8
                blkk = kt * 128 // BLK
                gc = FAT_OF_BLOCK[blkk] * BLK + (kt * 128) % BLK
                def flk(eng, si=si, gc=gc, hk0=hk0):
                    return eng.dma_start(
                        out=KVA[0:dh, koff[si]:koff[si] + 128],
                        in_=kloc[hk0:hk0 + dh, gc:gc + 128])
                em.op("sync", flk, r=("kloc", f"kslot{si}"), w=(f"kslot{si}",),
                      dma=True, tag="fl.ldk")
                if base == "l3a":
                    def flk2(eng, si=si, gc=gc):
                        return eng.dma_start(
                            out=KVA[64:128, koff[si]:koff[si] + 128],
                            in_=kloc[0:64, gc:gc + 128])
                    em.op("sync", flk2, r=("kloc", f"kslot{si}"),
                          w=(f"kslot{si}",), dma=True, tag="fl.ldk2")
                def flv(eng, si=si, gc=gc, hv0=hv0):
                    return eng.dma_start(
                        out=KVA[:, voff[si]:voff[si] + dv],
                        in_=vloc[gc:gc + 128, hv0:hv0 + dv])
                em.op("sync", flv, r=("vloc", f"vslot{si}"), w=(f"vslot{si}",),
                      dma=True, tag="fl.ldv")

        def stage_av(it, h=h, goff=goff):
            buf, poff, pres = PSL[it % 4]
            ob = OBANK[h]
            st = (it == 0)
            sp = (it == n_it - 1)
            if use_g and it == 0:
                def favg(eng, buf=buf, poff=poff, ob=ob, sp=sp):
                    return eng.matmul(
                        ctx.ps[ob][0:dv, 0:512],
                        lhsT=KVA[0:16, 2120 + h * 132:2120 + h * 132 + dv],
                        rhs=buf[0:16, poff:poff + 512],
                        start=True, stop=sp)
                em.op("tensor", favg, r=(pres, "vglob"), w=(f"ps{ob}",),
                      tag="gf.av")
                def fdng(eng, buf=buf, poff=poff, sp=sp, db=DBANK[h]):
                    return eng.matmul(
                        ctx.ps[db][0:1, 0:512],
                        lhsT=ctx.sb["c_ones_col"][0:16, 0:1],
                        rhs=buf[0:16, poff:poff + 512],
                        start=True, stop=sp)
                em.op("tensor", fdng, r=(pres, "c_ones_col"),
                      w=(f"ps{DBANK[h]}",), tag="gf.dn")
                return
            rx = it - goff
            kt = kt_list[rx]
            si = rx % 8
            jlo, jhi, _, _ = jrange(kt)
            nq = jhi - jlo + 1
            vw = dv + 1 if base == "l3a" else dv
            def fav(eng, jlo=jlo, nq=nq, buf=buf, poff=poff, si=si,
                    ob=ob, st=st, sp=sp, vw=vw):
                return eng.matmul(
                    ctx.ps[ob][0:vw, jlo * 128:(jlo + nq) * 128],
                    lhsT=KVA[:, voff[si]:voff[si] + vw],
                    rhs=buf[:, poff:poff + nq * 128],
                    start=st, stop=sp)
            em.op("tensor", fav, r=(pres, f"vslot{si}", "vones"),
                  w=(f"ps{ob}",), tag="fl.av")
            if base != "l3a":
                def fdn(eng, jlo=jlo, nq=nq, buf=buf, poff=poff,
                        st=st, sp=sp, db=DBANK[h]):
                    return eng.matmul(
                        ctx.ps[db][0:1, jlo * 128:(jlo + nq) * 128],
                        lhsT=ctx.sb["c_ones_col"][:],
                        rhs=buf[:, poff:poff + nq * 128],
                        start=st, stop=sp)
                em.op("tensor", fdn, r=(pres, "c_ones_col"),
                      w=(f"ps{DBANK[h]}",), tag="fl.dn")

        for it in range(n_it):
            buf, poff, pres = PSL[it % 4]
            sb_ = SBK[it % 3]
            if use_g and it == 0:
                def fqkg(eng, h=h, f=f, sb_=sb_):
                    return eng.matmul(
                        ctx.ps[sb_][0:16, 0:512],
                        lhsT=KVA[:, 2080 + h * 16:2096 + h * 16],
                        rhs=qsrc(h)[:, f * 512:(f + 1) * 512],
                        start=True, stop=True)
                em.op("tensor", fqkg, r=("kglob", "QB"), w=(f"ps{sb_}",),
                      tag="gf.qk")
                def fexg(eng, sb_=sb_, buf=buf, poff=poff):
                    return eng.activation(
                        out=buf[0:16, poff:poff + 512],
                        in_=ctx.ps[sb_][0:16, 0:512], func=AF.Exp, scale=scale,
                        bias=ctx.sb["c_mc"][0:16, cfg["mcc"]:cfg["mcc"] + 1])
                em.op("scalar", fexg, r=(f"ps{sb_}", "c_mc"), w=(pres,),
                      tag="gf.exp")
            else:
                rx = it - goff
                if rx % 4 == 0:
                    emit_loads(rx)
                kt = kt_list[rx]
                si = rx % 8
                jlo, jhi, diag_j, uedge_j = jrange(kt)
                nq = jhi - jlo + 1
                def fqk(eng, h=h, si=si, f=f, jlo=jlo, nq=nq, sb_=sb_,
                        qb_=qb_):
                    return eng.matmul(
                        ctx.ps[sb_][:, 0:nq * 128],
                        lhsT=KVA[qb_:qb_ + dh, koff[si]:koff[si] + 128],
                        rhs=qsrc(h)[:, f * 512 + jlo * 128:f * 512 + (jlo + nq) * 128],
                        start=True, stop=True)
                em.op("tensor", fqk, r=(f"kslot{si}", "QB"), w=(f"ps{sb_}",),
                      tag="fl.qk")
                def fex(eng, sb_=sb_, buf=buf, poff=poff, nq=nq):
                    return eng.activation(out=buf[:, poff:poff + nq * 128],
                                          in_=ctx.ps[sb_][:, 0:nq * 128],
                                          func=func, **akw)
                em.op("scalar", fex, r=(f"ps{sb_}", "c_mc"), w=(pres,),
                      tag="fl.ex")
                for (mj, mask) in ((diag_j, "c_maskL"), (uedge_j, "c_maskU")):
                    if mj is None:
                        continue
                    def fmk(eng, buf=buf, poff=poff, mj=mj, jlo=jlo, mask=mask):
                        return eng.tensor_mul(
                            out=buf[:, poff + (mj - jlo) * 128:poff + (mj - jlo + 1) * 128],
                            in0=buf[:, poff + (mj - jlo) * 128:poff + (mj - jlo + 1) * 128],
                            in1=ctx.sb[mask][:])
                    em.op("vector", fmk, r=(pres, mask), w=(pres,),
                          tag="fl.mk")
            if it >= 3:
                stage_av(it - 3)
        for it in range(max(0, n_it - 3), n_it):
            stage_av(it)
        ob = OBANK[h]
        rsl = 1024 + (h % 2) * 512
        rres = "SC_C" if h % 2 == 0 else "SC_D"
        esl = (h % 2) * 512
        eres = "pb2" if h % 2 == 0 else "pb3"
        if base == "l3a":
            den_in = (lambda ob=ob: ctx.ps[ob][64:65, 0:512])
            den_res = f"ps{ob}"
        else:
            den_in = (lambda db=DBANK[h]: ctx.ps[db][0:1, 0:512])
            den_res = f"ps{DBANK[h]}"
        if cfg.get("denom_eps"):
            def fre(eng, rsl=rsl, den_in=den_in):
                return eng.tensor_scalar(
                    out=ctx.sb["SC"][0:1, rsl:rsl + 512], in0=den_in(),
                    scalar1=cfg["denom_eps"], scalar2=None, op0=ALU.add)
            em.op("vector", fre, r=(den_res, rres), w=(rres,), tag="fl.rce")
            def fr(eng, rsl=rsl):
                return eng.reciprocal(out=ctx.sb["SC"][0:1, rsl:rsl + 512],
                                      in_=ctx.sb["SC"][0:1, rsl:rsl + 512])
            em.op("vector", fr, r=(rres,), w=(rres,), tag="fl.rcp")
        else:
            def fr(eng, rsl=rsl, den_in=den_in):
                return eng.reciprocal(out=ctx.sb["SC"][0:1, rsl:rsl + 512],
                                      in_=den_in())
            em.op("vector", fr, r=(den_res, rres), w=(rres,), tag="fl.rcp")
        def fbc(eng, rsl=rsl):
            return eng.matmul(
                ctx.ps["tr"][:].bitcast(F32)[:, 0:512],
                lhsT=ctx.sb["c_onesrf"][:],
                rhs=ctx.sb["SC"][0:1, rsl:rsl + 512],
                start=True, stop=True)
        em.op("tensor", fbc, r=(rres, "c_onesrf"), w=("ps_tr",), tag="fl.bc")
        def fev(eng, esl=esl):
            return eng.copy(out=ES[:, esl:esl + 512],
                            in_=ctx.ps["tr"][:].bitcast(F32)[:, 0:512])
        em.op("scalar", fev, r=("ps_tr", eres), w=(eres,), tag="fl.be")
        if base == "l3a":
            def fmu(eng, h=h, ob=ob, esl=esl):
                return eng.tensor_mul(
                    out=OF[(h % 2) * 64:(h % 2) * 64 + 64,
                           (h // 2) * 512:(h // 2) * 512 + 512],
                    in0=ctx.ps[ob][0:dv, 0:512], in1=ES[0:dv, esl:esl + 512])
            em.op("vector", fmu, r=(f"ps{ob}", eres), w=("OFr",), tag="fl.om")
        else:
            def fmu(eng, h=h, ob=ob, esl=esl):
                return eng.tensor_mul(
                    out=OF[:, h * 512:(h + 1) * 512],
                    in0=ctx.ps[ob][0:dv, 0:512], in1=ES[:, esl:esl + 512])
            em.op("vector", fmu, r=(f"ps{ob}", eres), w=("OFr",), tag="fl.om")


def partial_out_fat(ctx, cfg, f):
    em = ctx.em
    OF, RT = ctx.sb["OF"], ctx.sb["RT"]
    OW = ctx.sb["OB"][:][:, 4096:8192].rearrange("p (c f2) -> p c f2", f2=2048)
    for ts_ in range(4):
        for half in range(2):
            for mf in (2 * half, 2 * half + 1):
                bank = mf % 2 + 2
                for c in range(2):
                    def fm(eng, ts_=ts_, mf=mf, c=c, bank=bank):
                        return eng.matmul(
                            ctx.ps[bank][:, 0:512],
                            lhsT=OF[:, c * 512 + ts_ * 128:c * 512 + (ts_ + 1) * 128],
                            rhs=OW[:, c, mf * 512:(mf + 1) * 512],
                            start=(c == 0), stop=(c == 1))
                    em.op("tensor", fm, r=("OFr", "OB_W"), w=(f"ps{bank}",),
                          tag="po.mm")
                def fe(eng, mf=mf, bank=bank):
                    return eng.copy(out=RT[:, (mf % 2) * 512:(mf % 2) * 512 + 512],
                                    in_=ctx.ps[bank][:, 0:512])
                em.op("scalar", fe, r=(f"ps{bank}", "RTpo"), w=("RTpo",),
                      tag="po.ev")
            def fst(eng, f=f, ts_=ts_, half=half):
                return eng.dma_start(
                    out=ctx.dram["rs_in"][f * 512 + ts_ * 128:f * 512 + (ts_ + 1) * 128,
                                          half * 1024:(half + 1) * 1024],
                    in_=RT[:, 0:1024])
            em.op("sync", fst, r=("RTpo",), w=("rs_in",), dma=True, tag="po.st")


USE_GELU_TANH_LUT = True


def gelu_tanh_chain(ctx, gate_ps, psres, up_ps, upres, out_ap, out_res):
    em = ctx.em
    SC, SC2 = ctx.sb["SC"], ctx.sb["SC2"]
    if USE_GELU_TANH_LUT:
        def f1(eng):
            return eng.activation(out=SC[:, 0:512], in_=gate_ps,
                                  func=AF.Gelu_apprx_tanh)
        em.op("scalar", f1, r=(psres,), w=("SC_A",), tag="gl.g")
        def f2(eng):
            return eng.tensor_tensor(out=out_ap, in0=SC[:, 0:512], in1=up_ps,
                                     op=ALU.mult)
        em.op("vector", f2, r=("SC_A", upres), w=(out_res,), tag="gl.m")
        return
    def f1(eng):
        return eng.activation(out=SC[:, 0:512], in_=gate_ps, func=AF.Square)
    em.op("scalar", f1, r=(psres,), w=("SC_A",), tag="gl.sq")
    def f2(eng):
        return eng.tensor_scalar(out=SC[:, 0:512], in0=SC[:, 0:512],
                                 scalar1=0.044715 * 0.7978845608028654,
                                 scalar2=0.7978845608028654,
                                 op0=ALU.mult, op1=ALU.add)
    em.op("vector", f2, r=("SC_A",), w=("SC_A",), tag="gl.af")
    def f3(eng):
        return eng.tensor_tensor(out=SC[:, 512:1024], in0=SC[:, 0:512],
                                 in1=gate_ps, op=ALU.mult)
    em.op("vector", f3, r=("SC_A", psres), w=("SC_B",), tag="gl.p")
    def f4(eng):
        return eng.activation(out=SC2[:, 0:512], in_=SC[:, 512:1024],
                              func=AF.Tanh)
    em.op("scalar", f4, r=("SC_B",), w=("gl_t",), tag="gl.t")
    def f5(eng):
        return eng.tensor_scalar(out=SC[:, 512:1024], in0=SC2[:, 0:512],
                                 scalar1=1.0, scalar2=0.5,
                                 op0=ALU.add, op1=ALU.mult)
    em.op("vector", f5, r=("gl_t",), w=("SC_B",), tag="gl.h")
    def f6(eng):
        return eng.tensor_tensor(out=SC[:, 0:512], in0=SC[:, 512:1024],
                                 in1=gate_ps, op=ALU.mult)
    em.op("vector", f6, r=("SC_B", psres), w=("SC_A",), tag="gl.xg")
    def f7(eng):
        return eng.tensor_tensor(out=out_ap, in0=SC[:, 0:512], in1=up_ps,
                                 op=ALU.mult)
    em.op("vector", f7, r=("SC_A", upres), w=(out_res,), tag="gl.up")


def ffn_down_group(ctx, w2name, gi, w2row0, dacc, first):
    em = ctx.em
    QB, WB, SC = ctx.sb["QB"], ctx.sb["WB"], ctx.sb["SC"]
    for dmf in range(4):
        def fw(eng, dmf=dmf, w2row0=w2row0):
            return eng.dma_start(
                out=WB[:, 0:8192].rearrange("p (c f2) -> p c f2", f2=512),
                in_=ctx.wc[w2name][w2row0:w2row0 + 2048,
                                   dmf * 512:(dmf + 1) * 512]
                    .rearrange("(c p) f2 -> p c f2", p=128))
        em.op("sync", fw, r=(("wc", w2name), "WB"), w=("WB",), dma=True,
              tag="fd.w")
        for ts_ in range(8):
            bank = 2 + ts_ % 2
            for cg in range(16):
                def fm(eng, ts_=ts_, cg=cg, bank=bank):
                    return eng.matmul(
                        ctx.ps[bank][:, 0:512],
                        lhsT=QB[:, cg * 1024 + ts_ * 128:cg * 1024 + (ts_ + 1) * 128],
                        rhs=WB[:, cg * 512:(cg + 1) * 512],
                        start=(cg == 0), stop=(cg == 15))
                em.op("tensor", fm, r=("QB", "WB"), w=(f"ps{bank}",), tag="fd.mm")
            sl = (ts_ % 2) * 512
            def fe(eng, bank=bank, sl=sl):
                return eng.tensor_copy(out=SC[:, sl:sl + 512],
                                       in_=ctx.ps[bank][:, 0:512])
            em.op("vector", fe, r=(f"ps{bank}",), w=(f"fd{sl}",), tag="fd.ev")
            def fst(eng, ts_=ts_, dmf=dmf, sl=sl, first=first):
                return eng.dma_start(
                    out=dacc[ts_ * 128:(ts_ + 1) * 128,
                             dmf * 512:(dmf + 1) * 512],
                    in_=SC[:, sl:sl + 512],
                    accum_op=(ALU.bypass if first else ALU.add))
            em.op("gpsimd", fst, r=(f"fd{sl}", "dacc"), w=("dacc",), dma=True,
                  tag="fd.st")


def ffn_site(ctx, site):
    em = ctx.em
    base = BASE.get(site, site)
    QB, WB, SC, ST, DN = (ctx.sb["QB"], ctx.sb["WB"], ctx.sb["SC"],
                          ctx.sb["ST"], ctx.sb["DN"])
    XN = ctx.sb["XN"]
    dacc = ctx.dram["dacc"]
    ln_to_xn(ctx, kind="ln")
    em.claim("QB", "WB", tag="ffn.cl")
    if base == "l1f":
        for c0 in range(0, 16, 8):
            def fwr(eng, c0=c0):
                return eng.dma_start(out=ctx.sb["RT"][0:16, 0:1024],
                                     in_=ctx.inp["L1_ffn_Wr"][:, c0 * 128:(c0 + 8) * 128])
            em.op("gpsimd", fwr, r=("RT",), w=("RT",), dma=True, tag="rt.ld")
            for j in range(8):
                def ft(eng, j=j):
                    return eng.transpose(ctx.ps["tr"][:, j * 16:(j + 1) * 16],
                                         ctx.sb["RT"][0:16, j * 128:(j + 1) * 128],
                                         ctx.sb["c_ident"][0:16, 0:16])
                em.op("tensor", ft, r=("RT", "c_ident"), w=("ps_tr",), tag="rt.t")
            def fcp(eng, c0=c0):
                return eng.tensor_copy(out=ctx.sb["SB16"][:, c0 * 16:(c0 + 8) * 16],
                                       in_=ctx.ps["tr"][:, 0:128])
            em.op("vector", fcp, r=("ps_tr",), w=("wrT",), tag="rt.cp")
        def fbr(eng):
            return eng.dma_start(out=ctx.sb["SG"][:, 32:48],
                                 in_=ctx.inp["L1_ffn_br"].unsqueeze(0)
                                 .broadcast_to((128, 16)))
        em.op("gpsimd", fbr, r=("SG",), w=("SG",), dma=True, tag="br.ld")
        n_exp = 16
    if base == "l3f":
        def fwg(eng):
            return eng.dma_start(out=WB[:, 0:128].rearrange("p (c f2) -> p c f2", f2=8),
                                 in_=ctx.inp["L3_ffn_wg"]
                                 .rearrange("(c p) e -> p c e", p=128))
        em.op("gpsimd", fwg, r=("WB",), w=("WB",), dma=True, tag="wg.ld")
        n_exp = 8
    if base in ("l1f", "l3f"):
        for ts_ in range(8):
            fat, tt0 = divmod(ts_, 4)
            for c in range(NCH):
                def fm(eng, c=c, ts_=ts_, fat=fat, tt0=tt0):
                    rhs = (ctx.sb["SB16"][:, c * 16:(c + 1) * 16] if base == "l1f"
                           else WB[:, c * 8:(c + 1) * 8])
                    return eng.matmul(ctx.ps[6][:, 0:n_exp],
                                      lhsT=XN[:, c, ts_ * 128:(ts_ + 1) * 128],
                                      rhs=rhs, start=(c == 0),
                                      stop=(c == NCH - 1))
                em.op("tensor", fm,
                      r=("XN", "wrT" if base == "l1f" else "WB"),
                      w=("ps6",), tag="gt.mm")
            if base == "l1f":
                def fl_(eng, ts_=ts_):
                    return eng.tensor_tensor(out=SC[:, 0:n_exp],
                                             in0=ctx.ps[6][:, 0:n_exp],
                                             in1=ctx.sb["SG"][:, 32:48],
                                             op=ALU.add)
                em.op("vector", fl_, r=("ps6", "SG"), w=("SC_A",), tag="gt.br")
            else:
                def fl_(eng, ts_=ts_):
                    return eng.tensor_copy(out=SC[:, 0:n_exp],
                                           in_=ctx.ps[6][:, 0:n_exp])
                em.op("vector", fl_, r=("ps6",), w=("SC_A",), tag="gt.cp")
            def fz(eng, ts_=ts_):
                return eng.memset(DN[:, ts_ * 16:ts_ * 16 + n_exp], 0.0)
            em.op("vector", fz, w=("DNa",), tag="gt.z")
            for it in range(4):
                def f1(eng):
                    return eng.reduce_max(out=ST[:, 0:1], in_=SC[:, 0:n_exp],
                                          axis=AXX)
                em.op("vector", f1, r=("SC_A",), w=("gtm",), tag="gt.mx")
                def f2(eng):
                    return eng.tensor_scalar(out=SC[:, 64:64 + n_exp],
                                             in0=SC[:, 0:n_exp],
                                             scalar1=ST[:, 0:1], scalar2=None,
                                             op0=ALU.is_ge)
                em.op("vector", f2, r=("SC_A", "gtm"), w=("gtge",), tag="gt.ge")
                def f3(eng, ts_=ts_):
                    return eng.tensor_tensor(out=DN[:, ts_ * 16:ts_ * 16 + n_exp],
                                             in0=DN[:, ts_ * 16:ts_ * 16 + n_exp],
                                             in1=SC[:, 64:64 + n_exp], op=ALU.add)
                em.op("vector", f3, r=("gtge", "DNa"), w=("DNa",), tag="gt.acc")
                def f4a(eng):
                    return eng.tensor_scalar(out=SC[:, 96:96 + n_exp],
                                             in0=SC[:, 64:64 + n_exp],
                                             scalar1=-1e30, scalar2=None,
                                             op0=ALU.mult)
                em.op("vector", f4a, r=("gtge",), w=("gtsu",), tag="gt.sa")
                def f4(eng):
                    return eng.tensor_tensor(out=SC[:, 0:n_exp],
                                             in0=SC[:, 96:96 + n_exp],
                                             in1=SC[:, 0:n_exp], op=ALU.add)
                em.op("vector", f4, r=("gtsu", "SC_A"), w=("SC_A",), tag="gt.su")
    if base == "l2f":
        n_chunks, w2name = 128, "l2f_d"
        def gu_tile(name, c, ng):
            r = ng // 16
            return ctx.wc[name][2048 * r + c * 128:2048 * r + (c + 1) * 128,
                               (ng % 16) * 128:(ng % 16 + 1) * 128]
        for gi in range(8):
            for cg in range(16):
                ng = gi * 16 + cg
                def fwg_(eng, ng=ng):
                    return eng.dma_start(
                        out=WB[:, 0:2048].rearrange("p (c f2) -> p c f2", f2=128),
                        in_=ctx.wc["l2f_g"][2048 * (ng // 16):2048 * (ng // 16) + 2048,
                                            (ng % 16) * 128:(ng % 16 + 1) * 128]
                            .rearrange("(c p) f2 -> p c f2", p=128))
                em.op("sync", fwg_, r=(("wc", "l2f_g"), "WB"), w=("WB",),
                      dma=True, tag="l2.w")
                def fwu_(eng, ng=ng):
                    return eng.dma_start(
                        out=WB[:, 2048:4096].rearrange("p (c f2) -> p c f2", f2=128),
                        in_=ctx.wc["l2f_u"][2048 * (ng // 16):2048 * (ng // 16) + 2048,
                                            (ng % 16) * 128:(ng % 16 + 1) * 128]
                            .rearrange("(c p) f2 -> p c f2", p=128))
                em.op("sync", fwu_, r=(("wc", "l2f_u"), "WB"), w=("WB",),
                      dma=True, tag="l2.wu")
                for fat in range(2):
                    for c in range(NCH):
                        def fg(eng, c=c, fat=fat):
                            return eng.matmul(ctx.ps[0][:, 0:512],
                                              lhsT=WB[:, c * 128:(c + 1) * 128],
                                              rhs=XN[:, c, fat * 512:(fat + 1) * 512],
                                              start=(c == 0), stop=(c == NCH - 1))
                        em.op("tensor", fg, r=("WB", "XN"), w=("ps0",), tag="l2.g")
                        def fu(eng, c=c, fat=fat):
                            return eng.matmul(ctx.ps[1][:, 0:512],
                                              lhsT=WB[:, 2048 + c * 128:2048 + (c + 1) * 128],
                                              rhs=XN[:, c, fat * 512:(fat + 1) * 512],
                                              start=(c == 0), stop=(c == NCH - 1))
                        em.op("tensor", fu, r=("WB", "XN"), w=("ps1",), tag="l2.u")
                    out_ap = QB[:, cg * 1024 + fat * 512:cg * 1024 + (fat + 1) * 512]
                    gelu_tanh_chain(ctx, ctx.ps[0][:, 0:512], "ps0",
                                    ctx.ps[1][:, 0:512], "ps1", out_ap, "QB")
            ffn_down_group(ctx, w2name, gi, gi * 2048, dacc, gi == 0)
    else:
        w1name = "l1f_w1" if base == "l1f" else "l3f_w1"
        w2name = "l1f_w2" if base == "l1f" else "l3f_w2"
        n_groups = 4
        for gi in range(n_groups):
            for ii in range(4):
                ncol0 = gi * 2048 + ii * 512
                def fw1(eng, ncol0=ncol0):
                    return eng.dma_start(
                        out=WB[:, 0:8192].rearrange("p (c f2) -> p c f2", f2=512),
                        in_=ctx.wc[w1name][:, ncol0:ncol0 + 512]
                            .rearrange("(c p) f2 -> p c f2", p=128))
                em.op("sync", fw1, r=(("wc", w1name), "WB"), w=("WB",),
                      dma=True, tag="f13.w1")
                for ts_ in range(8):
                    fat, tt0 = divmod(ts_, 4)
                    bank = ts_ % 2
                    for c in range(NCH):
                        def fm(eng, c=c, ts_=ts_, bank=bank, fat=fat, tt0=tt0):
                            return eng.matmul(
                                ctx.ps[bank][:, 0:512],
                                lhsT=XN[:, c, fat * 512 + tt0 * 128:fat * 512 + (tt0 + 1) * 128],
                                rhs=WB[:, c * 512:(c + 1) * 512],
                                start=(c == 0), stop=(c == NCH - 1))
                        em.op("tensor", fm, r=("XN", "WB"), w=(f"ps{bank}",),
                              tag="f13.h")
                    ncol_abs = ncol0
                    if base == "l1f":
                        blk_i = ncol_abs // 512
                        def fe1(eng, bank=bank):
                            return eng.activation(out=SC[:, 0:512],
                                                  in_=ctx.ps[bank][:, 0:512],
                                                  func=AF.Erf,
                                                  scale=0.7071067811865476)
                        em.op("scalar", fe1, r=(f"ps{bank}",), w=("SC_A",),
                              tag="f13.erf")
                        def fe2(eng, bank=bank):
                            return eng.scalar_tensor_tensor(
                                out=SC[:, 512:1024], in0=SC[:, 0:512],
                                scalar=1.0, in1=ctx.ps[bank][:, 0:512],
                                op0=ALU.add, op1=ALU.mult)
                        em.op("vector", fe2, r=("SC_A", f"ps{bank}"),
                              w=("SC_B",), tag="f13.g")
                        def fe3(eng, ts_=ts_, blk_i=blk_i):
                            return eng.tensor_scalar(
                                out=ctx.sb["RT"][:, 0:512], in0=SC[:, 512:1024],
                                scalar1=DN[:, ts_ * 16 + blk_i:ts_ * 16 + blk_i + 1],
                                scalar2=0.5, op0=ALU.mult, op1=ALU.mult)
                        em.op("vector", fe3, r=("SC_B", "DNa"), w=("RTff",),
                              tag="f13.m")
                    else:
                        e_i = ncol_abs // 1024
                        def fe3(eng, bank=bank, ts_=ts_, e_i=e_i):
                            return eng.tensor_scalar(
                                out=ctx.sb["RT"][:, 0:512],
                                in0=ctx.ps[bank][:, 0:512],
                                scalar1=0.0,
                                scalar2=DN[:, ts_ * 16 + e_i:ts_ * 16 + e_i + 1],
                                op0=ALU.max, op1=ALU.mult)
                        em.op("vector", fe3, r=(f"ps{bank}", "DNa"), w=("RTff",),
                              tag="f13.rm")
                    for nc_ in range(4):
                        def ft(eng, nc_=nc_):
                            return eng.transpose(
                                ctx.ps["tr"][:, nc_ * 128:(nc_ + 1) * 128],
                                ctx.sb["RT"][:, nc_ * 128:(nc_ + 1) * 128],
                                ctx.sb["c_ident"][:])
                        em.op("tensor", ft, r=("RTff", "c_ident"), w=("ps_tr",),
                              tag="f13.t")
                    def fcp2(eng, ii=ii, ts_=ts_):
                        return eng.tensor_copy(
                            out=QB[:].rearrange("p (a b2) -> p a b2", b2=1024)
                                [:, ii * 4:(ii + 1) * 4, ts_ * 128:(ts_ + 1) * 128],
                            in_=ctx.ps["tr"][:, 0:512]
                                .rearrange("p (a b2) -> p a b2", b2=128))
                    em.op("vector", fcp2, r=("ps_tr",), w=("QB",), tag="f13.cp")
            ffn_down_group(ctx, w2name, gi, gi * 2048, dacc, gi == 0)
    scale = 0.5
    add_tokmajor_dram_to_R(ctx, dacc, scale, extra_r=("dacc",))


def build_program(subs):
    nc = bass.Bass("TRN2", target_bir_lowering=False, debug=False)
    em = Emitter()
    ctx = Ctx(nc, em, subs)
    bases = [BASE.get(s, s) for s in subs]
    needed_in = set()
    needed_w = []
    for b in bases:
        for x in SUB_IN[b]:
            needed_in.add(x)
        for wname in SUB_W[b]:
            if wname not in needed_w:
                needed_w.append(wname)
    ctx.inp["res_in"] = nc.dram_tensor("res_in", [D, TPC], F32,
                                       kind="ExternalInput").ap()
    for name in sorted(needed_in):
        shp, dt = INPUT_SHAPES[name]
        ctx.inp[name] = nc.dram_tensor(name, list(shp), dt,
                                       kind="ExternalInput").ap()
    for name, (shape, dt) in SBUF_CONSTS.items():
        ctx.inp[name] = nc.dram_tensor(name, shape, dt, kind="ExternalInput").ap()
    any_rope = any(b in ("l2a", "l3a") for b in bases)
    for name, (shape, dt) in DRAM_CONSTS.items():
        if any_rope:
            ctx.inp[name] = nc.dram_tensor(name, shape, dt,
                                           kind="ExternalInput").ap()
    ctx.out_res = nc.dram_tensor("res_out", [D, TPC], F32,
                                 kind="ExternalOutput").ap()
    for wname in needed_w:
        s = WSPEC[wname]
        ctx.wc[wname] = nc.dram_tensor("wc_" + wname, [s["rows"], s["cols"]],
                                       BF16, kind="Internal").ap()
        if s["kind"] in ("deq_gu", "deq_d"):
            ctx.dram["gin_" + wname] = nc.dram_tensor(
                "gin_" + wname, [2048, 2048], BF16, kind="Internal").ap()
    has_attn = any(b in ("l0f", "l0a", "l1a", "l2a", "l3a") for b in bases)
    if has_attn:
        ctx.dram["xg_in"] = nc.dram_tensor("xg_in", [D, TPC], BF16,
                                           kind="Internal").ap()
        ctx.dram["xg"] = nc.dram_tensor("xg", [NCORE * D, TPC], BF16,
                                        kind="Internal").ap()
        ctx.dram["kloc"] = nc.dram_tensor("kloc", [256, T + 16], BF16,
                                          kind="Internal").ap()
        ctx.dram["vloc"] = nc.dram_tensor("vloc", [T + 16, 256], BF16,
                                          kind="Internal").ap()
        ctx.dram["rs_in"] = nc.dram_tensor("rs_in", [T, D], BF16,
                                           kind="Internal").ap()
        ctx.dram["rs_out"] = nc.dram_tensor("rs_out", [TPC, D], BF16,
                                            kind="Internal").ap()
    if any(b in ("l1f", "l2f", "l3f") for b in bases):
        ctx.dram["dacc"] = nc.dram_tensor("dacc", [TPC, D], F32,
                                          kind="Internal").ap()
    dbg_shapes = {"xn": ([128, NCH, TPC], BF16), "xg": ([NCORE * D, TPC], BF16),
                  "kloc": ([256, T + 16], BF16), "vloc": ([T + 16, 256], BF16),
                  "rsin": ([T, D], BF16), "rsout": ([TPC, D], BF16),
                  "wcq": ([2048, 256], BF16), "sg": ([128, 16], F32),
                  "qb": ([128, 16384], BF16)}
    for t in TAPS:
        shp, dt = dbg_shapes[t]
        ctx.dram["dbg_" + t] = nc.dram_tensor("dbg_" + t, list(shp), dt,
                                              kind="ExternalOutput").ap()

    with nc.Block() as block:
        alloc_common(ctx)
        def fwarm0(eng):
            return eng.memset(ctx.sb["LR"][32:33, 504:506], 0.0)
        em.op("vector", fwarm0, w=("warm",), tag="warmup0")
        def fsgz(eng):
            return eng.memset(ctx.sb["SG"][:], 0.0)
        em.op("vector", fsgz, w=("SG",), tag="sgz")
        def fwarm(eng):
            return eng.activation(out=ctx.sb["LR"][32:33, 506:508],
                                  in_=ctx.sb["LR"][32:33, 504:506], func=AF.Exp)
        em.op("scalar", fwarm, r=("warm",), w=("warm",), tag="warmup")
        def fres(eng):
            return eng.dma_start(
                out=ctx.sb["R"][:],
                in_=ctx.inp["res_in"].rearrange("(c p) t -> p c t", p=128))
        em.op("sync", fres, w=("R",), dma=True, tag="ld.res")
        prep_tables(ctx, subs)
        done_w = set()
        def prep_for(site):
            for wname in SUB_W[BASE.get(site, site)]:
                if wname in needed_w and wname not in done_w:
                    done_w.add(wname)
                    prep_matrix(ctx, wname)
        for s in subs[:2]:
            prep_for(s)
        for i, site in enumerate(subs):
            base = BASE.get(site, site)
            if base in ("l0f", "l0a", "l1a", "l2a", "l3a"):
                attn_site(ctx, site)
            else:
                ffn_site(ctx, site)
            if i + 2 < len(subs):
                prep_for(subs[i + 2])
        def fout(eng):
            return eng.dma_start(
                out=ctx.out_res.rearrange("(c p) t -> p c t", p=128),
                in_=ctx.sb["R"][:])
        em.op("sync", fout, r=("R",), w=("res_out",), dma=True, tag="st.res")
        em.emit(nc, block)
    return nc


INPUT_SHAPES = {}


def _init_input_shapes():
    sp = {
        "L0_ffn_Wp": ((2048, 2048), F32),
        "hs_l0f_wstd": ((2,), F32), "hs_l0f_wrec": ((2,), F32),
        "hs_l0f_waq": ((2048, 240), F32), "hs_l0f_wak": ((2048, 240), F32),
        "hs_l0f_waql": ((2048, 16), F32), "hs_l0f_wakl": ((2048, 16), F32),
        "hs_l0f_wv": ((2048, 256), F32), "hs_l0f_wosl": ((256, 2048), F32),
        "L0_attn_global_tokens": ((1, 16, 2048), F32),
        "L0_attn_ln_scale": ((2048,), F32), "L0_attn_ln_bias": ((2048,), F32),
        "L1_ffn_W1": ((8192, 2048), F32), "L1_ffn_W2": ((2048, 8192), F32),
        "L1_ffn_Wr": ((16, 2048), F32), "L1_ffn_br": ((16,), F32),
        "L1_attn_gamma": ((128,), F32), "L1_attn_beta": ((128,), F32),
        "L2_attn_W_kv": ((256, 2048), F32),
        "hs_l2a_wkh": ((2, 128, 128), F32), "hs_l2a_wvh": ((2, 128, 128), F32),
        "hs_l2f_gpk": ((2048, 1024), I32), "hs_l2f_gsc": ((2048, 64), F32),
        "hs_l2f_upk": ((2048, 1024), I32), "hs_l2f_usc": ((2048, 64), F32),
        "hs_l2f_dpk": ((2048, 1024), I32), "hs_l2f_dsc": ((2048, 64), F32),
        "L3_ffn_wg": ((2048, 8), F32),
        "L3_ffn_w1": ((8, 1024, 2048), F32), "L3_ffn_w2": ((8, 2048, 1024), F32),
        "L3_attn_rms_w": ((2048,), F32),
        "hs_l3a_wq": ((256, 2048), F32), "hs_l3a_wkv": ((128, 2048), F32),
        "hs_l3a_wosl": ((256, 2048), F32),
    }
    for tag in ("l0a", "l1a"):
        for w in ("wq", "wk", "wv", "wosl"):
            sp[f"hs_{tag}_{w}"] = ((256, 2048), F32)
    sp["hs_l2a_wq"] = ((256, 2048), F32)
    sp["hs_l2a_wosl"] = ((256, 2048), F32)
    INPUT_SHAPES.update(sp)


_init_input_shapes()


def kernel(**inputs):
    consts = _host_consts()
    x = np.asarray(inputs["L0_attn_x"])[0]
    res_shards = []
    for c in range(NCORE):
        bA, bB = OWN_BLOCKS[c]
        sl = np.concatenate([x[bA * BLK:(bA + 1) * BLK],
                             x[bB * BLK:(bB + 1) * BLK]], 0)
        res_shards.append(np.ascontiguousarray(sl.T))
    slices = [make_host_slices(inputs, c) for c in range(NCORE)]
    for group in LAUNCH_GROUPS:
        nc = build_program(group)
        bases = [BASE.get(s, s) for s in group]
        needed = set()
        for b in bases:
            needed.update(SUB_IN[b])
        any_rope = any(b in ("l2a", "l3a") for b in bases)
        in_maps = []
        for c in range(NCORE):
            m = {"res_in": res_shards[c]}
            for name in needed:
                m[name] = (slices[c][name] if name.startswith("hs_")
                           else np.ascontiguousarray(inputs[name]))
            for cname in SBUF_CONSTS:
                m[cname] = consts[cname]
            if any_rope:
                for cname in DRAM_CONSTS:
                    m[cname] = consts[cname]
            in_maps.append(m)
        res = run_bass_kernel_spmd(nc, in_maps, core_ids=list(range(NCORE)))
        if TAPS:
            TAP_RESULTS.append([{k: np.asarray(v) for k, v in r.items()
                                 if k.startswith("dbg_")} for r in res.results])
        res_shards = [np.asarray(r["res_out"]) for r in res.results]
    out = np.empty((T, D), np.float32)
    for c in range(NCORE):
        bA, bB = OWN_BLOCKS[c]
        sh = res_shards[c].T
        out[bA * BLK:(bA + 1) * BLK] = sh[0:BLK]
        out[bB * BLK:(bB + 1) * BLK] = sh[BLK:]
    return out[None]


# version 07
# speedup vs baseline: 1.0358x; 1.0314x over previous
import math
import numpy as np
import ml_dtypes
import concourse.bass as bass
import concourse.mybir as mybir
from concourse.bass_utils import run_bass_kernel_spmd

F32 = mybir.dt.float32
BF16 = mybir.dt.bfloat16
I32 = mybir.dt.int32
AF = mybir.ActivationFunctionType
ALU = mybir.AluOpType
AXX = mybir.AxisListType.X

NCORE = 8
D = 2048
T = 8192
NCH = 16
TPC = 1024
NBLK = 16
BLK = 512
NFAT = 16

AB = []
for _r in range(NCORE):
    AB += [_r, NBLK - 1 - _r]
FAT_OF_BLOCK = {b: f for f, b in enumerate(AB)}
OWN_BLOCKS = [(c, NBLK - 1 - c) for c in range(NCORE)]

def gcol_of_abs(tok):
    b = tok // BLK
    return FAT_OF_BLOCK[b] * BLK + tok % BLK

M_L0F = 10.0
M_L0A = 12.0
M_L2A = 40.0
M_L3A = 12.0

LN_EPS = 1e-5
SUBLAYERS = ["l0f", "l0a", "l0f2", "l1f", "l1a", "l1f2",
             "l2f", "l2a", "l2f2", "l3f", "l3a", "l3f2"]
BASE = {"l0f2": "l0f", "l1f2": "l1f", "l2f2": "l2f", "l3f2": "l3f"}
LAUNCH_GROUPS = [list(SUBLAYERS)]
TAPS = []
TAP_RESULTS = []


def _host_consts():
    pos = np.array([AB[f] * BLK + i for f in range(NFAT) for i in range(BLK)],
                   np.float64)
    fr2 = 1.0 / (10000.0 ** (np.arange(32, dtype=np.float64) / 32))
    ang2 = pos[:, None] * fr2[None, :]
    r2c = np.repeat(np.cos(ang2), 2, -1).T
    r2s = np.repeat(np.sin(ang2), 2, -1).T
    r2s = np.concatenate([r2s[32:64], r2s[0:32]], 0)
    r2c = r2c.astype(ml_dtypes.bfloat16)
    r2s = r2s.astype(ml_dtypes.bfloat16)
    fr3 = 1.0 / (10000.0 ** (np.arange(0, 64, 2, dtype=np.float64) / 64))
    ang3 = pos[:, None] * fr3[None, :]
    r3c = np.cos(ang3).T
    r3s = np.sin(ang3).T
    r3c = np.concatenate([r3c, r3c], 0)
    r3s = np.concatenate([r3s, r3s], 0)
    r3c = r3c.astype(ml_dtypes.bfloat16)
    r3s = r3s.astype(ml_dtypes.bfloat16)
    iL = np.arange(128)[:, None]
    jL = np.arange(128)[None, :]
    e8 = np.zeros((16, 128), ml_dtypes.bfloat16)
    for h in range(16):
        e8[h, 8 * h:8 * h + 8] = 1
    return {
        "c_rope2c": r2c, "c_rope2s": r2s, "c_rope3c": r3c, "c_rope3s": r3s,
        "c_ident": np.eye(128, dtype=ml_dtypes.bfloat16),
        "c_maskL": (iL <= jL).astype(ml_dtypes.bfloat16),
        "c_maskU": (iL > jL).astype(ml_dtypes.bfloat16),
        "c_mask120": (np.arange(128) < 120).astype(np.float32)[:, None],
        "c_ones_col": np.ones((128, 1), ml_dtypes.bfloat16),
        "c_ones_row": np.ones((1, 128), ml_dtypes.bfloat16),
        "c_i16": np.eye(16, dtype=ml_dtypes.bfloat16),
        "c_onesrf": np.ones((1, 128), np.float32),
        "c_e32": np.full((128, 512), 0x3F000000, np.int32),
        "c_mc": np.tile(np.array([[-M_L0F, -M_L0A, -M_L2A, -M_L3A,
                                   1e-5, 1e-10, 1e-11, 2.0]], np.float32),
                        (128, 1)),
    }


DRAM_CONSTS = {"c_rope2c": ([64, T], BF16), "c_rope2s": ([64, T], BF16),
               "c_rope3c": ([64, T], BF16), "c_rope3s": ([64, T], BF16)}
SBUF_CONSTS = {"c_ident": ([128, 128], BF16), "c_e32": ([128, 512], I32),
               "c_maskL": ([128, 128], BF16), "c_maskU": ([128, 128], BF16),
               "c_mask120": ([128, 1], F32),
               "c_ones_col": ([128, 1], BF16), "c_ones_row": ([1, 128], BF16),
               "c_onesrf": ([1, 128], F32), "c_i16": ([16, 16], BF16),
               "c_mc": ([128, 8], F32)}


class Op:
    __slots__ = ("engine", "fn", "deps", "dma", "cc", "inc", "idx", "tag")

    def __init__(self, engine, fn, deps, dma, cc, idx, tag):
        self.engine, self.fn, self.deps = engine, fn, deps
        self.dma, self.cc, self.idx, self.tag = dma, cc, idx, tag
        self.inc = False


RES_ALIAS = {}
MULTI_ALIAS = {"esKH": ("kslot0", "kslot1")}
for _b, _names in {
    "SB16": ["rep_ones", "rep_bc", "blA", "blB", "blC", "gT", "brb", "wrT",
             "fm_y"],
    "SG": ["rp_m"],
    "ES": ["lnb0", "lnb512", "esK", "esV", "esd", "ra_t", "esKC"],
    "PB": ["pb0", "pb1", "rp_t", "rp_u", "rp_w"],
    "ST": ["st_w", "st_c", "st_l", "st_s", "st_m", "br", "gtm", "fm_s",
           "fm_s2", "fm_mu", "fm_va", "fm_m2", "fm_l", "fm_r"],
    "SC_A": ["dq_mf", "dq_fix", "gtge", "gtsu", "fm_q", "fd0", "sgt", "sgt2", "sgt3", "sgt4", "sgt5", "sgt6"],
    "SC_B": ["fd512"],
    "SC_C": ["dq_r", "apl0"],
    "SC_D": ["apl512", "rp_std", "rp_rec", "rp_im"],
    "SC2": ["gbt", "st_gb", "r1A", "r1B", "lrb", "lr32", "gl_t", "dq_sf"],
    "RT": ["RTpo", "RTff", "lnq0", "lnq512"],
    "QB": ["dq_pk", "dq_t0", "dq_t1", "dq_m", "dq_s", "dq_B"],
    "LR": ["lr0"],
}.items():
    for _n in _names:
        RES_ALIAS[_n] = _b


class Emitter:
    ENGINES = ("tensor", "scalar", "vector", "gpsimd", "sync")

    def __init__(self):
        self.ops = []
        self.last_writer = {}
        self.readers = {}

    def op(self, engine, fn, r=(), w=(), dma=False, cc=False, tag=""):
        r = tuple(r) + tuple(RES_ALIAS[x] for x in r if x in RES_ALIAS)
        w = tuple(w) + tuple(RES_ALIAS[x] for x in w if x in RES_ALIAS)
        for x in tuple(r):
            if x in MULTI_ALIAS:
                r = r + MULTI_ALIAS[x]
        for x in tuple(w):
            if x in MULTI_ALIAS:
                w = w + MULTI_ALIAS[x]
        deps = set()
        for x in r:
            lw = self.last_writer.get(x)
            if lw is not None:
                deps.add(lw)
        for x in w:
            lw = self.last_writer.get(x)
            if lw is not None:
                deps.add(lw)
            for rd in self.readers.get(x, ()):
                deps.add(rd)
        i = len(self.ops)
        self.ops.append(Op(engine, fn, deps, dma, cc, i, tag))
        for x in r:
            self.readers.setdefault(x, []).append(i)
        for x in w:
            self.last_writer[x] = i
            self.readers[x] = []
        return i

    def claim(self, *res, tag="claim"):
        self.op("vector", lambda eng: eng.engine_nop(), w=res, tag=tag)

    def emit(self, nc, block):
        NPAR = 12
        ops_by_engine = {e: [o for o in self.ops if o.engine == e]
                         for e in self.ENGINES}
        consumers = {i: [] for i in range(len(self.ops))}
        for o in self.ops:
            for d in o.deps:
                consumers[d].append(o.idx)
        for o in self.ops:
            if o.dma or o.cc:
                o.inc = True
            else:
                for cidx in consumers[o.idx]:
                    c = self.ops[cidx]
                    if (c.engine != o.engine or c.dma or o.dma
                            or o.engine in ("scalar", "vector", "gpsimd")):
                        o.inc = True
                        break

        semkey = {}
        semval = {}
        qcount = {}
        for o in self.ops:
            if not o.inc:
                continue
            if o.dma or o.cc:
                q = o.engine + ("_cc" if o.cc else "_dma")
                n = qcount.get(q, 0)
                qcount[q] = n + 1
                k = f"{q}{n % NPAR}"
                semkey[o.idx] = k
                semval[o.idx] = ((n // NPAR) + 1) * (16 if o.dma else 1)
            else:
                k = o.engine
                n = qcount.get(k, 0) + 1
                qcount[k] = n
                semkey[o.idx] = k
                semval[o.idx] = n
        sems = {}
        for idx, k in semkey.items():
            if k not in sems:
                sems[k] = nc.semaphore("sem_" + k).__enter__()
        emitter = self

        def run_engine(eng, ename):
            waited = {}

            def wait(k, v):
                if waited.get(k, -1) >= v:
                    return
                eng.wait_ge(sems[k], v)
                waited[k] = v

            for o in ops_by_engine[ename]:
                need = {}
                for d in o.deps:
                    p = emitter.ops[d]
                    if (p.engine == ename and not p.dma and not p.cc
                            and not o.dma and ename == "tensor"):
                        continue
                    if not p.inc:
                        raise RuntimeError(f"dep without inc: {p.tag} <- {o.tag}")
                    k, v = semkey[p.idx], semval[p.idx]
                    if v > need.get(k, -1):
                        need[k] = v
                for k, v in sorted(need.items()):
                    wait(k, v)
                if o.fn is None:
                    continue
                if o.inc and (o.dma or o.cc):
                    k, v = semkey[o.idx], semval[o.idx]
                    step = 16 if o.dma else 1
                    if v - step > 0:
                        wait(k, v - step)
                inst = o.fn(eng)
                if o.inc:
                    if inst is None:
                        inst = eng.nop()
                    inst.then_inc(sems[semkey[o.idx]], 16 if o.dma else 1)
            if ename in ("gpsimd", "sync"):
                totals = {}
                for o in ops_by_engine[ename]:
                    if o.inc and (o.dma or o.cc):
                        k, v = semkey[o.idx], semval[o.idx]
                        totals[k] = max(totals.get(k, 0), v)
                for k, v in sorted(totals.items()):
                    wait(k, v)

        for ename, meth in (("sync", block.sync), ("tensor", block.tensor),
                            ("scalar", block.scalar), ("vector", block.vector),
                            ("gpsimd", block.gpsimd)):
            if ops_by_engine[ename]:
                meth(lambda eng, _e=ename: run_engine(eng, _e))


def wspec_all():
    S = {}
    def add(name, rows, cols, kind, **kw):
        S[name] = dict(rows=rows, cols=cols, kind=kind, **kw)
    add("l0f_wq", D, 256, "l0qk", side="q")
    add("l0f_wk", D, 256, "l0qk", side="k")
    add("l0f_wv", D, 256, "cast", src="hs_l0f_wv")
    add("l0f_wp", D, D, "cast", src="L0_ffn_Wp")
    add("l0f_wosl", 256, D, "cast", src="hs_l0f_wosl")
    add("l0a_wq", D, 256, "T", src="hs_l0a_wq")
    add("l0a_wk", D, 256, "T", src="hs_l0a_wk")
    add("l0a_wv", D, 256, "T", src="hs_l0a_wv")
    add("l0a_wosl", 256, D, "cast", src="hs_l0a_wosl")
    add("l1a_wq", D, 256, "T", src="hs_l1a_wq")
    add("l1a_wk", D, 256, "T", src="hs_l1a_wk")
    add("l1a_wv", D, 256, "T", src="hs_l1a_wv")
    add("l1a_wosl", 256, D, "cast", src="hs_l1a_wosl")
    add("l1f_w1", D, 8192, "T", src="L1_ffn_W1")
    add("l1f_w2", 8192, D, "T", src="L1_ffn_W2")
    add("l2a_wq", D, 256, "T", src="hs_l2a_wq")
    add("l2a_wkv", D, 256, "T", src="L2_attn_W_kv")
    add("l2a_wosl", 256, D, "cast_vw", src="hs_l2a_wosl")
    add("l2f_g", 16384, D, "deq_gu", pk="hs_l2f_gpk", sc="hs_l2f_gsc")
    add("l2f_u", 16384, D, "deq_gu", pk="hs_l2f_upk", sc="hs_l2f_usc")
    add("l2f_d", 16384, D, "deq_d", pk="hs_l2f_dpk", sc="hs_l2f_dsc")
    add("l3a_wq", D, 256, "T", src="hs_l3a_wq", scale="rms_w")
    add("l3a_wkv", D, 128, "T", src="hs_l3a_wkv", scale="rms_w")
    add("l3a_wosl", 256, D, "cast", src="hs_l3a_wosl")
    add("l3f_w1", D, 8192, "T3", src="L3_ffn_w1")
    add("l3f_w2", 8192, D, "T3b", src="L3_ffn_w2")
    return S


WSPEC = wspec_all()

SUB_W = {
    "l0f": ["l0f_wq", "l0f_wk", "l0f_wv", "l0f_wp", "l0f_wosl"],
    "l0a": ["l0a_wq", "l0a_wk", "l0a_wv", "l0a_wosl"],
    "l1f": ["l1f_w1", "l1f_w2"],
    "l1a": ["l1a_wq", "l1a_wk", "l1a_wv", "l1a_wosl"],
    "l2f": ["l2f_g", "l2f_u", "l2f_d"],
    "l2a": ["l2a_wq", "l2a_wkv", "l2a_wosl"],
    "l3f": ["l3f_w1", "l3f_w2"],
    "l3a": ["l3a_wq", "l3a_wkv", "l3a_wosl"],
}

SUB_IN = {
    "l0f": ["L0_ffn_Wp", "hs_l0f_wstd", "hs_l0f_wrec", "hs_l0f_waq",
            "hs_l0f_wak", "hs_l0f_waql", "hs_l0f_wakl", "hs_l0f_wv",
            "hs_l0f_wosl"],
    "l0a": ["hs_l0a_wq", "hs_l0a_wk", "hs_l0a_wv", "hs_l0a_wosl",
            "L0_attn_global_tokens", "L0_attn_ln_scale", "L0_attn_ln_bias"],
    "l1f": ["L1_ffn_W1", "L1_ffn_W2", "L1_ffn_Wr", "L1_ffn_br"],
    "l1a": ["hs_l1a_wq", "hs_l1a_wk", "hs_l1a_wv", "hs_l1a_wosl",
            "L1_attn_gamma", "L1_attn_beta"],
    "l2f": ["hs_l2f_gpk", "hs_l2f_gsc", "hs_l2f_upk", "hs_l2f_usc",
            "hs_l2f_dpk", "hs_l2f_dsc"],
    "l2a": ["hs_l2a_wq", "L2_attn_W_kv", "hs_l2a_wkh", "hs_l2a_wvh",
            "hs_l2a_wosl"],
    "l3f": ["L3_ffn_wg", "L3_ffn_w1", "L3_ffn_w2"],
    "l3a": ["hs_l3a_wq", "hs_l3a_wkv", "hs_l3a_wosl", "L3_attn_rms_w"],
}


def make_host_slices(inp, core):
    h0 = 2 * core
    out = {}
    wa = inp["L0_ffn_Wa"]
    out["hs_l0f_waq"] = np.ascontiguousarray(wa[:, 120 * h0:120 * (h0 + 2)])
    out["hs_l0f_wak"] = np.ascontiguousarray(wa[:, 1920 + 120 * h0:1920 + 120 * (h0 + 2)])
    out["hs_l0f_waql"] = np.ascontiguousarray(wa[:, 5888 + 8 * h0:5888 + 8 * (h0 + 2)])
    out["hs_l0f_wakl"] = np.ascontiguousarray(wa[:, 6016 + 8 * h0:6016 + 8 * (h0 + 2)])
    out["hs_l0f_wv"] = np.ascontiguousarray(wa[:, 3840 + 128 * h0:3840 + 128 * (h0 + 2)])
    out["hs_l0f_wstd"] = np.ascontiguousarray(inp["L0_ffn_w_std"][h0:h0 + 2])
    out["hs_l0f_wrec"] = np.ascontiguousarray(inp["L0_ffn_w_rec"][h0:h0 + 2])
    out["hs_l0f_wosl"] = np.ascontiguousarray(inp["L0_ffn_Wp"][128 * h0:128 * (h0 + 2), :])
    for tag, pre in (("l0a", "L0_attn_"), ("l1a", "L1_attn_")):
        out[f"hs_{tag}_wq"] = np.ascontiguousarray(inp[pre + "Wq"][128 * h0:128 * (h0 + 2), :])
        out[f"hs_{tag}_wk"] = np.ascontiguousarray(inp[pre + "Wk"][128 * h0:128 * (h0 + 2), :])
        out[f"hs_{tag}_wv"] = np.ascontiguousarray(inp[pre + "Wv"][128 * h0:128 * (h0 + 2), :])
        out[f"hs_{tag}_wosl"] = np.ascontiguousarray(
            inp[pre + "Wo"][:, 128 * h0:128 * (h0 + 2)].T)
    out["hs_l2a_wq"] = np.ascontiguousarray(inp["L2_attn_W_q"][128 * h0:128 * (h0 + 2), :])
    out["hs_l2a_wkh"] = np.ascontiguousarray(inp["L2_attn_W_k_heads"][h0:h0 + 2])
    out["hs_l2a_wvh"] = np.ascontiguousarray(inp["L2_attn_W_v_heads"][h0:h0 + 2])
    out["hs_l2a_wosl"] = np.ascontiguousarray(
        inp["L2_attn_W_o"][:, 128 * h0:128 * (h0 + 2)].T)
    n0 = 2048 * core
    out["hs_l2f_gpk"] = np.ascontiguousarray(inp["L2_ffn_gate_W_packed"][n0:n0 + 2048, :])
    out["hs_l2f_gsc"] = np.ascontiguousarray(inp["L2_ffn_gate_scales"][n0:n0 + 2048, :])
    out["hs_l2f_upk"] = np.ascontiguousarray(inp["L2_ffn_up_W_packed"][n0:n0 + 2048, :])
    out["hs_l2f_usc"] = np.ascontiguousarray(inp["L2_ffn_up_scales"][n0:n0 + 2048, :])
    out["hs_l2f_dpk"] = np.ascontiguousarray(
        inp["L2_ffn_down_W_packed"][:, 1024 * core:1024 * (core + 1)])
    out["hs_l2f_dsc"] = np.ascontiguousarray(
        inp["L2_ffn_down_scales"][:, 64 * core:64 * (core + 1)])
    win = inp["L3_attn_w_in"]
    g = core
    out["hs_l3a_wq"] = np.ascontiguousarray(win[256 * g:256 * (g + 1), :])
    out["hs_l3a_wkv"] = np.ascontiguousarray(np.concatenate(
        [win[2048 + 64 * g:2048 + 64 * (g + 1), :],
         win[2560 + 64 * g:2560 + 64 * (g + 1), :]], 0))
    out["hs_l3a_wosl"] = np.ascontiguousarray(
        inp["L3_attn_w_out"][:, 256 * g:256 * (g + 1)].T)
    return out


class Ctx:
    def __init__(self, nc, em, subs):
        self.nc, self.em, self.subs = nc, em, subs
        self.inp = {}
        self.wc = {}
        self.sb = {}
        self.ps = {}
        self.dram = {}


def alloc_common(ctx):
    nc = ctx.nc
    A = nc.alloc_sbuf_tensor
    ctx.sb["R"] = A("R", [128, NCH, TPC], F32)
    ctx.sb["XN"] = A("XN", [128, NCH, TPC], BF16)
    ctx.sb["QB"] = A("QB", [128, 16384], BF16)
    ctx.sb["OB"] = A("OB", [128, 16384], BF16)
    ctx.sb["WB"] = A("WB", [128, 8192], BF16)
    ctx.sb["SC"] = A("SC", [128, 2048], F32)
    ctx.sb["SC2"] = A("SC2", [128, 1024], F32)
    ctx.sb["ES"] = A("ES", [128, 1024], BF16)
    ctx.sb["PB"] = A("PB", [128, 1024], BF16)
    ctx.sb["KVA"] = A("KVA", [128, 2384], BF16)
    ctx.sb["RT"] = A("RT", [128, 1024], BF16)
    ctx.sb["DN"] = A("DN", [128, 144], F32)
    ctx.sb["SG"] = A("SG", [128, 48], F32)
    ctx.sb["SB16"] = A("SB16", [128, 256], BF16)
    ctx.sb["ST"] = A("ST", [128, 32], F32)
    ctx.sb["LR"] = A("LR", [64, 512], F32)
    ctx.sb["RP"] = A("RP", [128, 1024], BF16)
    ctx.sb["OF"] = ctx.sb["RP"]
    ctx.sb["LS"] = A("LS", [128, 32], F32)
    for i in range(7):
        ctx.ps[i] = nc.alloc_psum_tensor(f"ps{i}", [128, 512], F32)
    ctx.ps["tr"] = nc.alloc_psum_tensor("ps_tr", [128, 1024], BF16)
    for name, (shape, dt) in SBUF_CONSTS.items():
        ctx.sb[name] = A("s_" + name, shape, dt)
        def f(eng, n=name):
            return eng.dma_start(out=ctx.sb[n][:], in_=ctx.inp[n])
        ctx.em.op("sync", f, w=(name,), dma=True, tag="ldc")


def ln_to_xn(ctx, kind="ln", ln_scale=False, r1_rows=False):
    em = ctx.em
    R, XN, SC, LR = ctx.sb["R"], ctx.sb["XN"], ctx.sb["SC"], ctx.sb["LR"]
    ones_col = ctx.sb["c_ones_col"]
    onesrf = ctx.sb["c_onesrf"]
    vscale = {"ln": 1.0, "ln_ln": 1.0 + 1e-5, "ln_rms": 1.0 + 1e-6}[kind]
    mccol = {"ln": 4, "ln_ln": 5, "ln_rms": 6}[kind]
    for fat in range(2):
        t0 = fat * 512
        for c in range(NCH):
            sl = (c % 2) * 512
            def fcp2(eng, c=c, t0=t0, sl=sl):
                return eng.tensor_copy(out=ctx.sb["ES"][:, sl:sl + 512],
                                       in_=R[:, c, t0:t0 + 512])
            em.op("vector", fcp2, r=("R",), w=(f"lnb{sl}",), tag="ln.cp")
            def fsq(eng, c=c, sl=sl):
                return eng.activation(out=ctx.sb["RT"][:, sl:sl + 512],
                                      in_=ctx.sb["ES"][:, sl:sl + 512],
                                      func=AF.Square)
            em.op("scalar", fsq, r=(f"lnb{sl}",), w=(f"lnq{sl}",), tag="ln.sq")
            def fs1(eng, c=c, sl=sl):
                return eng.matmul(ctx.ps[5][0:1, 0:512], lhsT=ones_col[:],
                                  rhs=ctx.sb["ES"][:, sl:sl + 512],
                                  start=(c == 0), stop=(c == NCH - 1))
            em.op("tensor", fs1, r=(f"lnb{sl}", "c_ones_col"), w=("ps5",),
                  tag="ln.s1")
            def fs2(eng, c=c, sl=sl):
                return eng.matmul(ctx.ps[6][0:1, 0:512], lhsT=ones_col[:],
                                  rhs=ctx.sb["RT"][:, sl:sl + 512],
                                  start=(c == 0), stop=(c == NCH - 1))
            em.op("tensor", fs2, r=(f"lnq{sl}", "c_ones_col"), w=("ps6",),
                  tag="ln.s2")
        def fmu(eng):
            return eng.tensor_scalar(out=ctx.ps[5][0:1, 0:512],
                                     in0=ctx.ps[5][0:1, 0:512],
                                     scalar1=1.0 / D, scalar2=None, op0=ALU.mult)
        em.op("vector", fmu, r=("ps5",), w=("ps5",), tag="ln.mu")
        def fes(eng):
            return eng.tensor_scalar(out=ctx.ps[4][0:1, 0:512],
                                     in0=ctx.ps[6][0:1, 0:512],
                                     scalar1=1.0 / D, scalar2=None, op0=ALU.mult)
        em.op("vector", fes, r=("ps6",), w=("ps4",), tag="ln.es")
        def fcesq(eng):
            return eng.tensor_copy(out=LR[0:1, :], in_=ctx.ps[4][0:1, 0:512])
        em.op("vector", fcesq, r=("ps4",), w=("lr0",), tag="ln.ce")
        def fmu2(eng):
            return eng.activation(out=ctx.ps[6][0:1, 0:512],
                                  in_=ctx.ps[5][0:1, 0:512], func=AF.Square)
        em.op("scalar", fmu2, r=("ps5",), w=("ps6",), tag="ln.mu2")
        def fvar(eng):
            return eng.tensor_tensor(out=ctx.ps[6][0:1, 0:512], in0=LR[0:1, :],
                                     in1=ctx.ps[6][0:1, 0:512], op=ALU.subtract)
        em.op("vector", fvar, r=("lr0", "ps6"), w=("ps6",), tag="ln.var")
        def fln(eng):
            return eng.activation(out=ctx.ps[4][0:1, 0:512],
                                  in_=ctx.ps[6][0:1, 0:512], func=AF.Ln,
                                  bias=ctx.sb["c_mc"][0:1, mccol:mccol + 1],
                                  scale=vscale)
        em.op("scalar", fln, r=("ps6", "c_mc"), w=("ps4",), tag="ln.ln")
        def fexp(eng):
            return eng.activation(out=ctx.ps[3][0:1, 0:512],
                                  in_=ctx.ps[4][0:1, 0:512], func=AF.Exp,
                                  scale=-0.5)
        em.op("scalar", fexp, r=("ps4",), w=("ps3",), tag="ln.exp")
        def fcr(eng):
            return eng.tensor_copy(out=LR[0:1, :], in_=ctx.ps[3][0:1, 0:512])
        em.op("vector", fcr, r=("ps3", "lr0"), w=("lr0",), tag="ln.cr")
        def fbA(eng):
            return eng.matmul(ctx.ps[2][:, 0:512], lhsT=onesrf[:],
                              rhs=LR[0:1, :], start=True, stop=True)
        em.op("tensor", fbA, r=("lr0", "c_onesrf"), w=("ps2",), tag="ln.bA")
        def fcA(eng):
            return eng.tensor_copy(out=SC[:, 0:512], in_=ctx.ps[2][:, 0:512])
        em.op("vector", fcA, r=("ps2",), w=("SC_A",), tag="ln.cA")
        def fs2r(eng):
            return eng.tensor_scalar(out=ctx.sb["SC2"][0:1, 512:1024], in0=ctx.ps[5][0:1, 0:512],
                                     scalar1=-1.0, scalar2=None, op0=ALU.mult)
        em.op("vector", fs2r, r=("ps5",), w=("lr32",), tag="ln.s2a")
        def fs2b(eng):
            return eng.tensor_mul(out=ctx.sb["SC2"][0:1, 512:1024],
                                  in0=ctx.sb["SC2"][0:1, 512:1024],
                                  in1=LR[0:1, :])
        em.op("vector", fs2b, r=("lr32", "lr0"), w=("lr32",), tag="ln.s2b")
        def fbB(eng):
            return eng.matmul(ctx.ps[2][:, 0:512], lhsT=onesrf[:],
                              rhs=ctx.sb["SC2"][0:1, 512:1024], start=True, stop=True)
        em.op("tensor", fbB, r=("lr32", "c_onesrf"), w=("ps2",), tag="ln.bB")
        def fcB(eng):
            return eng.tensor_copy(out=SC[:, 512:1024], in_=ctx.ps[2][:, 0:512])
        em.op("vector", fcB, r=("ps2",), w=("SC_B",), tag="ln.cB")
        if r1_rows:
            def fln1(eng):
                return eng.activation(out=ctx.ps[4][0:1, 0:512],
                                      in_=ctx.ps[6][0:1, 0:512], func=AF.Ln,
                                      bias=ctx.sb["c_mc"][0:1, 4:5])
            em.op("scalar", fln1, r=("ps6", "c_mc"), w=("ps4",), tag="ln.l1")
            def fex1(eng):
                return eng.activation(out=ctx.ps[3][0:1, 0:512],
                                      in_=ctx.ps[4][0:1, 0:512], func=AF.Exp,
                                      scale=-0.5)
            em.op("scalar", fex1, r=("ps4",), w=("ps3",), tag="ln.e1")
            def fp1(eng):
                return eng.tensor_scalar(out=LR[0:1, :], in0=ctx.ps[3][0:1, 0:512],
                                         scalar1=1.0, scalar2=None, op0=ALU.add)
            em.op("vector", fp1, r=("ps3", "lr0"), w=("lr0",), tag="ln.p1")
            def fbA1(eng):
                return eng.matmul(ctx.ps[2][:, 0:512], lhsT=onesrf[:],
                                  rhs=LR[0:1, :], start=True, stop=True)
            em.op("tensor", fbA1, r=("lr0", "c_onesrf"), w=("ps2",), tag="ln.b1")
            def fcA1(eng):
                return eng.tensor_copy(out=ctx.sb["SC2"][:, 0:512],
                                       in_=ctx.ps[2][:, 0:512])
            em.op("vector", fcA1, r=("ps2",), w=("r1A",), tag="ln.c1")
            def fs7a(eng):
                return eng.tensor_scalar(out=ctx.sb["SC2"][0:1, 512:1024],
                                         in0=ctx.ps[5][0:1, 0:512],
                                         scalar1=-1.0, scalar2=None, op0=ALU.mult)
            em.op("vector", fs7a, r=("ps5", "lr32"), w=("lr32",), tag="ln.s7a")
            def fs7b(eng):
                return eng.tensor_tensor(out=ctx.sb["SC2"][0:1, 512:1024],
                                         in0=ctx.sb["SC2"][0:1, 512:1024],
                                         in1=ctx.ps[3][0:1, 0:512], op=ALU.mult)
            em.op("vector", fs7b, r=("lr32", "ps3"), w=("lr32",), tag="ln.s7b")
            def fbB1(eng):
                return eng.matmul(ctx.ps[2][:, 0:512], lhsT=onesrf[:],
                                  rhs=ctx.sb["SC2"][0:1, 512:1024], start=True, stop=True)
            em.op("tensor", fbB1, r=("lr32", "c_onesrf"), w=("ps2",), tag="ln.b2")
            def fcB1(eng):
                return eng.tensor_copy(out=ctx.sb["SC2"][:, 512:1024],
                                       in_=ctx.ps[2][:, 0:512])
            em.op("vector", fcB1, r=("ps2",), w=("r1B",), tag="ln.c2")
        for c in range(NCH):
            ename = "vector" if c % 2 == 0 else "gpsimd"
            slot = 0 if ename == "vector" else 512
            def fa1(eng, c=c, t0=t0, slot=slot):
                return eng.tensor_mul(out=ctx.sb["SC"][:, 1024 + slot:1536 + slot],
                                      in0=R[:, c, t0:t0 + 512],
                                      in1=SC[:, 0:512])
            em.op(ename, fa1, r=("R", "SC_A"), w=(f"apl{slot}",), tag="ln.a1")
            if not ln_scale:
                def fa2(eng, c=c, t0=t0, slot=slot):
                    return eng.tensor_add(out=XN[:, c, t0:t0 + 512],
                                          in0=ctx.sb["SC"][:, 1024 + slot:1536 + slot],
                                          in1=SC[:, 512:1024])
                em.op(ename, fa2, r=(f"apl{slot}", "SC_B"), w=("XN",), tag="ln.a2")
            else:
                def fa2(eng, c=c, t0=t0, slot=slot):
                    return eng.tensor_add(out=ctx.sb["SC"][:, 1024 + slot:1536 + slot],
                                          in0=ctx.sb["SC"][:, 1024 + slot:1536 + slot],
                                          in1=SC[:, 512:1024])
                em.op(ename, fa2, r=(f"apl{slot}", "SC_B"), w=(f"apl{slot}",),
                      tag="ln.a2s")
                def fa3(eng, c=c, t0=t0, slot=slot):
                    return eng.tensor_scalar(out=XN[:, c, t0:t0 + 512],
                                             in0=ctx.sb["SC"][:, 1024 + slot:1536 + slot],
                                             scalar1=ctx.sb["LS"][:, c:c + 1],
                                             scalar2=ctx.sb["LS"][:, 16 + c:17 + c],
                                             op0=ALU.mult, op1=ALU.add)
                em.op(ename, fa3, r=(f"apl{slot}", "LS"), w=("XN",), tag="ln.a3")
        if r1_rows:
            for c in range(NCH):
                ename = "vector" if c % 2 == 0 else "gpsimd"
                def fu1(eng, c=c, t0=t0):
                    return eng.tensor_mul(out=R[:, c, t0:t0 + 512],
                                          in0=R[:, c, t0:t0 + 512],
                                          in1=ctx.sb["SC2"][:, 0:512])
                em.op(ename, fu1, r=("R", "r1A", "XN"), w=("R",), tag="ru.1")
                def fu2(eng, c=c, t0=t0):
                    return eng.tensor_add(out=R[:, c, t0:t0 + 512],
                                          in0=R[:, c, t0:t0 + 512],
                                          in1=ctx.sb["SC2"][:, 512:1024])
                em.op(ename, fu2, r=("R", "r1B"), w=("R",), tag="ru.2")


def add_tokmajor_dram_to_R(ctx, dram_ap, scale, dt_=F32, extra_r=()):
    em = ctx.em
    R = ctx.sb["R"]
    for ts_ in range(8):
        for g0 in range(0, 16, 8):
            def fl(eng, ts_=ts_, g0=g0):
                return eng.dma_start(out=ctx.sb["RT"][:, 0:1024],
                                     in_=dram_ap[ts_ * 128:(ts_ + 1) * 128,
                                                 g0 * 128:(g0 + 8) * 128])
            em.op("gpsimd", fl, r=("RT",) + extra_r, w=("RT",), dma=True,
                  tag="ra.ld")
            for j in range(8):
                def ft(eng, j=j):
                    return eng.transpose(ctx.ps["tr"][:, j * 128:(j + 1) * 128],
                                         ctx.sb["RT"][:, j * 128:(j + 1) * 128],
                                         ctx.sb["c_ident"][:])
                em.op("tensor", ft, r=("RT", "c_ident"), w=("ps_tr",), tag="ra.t")
            for j in range(8):
                c = g0 + j
                fat, t0 = (0, 0) if ts_ < 4 else (1, 512)
                tt0 = (ts_ % 4) * 128
                if scale == 1.0:
                    def fa(eng, c=c, t0=t0, tt0=tt0, j=j):
                        return eng.tensor_tensor(
                            out=R[:, c, t0 + tt0:t0 + tt0 + 128],
                            in0=ctx.ps["tr"][:, j * 128:(j + 1) * 128],
                            in1=R[:, c, t0 + tt0:t0 + tt0 + 128], op=ALU.add)
                    em.op("vector", fa, r=("ps_tr", "R"), w=("R",), tag="ra.a")
                else:
                    def fa0(eng, j=j):
                        return eng.tensor_scalar(
                            out=ctx.sb["ES"][:, 0:128],
                            in0=ctx.ps["tr"][:, j * 128:(j + 1) * 128],
                            scalar1=scale, scalar2=None, op0=ALU.mult)
                    em.op("vector", fa0, r=("ps_tr",), w=("ra_t",), tag="ra.s")
                    def fa(eng, c=c, t0=t0, tt0=tt0):
                        return eng.tensor_tensor(
                            out=R[:, c, t0 + tt0:t0 + tt0 + 128],
                            in0=ctx.sb["ES"][:, 0:128],
                            in1=R[:, c, t0 + tt0:t0 + tt0 + 128], op=ALU.add)
                    em.op("vector", fa, r=("ra_t", "R"), w=("R",), tag="ra.a")




def _emit_T_block(ctx, name, src_block, dst, dst_r0, dst_c0, n_in, n_out,
                  scale=None):
    em = ctx.em
    OBf = ctx.sb["OB"]
    for rc in range(n_in // 128):
        def fl(eng, rc=rc):
            return eng.dma_start(
                out=ctx.sb["WB"][:, 0:n_out].rearrange("p (c f) -> p c f", f=128),
                in_=src_block[:, rc * 128:(rc + 1) * 128]
                    .rearrange("(c p) f -> p c f", p=128))
        em.op("gpsimd", fl, w=("WB",), dma=True, tag="pT.ld")
        for fc0 in range(0, n_out // 128, 8):
            g = min(8, n_out // 128 - fc0)
            for j in range(g):
                def ft(eng, fc=fc0 + j, j=j):
                    return eng.transpose(
                        ctx.ps["tr"][:, j * 128:(j + 1) * 128],
                        ctx.sb["WB"][:, fc * 128:(fc + 1) * 128],
                        ctx.sb["c_ident"][:])
                em.op("tensor", ft, r=("WB", "c_ident"), w=("ps_tr",), tag="pT.t")
            def fcp(eng, fc0=fc0, g=g):
                return eng.tensor_copy(out=OBf[:, fc0 * 128:(fc0 + g) * 128],
                                       in_=ctx.ps["tr"][:, 0:g * 128])
            em.op("vector", fcp, r=("ps_tr",), w=("OB_P",), tag="pT.cp")
        if scale is not None:
            sc_ap, sc_res = scale(dst_r0 // 128 + rc)
            def fsc(eng, sc_ap=sc_ap):
                return eng.tensor_scalar(out=OBf[:, 0:n_out], in0=OBf[:, 0:n_out],
                                         scalar1=sc_ap, scalar2=None, op0=ALU.mult)
            em.op("vector", fsc, r=("OB_P", sc_res), w=("OB_P",), tag="pT.sc")
        def fst(eng, rc=rc):
            return eng.dma_start(
                out=dst[dst_r0 + rc * 128:dst_r0 + (rc + 1) * 128,
                        dst_c0:dst_c0 + n_out],
                in_=OBf[:, 0:n_out])
        em.op("sync", fst, r=("OB_P",), w=(("wc", name),), dma=True, tag="pT.st")


def nib_decode(ctx, b_res, plane, hw):
    em = ctx.em
    w = 512
    QBi = ctx.sb["QB"][:].bitcast(I32)
    OBv = ctx.sb["OB"][:].bitcast(F32)
    SC, SC2 = ctx.sb["SC"], ctx.sb["SC2"]
    o0 = hw * 1024
    out_ap = OBv[:, o0:o0 + 1024].rearrange("p (a b2) -> p a b2", b2=2)[:, :, plane:plane + 1]
    if plane == 1:
        def fsh(eng, hw=hw):
            return eng.tensor_scalar(out=QBi[:, 3072 + hw * w:3072 + (hw + 1) * w],
                                     in0=QBi[:, hw * w:(hw + 1) * w],
                                     scalar1=4, scalar2=None,
                                     op0=ALU.logical_shift_right)
        em.op("vector", fsh, r=(b_res,), w=("dq_B",), tag="dq.sh")
        B, Bres = QBi[:, 3072 + hw * w:3072 + (hw + 1) * w], "dq_B"
    else:
        B, Bres = QBi[:, hw * w:(hw + 1) * w], b_res
    t0 = QBi[:, 1024 + hw * w:1024 + (hw + 1) * w]
    t1 = QBi[:, 2048 + hw * w:2048 + (hw + 1) * w]
    def f0(eng):
        return eng.tensor_scalar(out=t0, in0=B, scalar1=7, scalar2=22,
                                 op0=ALU.bitwise_and, op1=ALU.logical_shift_left)
    em.op("vector", f0, r=(Bres,), w=("dq_t0",), tag="dq.0")
    def f1(eng):
        return eng.tensor_scalar(out=t1, in0=B, scalar1=8, scalar2=28,
                                 op0=ALU.bitwise_and, op1=ALU.logical_shift_left)
    em.op("vector", f1, r=(Bres,), w=("dq_t1",), tag="dq.1")
    def f2(eng):
        return eng.tensor_tensor(out=t0, in0=t0, in1=t1, op=ALU.add)
    em.op("vector", f2, r=("dq_t0", "dq_t1"), w=("dq_t0",), tag="dq.2")
    def f3(eng):
        return eng.tensor_tensor(out=t0, in0=t0, in1=ctx.sb["c_e32"][:],
                                 op=ALU.add)
    em.op("vector", f3, r=("dq_t0", "c_e32"), w=("dq_t0",), tag="dq.3")
    def f4(eng):
        return eng.tensor_scalar(out=t1, in0=B, scalar1=7, scalar2=None,
                                 op0=ALU.bitwise_and)
    em.op("vector", f4, r=(Bres, "dq_t1"), w=("dq_m",), tag="dq.4")
    def f5(eng):
        return eng.copy(out=SC[:, 0:w], in_=t1)
    em.op("scalar", f5, r=("dq_m",), w=("dq_mf",), tag="dq.5")
    def f6(eng):
        return eng.activation(out=SC[:, 1024:1024 + w], in_=SC[:, 0:w],
                              func=AF.Relu, scale=-1.0,
                              bias=ctx.sb["c_mc"][:, 7:8])
    em.op("scalar", f6, r=("dq_mf", "c_mc"), w=("dq_r",), tag="dq.6")
    def f7(eng):
        return eng.tensor_scalar(out=t1, in0=B, scalar1=8, scalar2=None,
                                 op0=ALU.bitwise_and)
    em.op("vector", f7, r=(Bres, "dq_mf"), w=("dq_s",), tag="dq.7")
    def f8(eng):
        return eng.copy(out=SC2[:, 0:w], in_=t1)
    em.op("scalar", f8, r=("dq_s",), w=("dq_sf",), tag="dq.8")
    def f9(eng):
        return eng.tensor_scalar(out=SC2[:, 0:w], in0=SC2[:, 0:w],
                                 scalar1=1.0 / 16, scalar2=-0.25,
                                 op0=ALU.mult, op1=ALU.add)
    em.op("vector", f9, r=("dq_sf",), w=("dq_sf",), tag="dq.9")
    def f10(eng):
        return eng.tensor_mul(out=SC[:, 0:w], in0=SC[:, 1024:1024 + w],
                              in1=SC2[:, 0:w])
    em.op("vector", f10, r=("dq_r", "dq_sf"), w=("dq_fix",), tag="dq.10")
    def f11(eng):
        return eng.tensor_tensor(out=out_ap, in0=t0.bitcast(F32).unsqueeze(2),
                                 in1=SC[:, 0:w].unsqueeze(2), op=ALU.add)
    em.op("vector", f11, r=("dq_t0", "dq_fix"), w=("OB_P",), tag="dq.11")


def prep_matrix(ctx, name):
    em = ctx.em
    spec = WSPEC[name]
    kind = spec["kind"]
    rows, cols = spec["rows"], spec["cols"]
    dst = ctx.wc[name]
    if kind in ("cast", "cast_vw"):
        src = ctx.inp[spec["src"]]
        def f(eng):
            return eng.dma_start(out=dst[:, :], in_=src)
        em.op("gpsimd", f, w=(("wc", name),), dma=True, tag="cast." + name)
        if kind == "cast_vw":
            for hh in range(2):
                def fl(eng, hh=hh):
                    return eng.dma_start(out=ctx.sb["WB"][:, 0:D],
                                         in_=dst[hh * 128:(hh + 1) * 128, :])
                em.op("sync", fl, r=(("wc", name),), w=("WB",), dma=True,
                      tag="vw.ld")
                def fs(eng, hh=hh):
                    return eng.tensor_scalar(out=ctx.sb["WB"][:, 0:D],
                                             in0=ctx.sb["WB"][:, 0:D],
                                             scalar1=ctx.sb["SG"][:, 4 + hh:5 + hh],
                                             scalar2=None, op0=ALU.mult)
                em.op("vector", fs, r=("WB", "SG"), w=("WB",), tag="vw.sc")
                def fst(eng, hh=hh):
                    return eng.dma_start(out=dst[hh * 128:(hh + 1) * 128, :],
                                         in_=ctx.sb["WB"][:, 0:D])
                em.op("sync", fst, r=("WB",), w=(("wc", name),), dma=True,
                      tag="vw.st")
        return
    if kind == "T":
        scale = None
        if spec.get("scale") == "rms_w":
            scale = lambda rc: (ctx.sb["SG"][:, 16 + rc:17 + rc], "SG")
        _emit_T_block(ctx, name, ctx.inp[spec["src"]], dst, 0, 0, rows, cols,
                      scale)
        return
    if kind == "T3":
        src = ctx.inp[spec["src"]].rearrange("e h d -> (e h) d")
        _emit_T_block(ctx, name, src, dst, 0, 0, rows, cols)
        return
    if kind == "T3b":
        src = ctx.inp[spec["src"]].rearrange("e o h -> (e o) h")
        for e in range(8):
            _emit_T_block(ctx, name, src[e * 2048:(e + 1) * 2048, :], dst,
                          e * 1024, 0, 1024, 2048)
        return
    if kind == "l0qk":
        side = spec["side"]
        std_src = ctx.inp["hs_l0f_waq" if side == "q" else "hs_l0f_wak"]
        ta = ctx.inp["hs_l0f_waql" if side == "q" else "hs_l0f_wakl"]
        tb = ctx.inp["hs_l0f_wakl" if side == "q" else "hs_l0f_waql"]
        for rc in range(16):
            for hh in range(2):
                def fs(eng, rc=rc, hh=hh):
                    return eng.dma_start(
                        out=dst[rc * 128:(rc + 1) * 128,
                                hh * 128:hh * 128 + 120],
                        in_=std_src[rc * 128:(rc + 1) * 128,
                                    120 * hh:120 * hh + 120])
                em.op("gpsimd", fs, w=(("wc", name),), dma=True, tag="l0.std")
            def fta(eng, rc=rc):
                return eng.dma_start(out=ctx.sb["SB16"][:, 0:16],
                                     in_=ta[rc * 128:(rc + 1) * 128, :])
            em.op("gpsimd", fta, w=("blA",), dma=True, tag="l0.lda")
            def ftb(eng, rc=rc):
                return eng.dma_start(out=ctx.sb["SB16"][:, 16:32],
                                     in_=tb[rc * 128:(rc + 1) * 128, :])
            em.op("gpsimd", ftb, w=("blB",), dma=True, tag="l0.ldb")
            def fb1(eng):
                return eng.tensor_sub(out=ctx.sb["SB16"][:, 32:48],
                                      in0=ctx.sb["SB16"][:, 16:32],
                                      in1=ctx.sb["SB16"][:, 0:16])
            em.op("vector", fb1, r=("blA", "blB"), w=("blC",), tag="l0.sub")
            for hh in range(2):
                def fb2(eng, hh=hh):
                    return eng.tensor_scalar(
                        out=ctx.sb["SB16"][:, 32 + 8 * hh:40 + 8 * hh],
                        in0=ctx.sb["SB16"][:, 32 + 8 * hh:40 + 8 * hh],
                        scalar1=ctx.sb["SG"][:, 8 + hh:9 + hh],
                        scalar2=None, op0=ALU.mult)
                em.op("vector", fb2, r=("blC", "SG"), w=("blC",), tag="l0.m")
            def fb3(eng):
                return eng.tensor_add(out=ctx.sb["SB16"][:, 32:48],
                                      in0=ctx.sb["SB16"][:, 32:48],
                                      in1=ctx.sb["SB16"][:, 0:16])
            em.op("vector", fb3, r=("blC", "blA"), w=("blC",), tag="l0.add")
            for hh in range(2):
                def fst(eng, rc=rc, hh=hh):
                    return eng.dma_start(
                        out=dst[rc * 128:(rc + 1) * 128,
                                hh * 128 + 120:hh * 128 + 128],
                        in_=ctx.sb["SB16"][:, 32 + 8 * hh:40 + 8 * hh])
                em.op("sync", fst, r=("blC",), w=(("wc", name),), dma=True,
                      tag="l0.st")
        return
    if kind in ("deq_gu", "deq_d"):
        pk = ctx.inp[spec["pk"]]
        sc = ctx.inp[spec["sc"]]
        gin = ctx.dram["gin_" + name]
        QBi = ctx.sb["QB"][:].bitcast(I32)
        OBv = ctx.sb["OB"][:].bitcast(F32)
        for c in range(16):
            def fl(eng, c=c):
                return eng.dma_start(out=QBi[:, 0:1024],
                                     in_=pk[c * 128:(c + 1) * 128, :])
            em.op("sync", fl, r=("dq_t0", "dq_m", "dq_s", "dq_B"), w=("dq_pk",),
                  dma=True, tag="dq.ldpk")
            def fls(eng, c=c):
                return eng.dma_start(out=ctx.sb["DN"][:, 0:64],
                                     in_=sc[c * 128:(c + 1) * 128, :])
            em.op("sync", fls, r=("dq_sc",), w=("dq_sc",), dma=True,
                  tag="dq.ldsc")
            for hw in range(2):
                nib_decode(ctx, "dq_pk", 0, hw)
                nib_decode(ctx, "dq_pk", 1, hw)
            for b0 in range(0, 64, 8):
                def fsc(eng, b0=b0):
                    for b in range(b0, b0 + 8):
                        inst = eng.tensor_scalar(
                            out=OBv[:, 32 * b:32 * (b + 1)],
                            in0=OBv[:, 32 * b:32 * (b + 1)],
                            scalar1=ctx.sb["DN"][:, b:b + 1], scalar2=None,
                            op0=ALU.mult)
                    return inst
                em.op("vector", fsc, r=("OB_P", "dq_sc"), w=("OB_P",),
                      tag="dq.sc")
            def fcst(eng):
                return eng.tensor_copy(out=ctx.sb["WB"][:, 0:2048],
                                       in_=OBv[:, 0:2048])
            em.op("vector", fcst, r=("OB_P",), w=("WB",), tag="dq.cst")
            for j0 in range(0, 16, 8):
                for j in range(8):
                    def ft(eng, j0=j0, j=j):
                        return eng.transpose(
                            ctx.ps["tr"][:, j * 128:(j + 1) * 128],
                            ctx.sb["WB"][:, (j0 + j) * 128:(j0 + j + 1) * 128],
                            ctx.sb["c_ident"][:])
                    em.op("tensor", ft, r=("WB", "c_ident"), w=("ps_tr",),
                          tag="dq.t")
                def fcp(eng, j0=j0):
                    return eng.tensor_copy(
                        out=ctx.sb["WB"][:, 2048 + j0 * 128:2048 + (j0 + 8) * 128],
                        in_=ctx.ps["tr"][:, 0:1024])
                em.op("vector", fcp, r=("ps_tr",), w=("WB2",), tag="dq.cp")
            def fst(eng, c=c):
                return eng.dma_start(
                    out=gin[:, c * 128:(c + 1) * 128]
                        .rearrange("(a p) f -> p a f", p=128),
                    in_=ctx.sb["WB"][:, 2048:4096]
                        .rearrange("p (a f) -> p a f", f=128))
            em.op("sync", fst, r=("WB2",), w=(("gin", name),), dma=True,
                  tag="dq.st")
        def fcc(eng):
            return eng.collective_compute(
                "AllGather", ALU.bypass, replica_groups=[list(range(NCORE))],
                ins=[gin.opt()], outs=[ctx.wc[name].opt()])
        em.op("gpsimd", fcc, r=(("gin", name),), w=(("wc", name),), cc=True,
              tag="dq.ag")
        return
    raise NotImplementedError(kind)


def prep_tables(ctx, subs):
    em = ctx.em
    ST, SG = ctx.sb["ST"], ctx.sb["SG"]
    bases = {BASE.get(s, s) for s in subs}
    if "l0f" in bases:
        def f0(eng):
            return eng.dma_start(out=ctx.sb["SC"][:, 0:2],
                                 in_=ctx.inp["hs_l0f_wstd"].unsqueeze(0)
                                 .broadcast_to((128, 2)))
        em.op("gpsimd", f0, r=("SC_A",), w=("sgt",), dma=True, tag="sg.ld")
        def f0b(eng):
            return eng.dma_start(out=ctx.sb["SC"][:, 2:4],
                                 in_=ctx.inp["hs_l0f_wrec"].unsqueeze(0)
                                 .broadcast_to((128, 2)))
        em.op("gpsimd", f0b, r=("SC_A",), w=("sgt",), dma=True, tag="sg.ld2")
        def f1(eng):
            return eng.tensor_scalar(out=ctx.sb["SC"][:, 4:8],
                                     in0=ctx.sb["SC"][:, 0:4],
                                     scalar1=1e-8, scalar2=None, op0=ALU.max)
        em.op("vector", f1, r=("sgt",), w=("sgt2",), tag="sg.clip")
        def f2(eng):
            return eng.activation(out=ctx.sb["SC"][:, 8:12],
                                  in_=ctx.sb["SC"][:, 4:8], func=AF.Ln)
        em.op("scalar", f2, r=("sgt2",), w=("sgt3",), tag="sg.ln")
        def f3(eng):
            return eng.activation(out=ctx.sb["SC"][:, 12:16],
                                  in_=ctx.sb["SC"][:, 8:12], func=AF.Exp,
                                  scale=0.5)
        em.op("scalar", f3, r=("sgt3",), w=("sgt4",), tag="sg.exp")
        def f4(eng):
            return eng.tensor_scalar(out=SG[:, 8:10], in0=ctx.sb["SC"][:, 2:4],
                                     scalar1=0.1, scalar2=None, op0=ALU.is_gt)
        em.op("vector", f4, r=("sgt",), w=("SG",), tag="sg.isgt")
        def f5(eng):
            return eng.tensor_scalar(out=SG[:, 0:2], in0=ctx.sb["SC"][:, 12:14],
                                     scalar1=ctx.sb["c_mask120"][:],
                                     scalar2=None, op0=ALU.mult)
        em.op("vector", f5, r=("sgt4", "c_mask120"), w=("SG",), tag="sg.a")
        def f6(eng):
            return eng.tensor_scalar(out=ctx.sb["SC"][:, 16:18],
                                     in0=ctx.sb["SC"][:, 14:16],
                                     scalar1=ctx.sb["c_mask120"][:],
                                     scalar2=None, op0=ALU.mult)
        em.op("vector", f6, r=("sgt4", "c_mask120"), w=("sgt5",), tag="sg.b")
        def f7(eng):
            return eng.tensor_sub(out=ctx.sb["SC"][:, 18:20],
                                  in0=ctx.sb["SC"][:, 14:16],
                                  in1=ctx.sb["SC"][:, 16:18])
        em.op("vector", f7, r=("sgt4", "sgt5"), w=("sgt6",), tag="sg.c")
        def f8(eng):
            return eng.tensor_add(out=SG[:, 0:2], in0=SG[:, 0:2],
                                  in1=ctx.sb["SC"][:, 18:20])
        em.op("vector", f8, r=("SG", "sgt6"), w=("SG",), tag="sg.d")
    if "l2a" in bases:
        for hh in range(2):
            for which, col, srcn in (("k", 2, "hs_l2a_wkh"), ("v", 4, "hs_l2a_wvh")):
                def fl(eng, hh=hh, srcn=srcn):
                    return eng.dma_start(out=ctx.sb["SC"][:, 0:128],
                                         in_=ctx.inp[srcn][hh])
                em.op("sync", fl, r=("SC_A",), w=("SC_A",), dma=True, tag="kv.ld")
                def fr(eng, hh=hh, col=col):
                    return eng.reduce_sum(out=SG[:, col + hh:col + hh + 1],
                                          in_=ctx.sb["SC"][:, 0:128], axis=AXX)
                em.op("vector", fr, r=("SC_A",), w=("SG",), tag="kv.red")
    if "l3a" in bases:
        def fl(eng):
            return eng.dma_start(out=ctx.sb["SG"][:, 16:32],
                                 in_=ctx.inp["L3_attn_rms_w"]
                                 .rearrange("(c p) -> p c", p=128),
                                 allow_slow_non_contiguous=True)
        em.op("sync", fl, w=("SG",), dma=True, tag="rw.ld")
    if "l0a" in bases:
        def fl(eng):
            return eng.dma_start(out=ctx.sb["LS"][:, 0:16],
                                 in_=ctx.inp["L0_attn_ln_scale"]
                                 .rearrange("(c p) -> p c", p=128),
                                 allow_slow_non_contiguous=True)
        em.op("sync", fl, w=("LS",), dma=True, tag="ls.ld")
        def fb(eng):
            return eng.dma_start(out=ctx.sb["LS"][:, 16:32],
                                 in_=ctx.inp["L0_attn_ln_bias"]
                                 .rearrange("(c p) -> p c", p=128),
                                 allow_slow_non_contiguous=True)
        em.op("sync", fb, w=("LS",), dma=True, tag="ls.ld2")


def rope_apply(ctx, ap, res, kind, tagp=""):
    em = ctx.em
    RP, PB = ctx.sb["RP"], ctx.sb["PB"]
    groups = [(0,)] if kind in ("l2", "l3k") else [(0,), (64,)]
    for (r0,) in groups:
        b1, b2 = r0, r0 + 32
        x1 = ap[b1:b1 + 32, :]
        x2 = ap[b2:b2 + 32, :]
        c1 = RP[b1:b1 + 32, 0:512]
        c2 = RP[b2:b2 + 32, 0:512]
        sx1 = RP[b1:b1 + 32, 512:1024]
        sx2 = RP[b2:b2 + 32, 512:1024]
        t_ = PB[b1:b1 + 32, 0:512]
        u_ = PB[b1:b1 + 32, 512:1024]
        w_ = PB[b2:b2 + 32, 0:512]
        def f0(eng, x1=x1, t_=t_):
            return eng.tensor_copy(out=t_, in_=x1)
        em.op("vector", f0, r=(res,), w=("rp_t",), tag=tagp + "rp.cp")
        def f1(eng, x2=x2, sx2=sx2, u_=u_):
            return eng.tensor_mul(out=u_, in0=x2, in1=sx2)
        em.op("vector", f1, r=(res, "RP", "rp_t"), w=("rp_u",), tag=tagp + "rp.u")
        def f2(eng, x1=x1, c1=c1):
            return eng.tensor_mul(out=x1, in0=x1, in1=c1)
        em.op("vector", f2, r=(res, "RP", "rp_t"), w=(res,), tag=tagp + "rp.m1")
        def f3(eng, x1=x1, u_=u_):
            return eng.tensor_sub(out=x1, in0=x1, in1=u_)
        em.op("vector", f3, r=(res, "rp_u"), w=(res,), tag=tagp + "rp.s1")
        def f4(eng, x2=x2, c2=c2):
            return eng.tensor_mul(out=x2, in0=x2, in1=c2)
        em.op("gpsimd", f4, r=(res, "RP", "rp_u"), w=(res,), tag=tagp + "rp.m2")
        def f5(eng, t_=t_, sx1=sx1, w_=w_):
            return eng.tensor_mul(out=w_, in0=t_, in1=sx1)
        em.op("gpsimd", f5, r=("rp_t", "RP"), w=("rp_w",), tag=tagp + "rp.m3")
        def f6(eng, x2=x2, w_=w_):
            return eng.tensor_add(out=x2, in0=x2, in1=w_)
        em.op("gpsimd", f6, r=(res, "rp_w"), w=(res,), tag=tagp + "rp.a2")


def attn_site(ctx, site):
    em = ctx.em
    base = BASE.get(site, site)
    QB, OB, WB, ES, OF = (ctx.sb["QB"], ctx.sb["OB"], ctx.sb["WB"],
                          ctx.sb["ES"], ctx.sb["OF"])
    OBb = OB[:]
    cfgs = {
        "l0f": dict(hl=2, dh=128, dv=128, M=M_L0F, mcc=0, scale=128 ** -0.5,
                    wq="l0f_wq", wk="l0f_wk", wv="l0f_wv", wo="l0f_wosl",
                    ln="ln", rope=None, out_scale=0.5, add_xn=False),
        "l0a": dict(hl=2, dh=128, dv=128, M=M_L0A, mcc=1, scale=128 ** -0.5,
                    wq="l0a_wq", wk="l0a_wk", wv="l0a_wv", wo="l0a_wosl",
                    ln="ln_ln", rope=None, band=256, use_global=True,
                    out_scale=1.0, add_xn=True),
        "l1a": dict(hl=2, dh=128, dv=128, M=None, scale=128 ** -0.5,
                    wq="l1a_wq", wk="l1a_wk", wv="l1a_wv", wo="l1a_wosl",
                    ln="ln", rope=None, fmap=True, denom_eps=1e-5,
                    out_scale=1.0, add_xn=False),
        "l2a": dict(hl=2, dh=128, dv=128, M=M_L2A, mcc=2, scale=128 ** -0.5,
                    wq="l2a_wq", wk=None, wv=None, wo="l2a_wosl",
                    ln="ln", rope="l2", kv_shared=True, out_scale=1.0,
                    add_xn=False),
        "l3a": dict(hl=4, dh=64, dv=64, M=M_L3A, mcc=3, scale=64 ** -0.5,
                    wq="l3a_wq", wk="l3a_wkv", wv=None, wo="l3a_wosl",
                    ln="ln_rms", rope="l3", l3=True, out_scale=1.0,
                    add_xn=True),
    }
    cfg = cfgs[base]
    hl, dh, dv = cfg["hl"], cfg["dh"], cfg["dv"]

    def dbg_sb(name, ap, res):
        if name in TAPS and ("dbg_" + name) in ctx.dram:
            def f(eng):
                return eng.dma_start(out=ctx.dram["dbg_" + name], in_=ap)
            em.op("gpsimd", f, r=(res,), w=(("dbg", name),), dma=True,
                  tag="dbg." + name)

    def dbg_dram(name, src_ap, res):
        if name in TAPS and ("dbg_" + name) in ctx.dram:
            def f(eng):
                return eng.dma_start(out=ctx.dram["dbg_" + name], in_=src_ap)
            em.op("gpsimd", f, r=(res,), w=(("dbg", name),), dma=True,
                  tag="dbg." + name)

    ln_to_xn(ctx, kind=cfg["ln"], ln_scale=(base == "l0a"),
             r1_rows=cfg["add_xn"])
    dbg_sb("xn", ctx.sb["XN"][:], "XN")
    if base == "l1a":
        build_gb_tiles(ctx)
    xin, xg = ctx.dram["xg_in"], ctx.dram["xg"]
    def fxo(eng):
        return eng.dma_start(out=xin[:, :].rearrange("(c p) t -> p c t", p=128),
                             in_=ctx.sb["XN"][:])
    em.op("gpsimd", fxo, r=("XN",), w=("xg_in",), dma=True, tag="at.xo")
    def fcc(eng):
        return eng.collective_compute(
            "AllGather", ALU.bypass, replica_groups=[list(range(NCORE))],
            ins=[xin.opt()], outs=[xg.opt()])
    em.op("gpsimd", fcc, r=("xg_in",), w=("xg",), cc=True, tag="at.ag")
    em.claim("OB_P", "OB_A", "OB_W", "OB_Q", "OB_K", tag="at.clob")
    def ldw(eng, name, c0, width):
        return eng.dma_start(
            out=OBb[:, c0:c0 + 16 * width].rearrange("p (c f) -> p c f", f=width),
            in_=ctx.wc[name][:, :].rearrange("(c p) f -> p c f", p=128))
    def fwo(eng):
        return eng.dma_start(
            out=OBb[:, 4096:8192].rearrange("p (c f) -> p c f", f=2048),
            in_=ctx.wc[cfg["wo"]][:, :].rearrange("(c p) f -> p c f", p=128))
    em.op("sync", fwo, r=(("wc", cfg["wo"]),), w=("OB_W",), dma=True, tag="at.lwo")
    def fwq(eng):
        return eng.dma_start(
            out=OBb[:, 8192:12288].rearrange("p (c f) -> p c f", f=256),
            in_=ctx.wc[cfg["wq"]][:, :].rearrange("(c p) f -> p c f", p=128))
    em.op("sync", fwq, r=(("wc", cfg["wq"]),), w=("OB_Q",), dma=True, tag="at.lwq")
    if base == "l3a":
        def fwk(eng):
            return eng.dma_start(
                out=OBb[:, 12288:14336].rearrange("p (c f) -> p c f", f=128),
                in_=ctx.wc["l3a_wkv"][:, :].rearrange("(c p) f -> p c f", p=128))
        em.op("sync", fwk, r=(("wc", "l3a_wkv"),), w=("OB_K",), dma=True,
              tag="at.lwk")
    elif base == "l2a":
        def fwk(eng):
            return eng.dma_start(
                out=OBb[:, 12288:16384].rearrange("p (c f) -> p c f", f=256),
                in_=ctx.wc["l2a_wkv"][:, :].rearrange("(c p) f -> p c f", p=128))
        em.op("sync", fwk, r=(("wc", "l2a_wkv"),), w=("OB_K",), dma=True,
              tag="at.lwk")
    else:
        def fwk(eng):
            return eng.dma_start(
                out=OBb[:, 12288:16384].rearrange("p (c f) -> p c f", f=256),
                in_=ctx.wc[cfg["wk"]][:, :].rearrange("(c p) f -> p c f", p=128))
        em.op("sync", fwk, r=(("wc", cfg["wk"]),), w=("OB_K",), dma=True,
              tag="at.lwk")
        def fwv(eng):
            return eng.dma_start(
                out=WB[:, 4096:8192].rearrange("p (c f) -> p c f", f=256),
                in_=ctx.wc[cfg["wv"]][:, :].rearrange("(c p) f -> p c f", p=128))
        em.op("sync", fwv, r=(("wc", cfg["wv"]),), w=("WB_V",), dma=True,
              tag="at.lwv")
    if cfg.get("use_global"):
        prep_globals_l0a(ctx)
    em.claim("XN", tag="at.clxn")
    for f in range(NFAT):
        xs = f % 2
        xslot = ctx.sb["XN"][:].rearrange("p a b -> p (a b)")[:, xs * 8192:(xs + 1) * 8192]
        xslot3 = xslot.rearrange("p (c f2) -> p c f2", f2=512)
        def fxl(eng, f=f, xslot3=xslot3):
            rf, half = f // 2, f % 2
            return eng.dma_start(
                out=xslot3,
                in_=xg[2048 * rf:2048 * (rf + 1), half * 512:(half + 1) * 512]
                    .rearrange("(c p) t -> p c t", p=128))
        em.op("sync", fxl, r=("xg", "XN", f"xsl{xs}"), w=(f"xsl{xs}",), dma=True,
              tag="at.xl")
        if cfg.get("rope"):
            rr, rows = ("c_rope2", 64) if cfg["rope"] == "l2" else ("c_rope3", 64)
            def frp(eng, f=f, rr=rr, rows=rows):
                return eng.dma_start(out=ctx.sb["RP"][0:rows, 0:512],
                                     in_=ctx.inp[rr + "c"][:, f * 512:(f + 1) * 512])
            em.op("gpsimd", frp, r=("RP",), w=("RP",), dma=True, tag="at.rp")
            def frpb(eng, f=f, rr=rr, rows=rows):
                return eng.dma_start(out=ctx.sb["RP"][0:rows, 512:1024],
                                     in_=ctx.inp[rr + "s"][:, f * 512:(f + 1) * 512])
            em.op("gpsimd", frpb, r=("RP",), w=("RP",), dma=True, tag="at.rpb")
            if cfg["rope"] == "l3":
                def frp2(eng, f=f, rr=rr):
                    return eng.dma_start(out=ctx.sb["RP"][64:128, 0:512],
                                         in_=ctx.inp[rr + "c"][:, f * 512:(f + 1) * 512])
                em.op("gpsimd", frp2, r=("RP",), w=("RP",), dma=True, tag="at.rp2")
                def frpb2(eng, f=f, rr=rr):
                    return eng.dma_start(out=ctx.sb["RP"][64:128, 512:1024],
                                         in_=ctx.inp[rr + "s"][:, f * 512:(f + 1) * 512])
                em.op("gpsimd", frpb2, r=("RP",), w=("RP",), dma=True, tag="at.rpb2")
        make_qkv_fat(ctx, cfg, base, f, xslot3, xs)
    dbg_dram("xg", ctx.dram["xg"], "xg")
    if "wcq" in TAPS and base == "l0f":
        dbg_dram("wcq", ctx.wc["l0f_wq"], ("wc", "l0f_wq"))
    dbg_sb("sg", ctx.sb["SG"][:], "SG")
    dbg_dram("kloc", ctx.dram["kloc"], "kloc")
    dbg_dram("vloc", ctx.dram["vloc"], "vloc")
    dbg_sb("qb", ctx.sb["QB"][:], "QB")
    for s in range(4):
        def fo(eng, s=s, dv=dv):
            return eng.memset(
                ctx.sb["KVA"][:, 1024 + s * 264:1024 + s * 264 + 264]
                .rearrange("p (g2 f2) -> p g2 f2", f2=132)[:, :, dv:132],
                1.0)
        em.op("vector", fo, r=(f"vslot{s}",), w=(f"vslot{s}", "vones"),
              tag="vones")
    for f in range(NFAT):
        flash_fat(ctx, cfg, base, f)
        partial_out_fat(ctx, cfg, f)
    dbg_dram("rsin", ctx.dram["rs_in"], "rs_in")
    def frs(eng):
        return eng.collective_compute(
            "ReduceScatter", ALU.add, replica_groups=[list(range(NCORE))],
            ins=[ctx.dram["rs_in"].opt()], outs=[ctx.dram["rs_out"].opt()])
    em.op("gpsimd", frs, r=("rs_in",), w=("rs_out",), cc=True, tag="at.rs")
    dbg_dram("rsout", ctx.dram["rs_out"], "rs_out")
    add_tokmajor_dram_to_R(ctx, ctx.dram["rs_out"], cfg["out_scale"],
                           extra_r=("rs_out",))


def _evac_fm(ctx, psres, ps_ap, dst_ap, dst_res, scale_ap=None, scale_res=None,
             eng_name="vector", tag="ev"):
    em = ctx.em
    if scale_ap is not None:
        def f(eng):
            return eng.tensor_scalar(out=dst_ap, in0=ps_ap, scalar1=scale_ap,
                                     scalar2=None, op0=ALU.mult)
        em.op(eng_name, f, r=(psres, scale_res), w=(dst_res,), tag=tag)
    else:
        def f(eng):
            return eng.tensor_copy(out=dst_ap, in_=ps_ap)
        em.op(eng_name, f, r=(psres,), w=(dst_res,), tag=tag)


def make_qkv_fat(ctx, cfg, base, f, xslot3, xs):
    em = ctx.em
    QB, OB, WB, ES = ctx.sb["QB"], ctx.sb["OB"], ctx.sb["WB"], ctx.sb["ES"]
    OBb = OB[:]
    OQ = OBb[:, 8192:12288].rearrange("p (c f2) -> p c f2", f2=256)
    OK = OBb[:, 12288:16384].rearrange("p (c f2) -> p c f2", f2=256)
    OK3 = OBb[:, 12288:14336].rearrange("p (c f2) -> p c f2", f2=128)
    WV = WB[:, 4096:8192].rearrange("p (c f2) -> p c f2", f2=256)
    xres = f"xsl{xs}"
    kloc, vloc = ctx.dram["kloc"], ctx.dram["vloc"]

    def mm_acc(bank, lhs_fn, rhs_fn, n, rres, wres, nch=16, tag="mm", pdim=128):
        for c in range(nch):
            def fm(eng, c=c, bank=bank):
                return eng.matmul(ctx.ps[bank][0:pdim, 0:n], lhsT=lhs_fn(c),
                                  rhs=rhs_fn(c), start=(c == 0),
                                  stop=(c == nch - 1))
            em.op("tensor", fm, r=rres, w=(wres,), tag=tag)

    if base == "l1a":
        make_qkv_fat_l1a(ctx, cfg, f, xslot3, xs)
        return
    if base == "l3a":
        for jj in range(2):
            bank = jj % 2
            mm_acc(bank, lambda c, jj=jj: OQ[:, c, :].rearrange("p (a b) -> p a b", b=128)[:, jj, :],
                   lambda c: xslot3[:, c, :], 512, ("OB_Q", xres), f"ps{bank}",
                   tag="q3.mm")
            dst = QB[:, jj * 8192 + f * 512:jj * 8192 + (f + 1) * 512]
            _evac_fm(ctx, f"ps{bank}", ctx.ps[bank][:, 0:512], dst, "QB",
                     tag="q3.ev")
            rope_apply(ctx, dst, "QB", "l3", tagp="q3.")
        mm_acc(0, lambda c: OK3[:, c, 0:64], lambda c: xslot3[:, c, :], 512,
               ("OB_K", xres), "ps0", tag="k3.mm", pdim=64)
        _evac_fm(ctx, "ps0", ctx.ps[0][0:64, 0:512], ES[0:64, 0:512], "esK",
                 tag="k3.ev")
        rope_apply(ctx, ES[0:64, 0:512], "esK", "l3k", tagp="k3.")
        def fks(eng, f=f):
            return eng.dma_start(out=kloc[0:64, f * 512:(f + 1) * 512],
                                 in_=ES[0:64, 0:512])
        em.op("sync", fks, r=("esK",), w=("kloc",), dma=True, tag="k3.st")
        for ts_ in range(4):
            bank = ts_ % 2
            mm_acc(bank, lambda c, ts_=ts_: xslot3[:, c, ts_ * 128:(ts_ + 1) * 128],
                   lambda c: OK3[:, c, 64:128], 64, (xres, "OB_K"), f"ps{bank}",
                   tag="v3.mm")
            _evac_fm(ctx, f"ps{bank}", ctx.ps[bank][:, 0:64],
                     ES[:, 512 + ts_ * 64:512 + (ts_ + 1) * 64], "esV", tag="v3.ev")
        def fvs(eng, f=f):
            return eng.dma_start(
                out=vloc[f * 512:(f + 1) * 512, 0:64]
                    .rearrange("(a p) b -> p a b", p=128),
                in_=ES[:, 512:768].rearrange("p (a b) -> p a b", b=64))
        em.op("sync", fvs, r=("esV",), w=("vloc",), dma=True, tag="v3.st")
        return
    if base == "l2a":
        mm_acc(0, lambda c: OK[:, c, 0:128], lambda c: xslot3[:, c, :], 512,
               ("OB_K", xres), "ps0", tag="kc.mm")
        _evac_fm(ctx, "ps0", ctx.ps[0][:, 0:512], ES[:, 0:512], "esKC",
                 tag="kc.ev")
        for hh in range(2):
            def fsc(eng, hh=hh):
                return eng.tensor_scalar(out=PBK(ctx)[:, 0:512], in0=ES[:, 0:512],
                                         scalar1=ctx.sb["SG"][:, 2 + hh:3 + hh],
                                         scalar2=None, op0=ALU.mult)
            em.op("vector", fsc, r=("esKC", "SG", "rp_t", "rp_u"), w=("esKH",),
                  tag="kh.sc")
            rope_apply(ctx, PBK(ctx)[:, 0:512], "esKH", "l2", tagp="kh.")
            def fks(eng, f=f, hh=hh):
                return eng.dma_start(
                    out=kloc[hh * 128:(hh + 1) * 128, f * 512:(f + 1) * 512],
                    in_=PBK(ctx)[:, 0:512])
            em.op("sync", fks, r=("esKH",), w=("kloc",), dma=True, tag="kh.st")
        for ts_ in range(4):
            bank = ts_ % 2
            mm_acc(bank, lambda c, ts_=ts_: xslot3[:, c, ts_ * 128:(ts_ + 1) * 128],
                   lambda c: OK[:, c, 128:256], 128, (xres, "OB_K"), f"ps{bank}",
                   tag="vc.mm")
            _evac_fm(ctx, f"ps{bank}", ctx.ps[bank][:, 0:128],
                     ES[:, 512 + ts_ * 128:512 + (ts_ + 1) * 128], "esV",
                     tag="vc.ev")
        def fvs(eng, f=f):
            return eng.dma_start(
                out=vloc[f * 512:(f + 1) * 512, 0:128]
                    .rearrange("(a p) b -> p a b", p=128),
                in_=ES[:, 512:1024].rearrange("p (a b) -> p a b", b=128))
        em.op("sync", fvs, r=("esV",), w=("vloc",), dma=True, tag="vc.st")
    else:
        for hh in range(2):
            bank = hh % 2
            mm_acc(bank, lambda c, hh=hh: OK[:, c, hh * 128:(hh + 1) * 128],
                   lambda c: xslot3[:, c, :], 512, ("OB_K", xres), f"ps{bank}",
                   tag="k.mm")
            sc = (ctx.sb["SG"][:, hh:hh + 1], "SG") if base == "l0f" else (None, None)
            _evac_fm(ctx, f"ps{bank}", ctx.ps[bank][:, 0:512],
                     ES[:, hh * 512:(hh + 1) * 512], "esK", sc[0], sc[1],
                     tag="k.ev")
        def fks(eng, f=f):
            return eng.dma_start(
                out=kloc[0:256, f * 512:(f + 1) * 512]
                    .rearrange("(a p) b -> p a b", p=128),
                in_=ES[:, 0:1024].rearrange("p (a b) -> p a b", b=512))
        em.op("sync", fks, r=("esK",), w=("kloc",), dma=True, tag="k.st")
        for ts_ in range(4):
            bank = ts_ % 2
            mm_acc(bank, lambda c, ts_=ts_: xslot3[:, c, ts_ * 128:(ts_ + 1) * 128],
                   lambda c: WV[:, c, :], 256, (xres, "WB_V"), f"ps{bank}",
                   tag="v.mm")
            _evac_fm(ctx, f"ps{bank}", ctx.ps[bank][:, 0:256],
                     ctx.sb["RT"][:, ts_ * 256:(ts_ + 1) * 256], "esV", tag="v.ev")
        def fvs(eng, f=f):
            return eng.dma_start(
                out=vloc[f * 512:(f + 1) * 512, 0:256]
                    .rearrange("(a p) b -> p a b", p=128),
                in_=ctx.sb["RT"][:, 0:1024].rearrange("p (a b) -> p a b", b=256))
        em.op("sync", fvs, r=("esV",), w=("vloc",), dma=True, tag="v.st")
    for hh in range(2):
        bank = hh % 2
        mm_acc(bank, lambda c, hh=hh: OQ[:, c, hh * 128:(hh + 1) * 128],
               lambda c: xslot3[:, c, :], 512, ("OB_Q", xres), f"ps{bank}",
               tag="q.mm")
        dst = QB[:, hh * 8192 + f * 512:hh * 8192 + (f + 1) * 512]
        sc = (ctx.sb["SG"][:, hh:hh + 1], "SG") if base == "l0f" else (None, None)
        _evac_fm(ctx, f"ps{bank}", ctx.ps[bank][:, 0:512], dst, "QB",
                 sc[0], sc[1], tag="q.ev")
        if base == "l2a":
            rope_apply(ctx, dst, "QB", "l2", tagp="q2.")


def PBK(ctx):
    return ctx.sb["KVA"][:, 0:512]


def make_qkv_fat_l1a(ctx, cfg, f, xslot3, xs):
    em = ctx.em
    QB, OB, WB, ES, SC, ST = (ctx.sb["QB"], ctx.sb["OB"], ctx.sb["WB"],
                              ctx.sb["ES"], ctx.sb["SC"], ctx.sb["ST"])
    OBb = OB[:]
    OQ = OBb[:, 8192:12288].rearrange("p (c f2) -> p c f2", f2=256)
    OK = OBb[:, 12288:16384].rearrange("p (c f2) -> p c f2", f2=256)
    WV = WB[:, 4096:8192].rearrange("p (c f2) -> p c f2", f2=256)
    xres = f"xsl{xs}"
    kloc, vloc = ctx.dram["kloc"], ctx.dram["vloc"]
    for ts_ in range(4):
        for side, WS, wres, b0 in (("q", OQ, "OB_Q", 0), ("k", OK, "OB_K", 1)):
            bank = b0
            for c in range(NCH):
                def fm(eng, c=c, bank=bank, WS=WS, ts_=ts_):
                    return eng.matmul(ctx.ps[bank][:, 0:256],
                                      lhsT=xslot3[:, c, ts_ * 128:(ts_ + 1) * 128],
                                      rhs=WS[:, c, :], start=(c == 0),
                                      stop=(c == NCH - 1))
                em.op("tensor", fm, r=(xres, wres), w=(f"ps{bank}",), tag="fm.mm")
            def fy(eng, bank=bank):
                return eng.tensor_mul(out=SC[:, 0:256], in0=ctx.ps[bank][:, 0:256],
                                      in1=ctx.sb["SC2"][:, 0:256])
            em.op("vector", fy, r=(f"ps{bank}", "gbt"), w=("SC_A",), tag="fm.g")
            def fy2(eng):
                return eng.tensor_add(out=SC[:, 0:256], in0=SC[:, 0:256],
                                      in1=ctx.sb["SC2"][:, 256:512])
            em.op("vector", fy2, r=("SC_A", "gbt"), w=("SC_A",), tag="fm.b")
            def fs(eng):
                return eng.reduce_sum(out=ST[:, 0:2],
                                      in_=SC[:, 0:256].rearrange("p (a b) -> p a b", b=128),
                                      axis=AXX)
            em.op("vector", fs, r=("SC_A",), w=("fm_s",), tag="fm.s")
            def fq(eng):
                return eng.activation(out=SC[:, 256:512], in_=SC[:, 0:256],
                                      func=AF.Square)
            em.op("scalar", fq, r=("SC_A",), w=("fm_q",), tag="fm.q")
            def fs2(eng):
                return eng.reduce_sum(out=ST[:, 2:4],
                                      in_=SC[:, 256:512].rearrange("p (a b) -> p a b", b=128),
                                      axis=AXX)
            em.op("vector", fs2, r=("fm_q",), w=("fm_s2",), tag="fm.s2")
            def fmu(eng):
                return eng.tensor_scalar(out=ST[:, 4:6], in0=ST[:, 0:2],
                                         scalar1=1.0 / 128, scalar2=None,
                                         op0=ALU.mult)
            em.op("vector", fmu, r=("fm_s",), w=("fm_mu",), tag="fm.mu")
            def fvar(eng):
                return eng.tensor_scalar(out=ST[:, 6:8], in0=ST[:, 2:4],
                                         scalar1=1.0 / 128, scalar2=None,
                                         op0=ALU.mult)
            em.op("vector", fvar, r=("fm_s2",), w=("fm_va",), tag="fm.va")
            def fvb(eng):
                return eng.scalar_tensor_tensor(out=ST[:, 8:10], in0=ST[:, 4:6],
                                                scalar=-1.0, in1=ST[:, 4:6],
                                                op0=ALU.mult, op1=ALU.mult)
            em.op("vector", fvb, r=("fm_mu",), w=("fm_m2",), tag="fm.m2")
            def fvc(eng):
                return eng.tensor_add(out=ST[:, 6:8], in0=ST[:, 6:8],
                                      in1=ST[:, 8:10])
            em.op("vector", fvc, r=("fm_va", "fm_m2"), w=("fm_va",), tag="fm.vc")
            def fl_(eng):
                return eng.activation(out=ST[:, 10:12], in_=ST[:, 6:8],
                                      func=AF.Ln,
                                      bias=ctx.sb["c_mc"][:, 4:5])
            em.op("scalar", fl_, r=("fm_va", "c_mc"), w=("fm_l",), tag="fm.l")
            def fe(eng):
                return eng.activation(out=ST[:, 12:14], in_=ST[:, 10:12],
                                      func=AF.Exp, scale=-0.5)
            em.op("scalar", fe, r=("fm_l",), w=("fm_r",), tag="fm.e")
            for hh in range(2):
                def fa(eng, hh=hh):
                    return eng.tensor_scalar(
                        out=ctx.sb["SB16"][:, 0:128], in0=SC[:, hh * 128:(hh + 1) * 128],
                        scalar1=ST[:, 4 + hh:5 + hh], scalar2=ST[:, 12 + hh:13 + hh],
                        op0=ALU.subtract, op1=ALU.mult)
                em.op("vector", fa, r=("SC_A", "fm_mu", "fm_r"), w=("fm_y",),
                      tag="fm.a")
                def ft(eng):
                    return eng.transpose(ctx.ps["tr"][:, 0:128],
                                         ctx.sb["SB16"][:, 0:128],
                                         ctx.sb["c_ident"][:])
                em.op("tensor", ft, r=("fm_y", "c_ident"), w=("ps_tr",), tag="fm.t")
                if side == "q":
                    def fc(eng, hh=hh, ts_=ts_, f=f):
                        return eng.tensor_copy(
                            out=QB[:, hh * 8192 + f * 512 + ts_ * 128:
                                   hh * 8192 + f * 512 + (ts_ + 1) * 128],
                            in_=ctx.ps["tr"][:, 0:128])
                    em.op("vector", fc, r=("ps_tr",), w=("QB",), tag="fm.cq")
                else:
                    def fc(eng, hh=hh, ts_=ts_):
                        return eng.tensor_copy(
                            out=ES[:, hh * 512 + ts_ * 128:hh * 512 + (ts_ + 1) * 128],
                            in_=ctx.ps["tr"][:, 0:128])
                    em.op("vector", fc, r=("ps_tr",), w=("esK",), tag="fm.ck")
    def fks(eng, f=f):
        return eng.dma_start(
            out=kloc[0:256, f * 512:(f + 1) * 512]
                .rearrange("(a p) b -> p a b", p=128),
            in_=ES[:, 0:1024].rearrange("p (a b) -> p a b", b=512))
    em.op("sync", fks, r=("esK",), w=("kloc",), dma=True, tag="fm.kst")
    for ts_ in range(4):
        bank = ts_ % 2
        for c in range(NCH):
            def fm(eng, c=c, bank=bank, ts_=ts_):
                return eng.matmul(ctx.ps[bank][:, 0:256],
                                  lhsT=xslot3[:, c, ts_ * 128:(ts_ + 1) * 128],
                                  rhs=WV[:, c, :], start=(c == 0),
                                  stop=(c == NCH - 1))
            em.op("tensor", fm, r=(xres, "WB_V"), w=(f"ps{bank}",), tag="fmv.mm")
        _evac_fm(ctx, f"ps{bank}", ctx.ps[bank][:, 0:256],
                 ctx.sb["RT"][:, ts_ * 256:(ts_ + 1) * 256], "esV", tag="fmv.ev")
    def fvs(eng, f=f):
        return eng.dma_start(
            out=vloc[f * 512:(f + 1) * 512, 0:256]
                .rearrange("(a p) b -> p a b", p=128),
            in_=ctx.sb["RT"][:, 0:1024].rearrange("p (a b) -> p a b", b=256))
    em.op("sync", fvs, r=("esV",), w=("vloc",), dma=True, tag="fmv.st")


def build_gb_tiles(ctx):
    em = ctx.em
    SC2 = ctx.sb["SC2"]
    for which, srcn, c0 in (("g", "L1_attn_gamma", 0), ("b", "L1_attn_beta", 256)):
        def fl(eng, srcn=srcn, c0=c0):
            return eng.dma_start(
                out=SC2[:, c0:c0 + 256].rearrange("p (a b) -> p a b", b=128),
                in_=ctx.inp[srcn].unsqueeze(0).unsqueeze(0)
                .broadcast_to((128, 2, 128)))
        em.op("gpsimd", fl, r=("gbt",), w=("gbt",), dma=True, tag="gb.ld")



def prep_globals_l0a(ctx):
    em = ctx.em
    OB, WB, KVA, SB16 = ctx.sb["OB"], ctx.sb["WB"], ctx.sb["KVA"], ctx.sb["SB16"]
    OK = OB[:][:, 12288:16384].rearrange("p (c f2) -> p c f2", f2=256)
    WV = WB[:, 4096:8192].rearrange("p (c f2) -> p c f2", f2=256)
    g = ctx.inp["L0_attn_global_tokens"]
    for c0 in range(0, 16, 8):
        def fl(eng, c0=c0):
            return eng.dma_start(out=ctx.sb["RT"][0:16, 0:1024],
                                 in_=g[0][:, c0 * 128:(c0 + 8) * 128])
        em.op("gpsimd", fl, r=("RT",), w=("RT",), dma=True, tag="gl.ld")
        for j in range(8):
            def ft(eng, j=j):
                return eng.transpose(ctx.ps["tr"][:, j * 16:(j + 1) * 16],
                                     ctx.sb["RT"][0:16, j * 128:(j + 1) * 128],
                                     ctx.sb["c_ident"][0:16, 0:16])
            em.op("tensor", ft, r=("RT", "c_ident"), w=("ps_tr",), tag="gl.t")
        def fcp(eng, c0=c0):
            return eng.tensor_copy(out=SB16[:, c0 * 16:(c0 + 8) * 16],
                                   in_=ctx.ps["tr"][:, 0:128])
        em.op("vector", fcp, r=("ps_tr",), w=("gT",), tag="gl.cp")
    for hh in range(2):
        for c in range(NCH):
            def fm(eng, c=c, hh=hh):
                return eng.matmul(ctx.ps[0][:, 0:16],
                                  lhsT=OK[:, c, hh * 128:(hh + 1) * 128],
                                  rhs=SB16[:, c * 16:(c + 1) * 16],
                                  start=(c == 0), stop=(c == NCH - 1))
            em.op("tensor", fm, r=("OB_K", "gT"), w=("ps0",), tag="gl.kmm")
        def fe(eng, hh=hh):
            return eng.tensor_copy(out=KVA[:, 2080 + hh * 16:2096 + hh * 16],
                                   in_=ctx.ps[0][:, 0:16])
        em.op("vector", fe, r=("ps0",), w=("kglob",), tag="gl.kev")
    def fo(eng):
        return eng.memset(KVA[0:16, 2120:2384].rearrange("p (g2 f2) -> p g2 f2", f2=132)[:, :, 128:129], 1.0)
    em.op("vector", fo, w=("vglob",), tag="gl.vo")
    for c in range(NCH):
        def fm(eng, c=c):
            return eng.matmul(ctx.ps[1][0:16, 0:256],
                              lhsT=SB16[:, c * 16:(c + 1) * 16],
                              rhs=WV[:, c, :], start=(c == 0),
                              stop=(c == NCH - 1))
        em.op("tensor", fm, r=("gT", "WB_V"), w=("ps1",), tag="gl.vmm")
    def fe2(eng):
        return eng.tensor_copy(
            out=KVA[0:16, 2120:2384].rearrange("p (g2 f2) -> p g2 f2", f2=132)[:, :, 0:128],
            in_=ctx.ps[1][0:16, 0:256].rearrange("p (g2 f2) -> p g2 f2", f2=128))
    em.op("vector", fe2, r=("ps1", "vglob"), w=("vglob",), tag="gl.vev")


def flash_fat(ctx, cfg, base, f):
    em = ctx.em
    hl, dh, dv = cfg["hl"], cfg["dh"], cfg["dv"]
    M, scale = cfg["M"], cfg["scale"]
    band = cfg.get("band")
    kv_shared = cfg.get("kv_shared", False) or base == "l3a"
    KVA, PB, QB = ctx.sb["KVA"], ctx.sb["PB"], ctx.sb["QB"]
    OBf32 = ctx.sb["OB"][:].bitcast(F32)
    DN = ctx.sb["DN"]
    kloc, vloc = ctx.dram["kloc"], ctx.dram["vloc"]
    n_kg = 1 if kv_shared else hl
    ksz = n_kg * 128
    vsz = n_kg * 132
    koff = [s * 256 for s in range(4)]
    voff = [1024 + s * 264 for s in range(4)]
    b = AB[f]

    def qsrc(h):
        if base == "l3a":
            return QB[(h % 2) * 64:(h % 2) * 64 + 64,
                      (h // 2) * 8192:(h // 2 + 1) * 8192]
        return QB[:, h * 8192:(h + 1) * 8192]

    def jrange(kt):
        q0a = b * BLK
        ktr = kt - q0a // 128
        jlo = max(0, ktr)
        jhi = min(3, ktr + 2) if band else 3
        if jlo > jhi:
            return None
        diag_j = ktr if 0 <= ktr <= 3 else None
        uedge_j = (ktr + 2) if (band and 0 <= ktr + 2 <= 3) else None
        return jlo, jhi, diag_j, uedge_j

    def fz(eng):
        return eng.memset(OBf32[:, 0:hl * 4 * dv], 0.0)
    em.op("vector", fz, w=("OBa",), tag="fl.zO")
    def fz2(eng):
        return eng.memset(DN[:, 128:128 + hl * 4], 0.0)
    em.op("vector", fz2, w=("DNa",), tag="fl.zD")

    kt0 = max(0, (b * BLK) // 128 - 2) if band else 0
    kt_list = [kt for kt in range(kt0, (b + 1) * 4) if jrange(kt) is not None]
    groups = [kt_list[i:i + 4] for i in range(0, len(kt_list), 4)]
    if cfg.get("use_global"):
        for h in range(hl):
            sb_ = h % 2
            def fqk(eng, h=h, sb_=sb_, f=f):
                return eng.matmul(
                    ctx.ps[sb_][0:16, 0:512],
                    lhsT=KVA[:, 2080 + h * 16:2096 + h * 16],
                    rhs=qsrc(h)[:, f * 512:(f + 1) * 512],
                    start=True, stop=True)
            em.op("tensor", fqk, r=("kglob", "QB"), w=(f"ps{sb_}",), tag="gf.qk")
            def fex(eng, sb_=sb_):
                return eng.activation(out=PB[0:16, sb_ * 512:(sb_ + 1) * 512],
                                      in_=ctx.ps[sb_][0:16, 0:512],
                                      func=AF.Exp, scale=scale,
                                      bias=ctx.sb["c_mc"][0:16, cfg["mcc"]:cfg["mcc"] + 1])
            em.op("scalar", fex, r=(f"ps{sb_}", "c_mc"), w=(f"pb{sb_}",), tag="gf.exp")
            for j in range(4):
                ob = 2 + j
                def fav(eng, h=h, j=j, ob=ob, sb_=sb_):
                    return eng.matmul(
                        ctx.ps[ob][:, 0:dv + 1],
                        lhsT=PB[0:16, sb_ * 512 + j * 128:sb_ * 512 + (j + 1) * 128],
                        rhs=KVA[0:16, 2120 + h * 132:2120 + h * 132 + dv + 1],
                        start=True, stop=True)
                em.op("tensor", fav, r=(f"pb{sb_}", "vglob"), w=(f"ps{ob}",),
                      tag="gf.av")
            _flash_acc(ctx, h, {j: True for j in range(4)}, dv)
    for grp in groups:
        touch = {}
        for kt in grp:
            jlo, jhi, _, _ = jrange(kt)
            for j in range(jlo, jhi + 1):
                touch.setdefault(j, []).append(kt)
        for si, kt in enumerate(grp):
            blkk = kt * 128 // BLK
            gc = FAT_OF_BLOCK[blkk] * BLK + (kt * 128) % BLK
            def flk(eng, si=si, gc=gc):
                return eng.dma_start(
                    out=KVA[0:dh, koff[si]:koff[si] + ksz]
                        .rearrange("p (a f2) -> p a f2", f2=128),
                    in_=kloc[0:n_kg * dh, gc:gc + 128]
                        .rearrange("(a p) f2 -> p a f2", p=dh))
            em.op("sync", flk, r=("kloc", f"kslot{si}"), w=(f"kslot{si}",),
                  dma=True, tag="fl.ldk")
            if base == "l3a":
                def flk2(eng, si=si, gc=gc):
                    return eng.dma_start(
                        out=KVA[64:128, koff[si]:koff[si] + 128],
                        in_=kloc[0:64, gc:gc + 128])
                em.op("sync", flk2, r=("kloc", f"kslot{si}"),
                      w=(f"kslot{si}",), dma=True, tag="fl.ldk2")
            def flv(eng, si=si, gc=gc):
                return eng.dma_start(
                    out=KVA[:, voff[si]:voff[si] + vsz]
                        .rearrange("p (a f2) -> p a f2", f2=132)[:, :, 0:dv],
                    in_=vloc[gc:gc + 128, 0:n_kg * dv]
                        .rearrange("p (a f2) -> p a f2", f2=dv))
            em.op("sync", flv, r=("vloc", f"vslot{si}"), w=(f"vslot{si}",),
                  dma=True, tag="fl.ldv")
        for h in range(hl):
            g_ = 0 if kv_shared else h

            def stage_av(si, kt, g_=g_):
                jlo, jhi, _, _ = jrange(kt)
                sb_ = si % 2
                pslot = sb_ * 512
                for j in range(jlo, jhi + 1):
                    ob = 2 + j
                    kts = touch[j]
                    def fav(eng, g_=g_, j=j, jlo=jlo, ob=ob,
                            pslot=pslot, si=si,
                            st=(kt == kts[0]), sp=(kt == kts[-1])):
                        return eng.matmul(
                            ctx.ps[ob][:, 0:dv + 1],
                            lhsT=PB[:, pslot + (j - jlo) * 128:pslot + (j - jlo + 1) * 128],
                            rhs=KVA[:, voff[si] + g_ * 132:voff[si] + g_ * 132 + dv + 1],
                            start=st, stop=sp)
                    em.op("tensor", fav, r=(f"pb{sb_}", f"vslot{si}", "vones"),
                          w=(f"ps{ob}",), tag="fl.av")

            for si, kt in enumerate(grp):
                jr = jrange(kt)
                jlo, jhi, diag_j, uedge_j = jr
                nq = jhi - jlo + 1
                sb_ = si % 2
                pslot = sb_ * 512
                qb_ = 64 * (h % 2) if base == "l3a" else 0
                def fqk(eng, h=h, si=si, g_=g_, f=f, jlo=jlo, nq=nq, sb_=sb_,
                        qb_=qb_):
                    return eng.matmul(
                        ctx.ps[sb_][:, 0:nq * 128],
                        lhsT=KVA[qb_:qb_ + dh,
                                 koff[si] + g_ * 128:koff[si] + (g_ + 1) * 128],
                        rhs=qsrc(h)[:, f * 512 + jlo * 128:f * 512 + (jlo + nq) * 128],
                        start=True, stop=True)
                em.op("tensor", fqk, r=(f"kslot{si}", "QB"), w=(f"ps{sb_}",),
                      tag="fl.qk")
                func = AF.Exp if M is not None else AF.Square
                kw = (dict(scale=scale) if M is None else
                      dict(scale=scale,
                           bias=ctx.sb["c_mc"][:, cfg["mcc"]:cfg["mcc"] + 1]))
                def fex(eng, sb_=sb_, pslot=pslot, nq=nq, func=func, kw=kw):
                    return eng.activation(out=PB[:, pslot:pslot + nq * 128],
                                          in_=ctx.ps[sb_][:, 0:nq * 128],
                                          func=func, **kw)
                em.op("scalar", fex, r=(f"ps{sb_}", "c_mc"), w=(f"pb{sb_}",),
                      tag="fl.ex")
                for (mj, mask) in ((diag_j, "c_maskL"), (uedge_j, "c_maskU")):
                    if mj is None:
                        continue
                    def fmk(eng, pslot=pslot, mj=mj, jlo=jlo, mask=mask):
                        return eng.tensor_mul(
                            out=PB[:, pslot + (mj - jlo) * 128:pslot + (mj - jlo + 1) * 128],
                            in0=PB[:, pslot + (mj - jlo) * 128:pslot + (mj - jlo + 1) * 128],
                            in1=ctx.sb[mask][:])
                    em.op("vector", fmk, r=(f"pb{sb_}", mask), w=(f"pb{sb_}",),
                          tag="fl.mk")
                if si > 0:
                    stage_av(si - 1, grp[si - 1])
            stage_av(len(grp) - 1, grp[-1])
            _flash_acc(ctx, h, touch, dv)
    OF, ES = ctx.sb["OF"], ctx.sb["ES"]
    for h in range(hl):
        for j in range(4):
            if cfg.get("denom_eps"):
                def fre(eng, h=h, j=j):
                    return eng.tensor_scalar(
                        out=ctx.sb["SC"][:, 1025:1026],
                        in0=DN[:, 128 + h * 4 + j:128 + h * 4 + j + 1],
                        scalar1=cfg["denom_eps"], scalar2=None, op0=ALU.add)
                em.op("vector", fre, r=("DNa", "SC_C"), w=("SC_C",), tag="fl.rce")
                def fr(eng):
                    return eng.reciprocal(out=ctx.sb["SC"][:, 1024:1025],
                                          in_=ctx.sb["SC"][:, 1025:1026])
                em.op("vector", fr, r=("SC_C",), w=("SC_C",), tag="fl.rcp")
            else:
                def fr(eng, h=h, j=j):
                    return eng.reciprocal(out=ctx.sb["SC"][:, 1024:1025],
                                          in_=DN[:, 128 + h * 4 + j:128 + h * 4 + j + 1])
                em.op("vector", fr, r=("DNa", "SC_C"), w=("SC_C",), tag="fl.rcp")
            def fd(eng, h=h, j=j):
                return eng.tensor_scalar(
                    out=ES[:, 0:dv],
                    in0=OBf32[:, (h * 4 + j) * dv:(h * 4 + j + 1) * dv],
                    scalar1=ctx.sb["SC"][:, 1024:1025], scalar2=None,
                    op0=ALU.mult)
            em.op("vector", fd, r=("OBa", "SC_C"), w=("esd",), tag="fl.div")
            def ftr(eng):
                return eng.transpose(ctx.ps["tr"][0:dv, 0:128], ES[:, 0:dv],
                                     ctx.sb["c_ident"][:])
            em.op("tensor", ftr, r=("esd", "c_ident"), w=("ps_tr",), tag="fl.tr")
            if base == "l3a":
                def fcp(eng, h=h, j=j):
                    return eng.tensor_copy(
                        out=OF[(h % 2) * 64:(h % 2) * 64 + 64,
                               (h // 2) * 512 + j * 128:(h // 2) * 512 + (j + 1) * 128],
                        in_=ctx.ps["tr"][0:dv, 0:128])
                em.op("vector", fcp, r=("ps_tr",), w=("OFr",), tag="fl.oc")
            else:
                def fcp(eng, h=h, j=j):
                    return eng.tensor_copy(
                        out=OF[:, h * 512 + j * 128:h * 512 + (j + 1) * 128],
                        in_=ctx.ps["tr"][0:dv, 0:128])
                em.op("vector", fcp, r=("ps_tr",), w=("OFr",), tag="fl.oc")


def _flash_acc(ctx, h, touch, dv):
    em = ctx.em
    OBf32 = ctx.sb["OB"][:].bitcast(F32)
    DN = ctx.sb["DN"]
    for j in sorted(touch):
        ob = 2 + j
        def fac(eng, h=h, j=j, ob=ob):
            return eng.tensor_tensor(
                out=OBf32[:, (h * 4 + j) * dv:(h * 4 + j + 1) * dv],
                in0=ctx.ps[ob][:, 0:dv],
                in1=OBf32[:, (h * 4 + j) * dv:(h * 4 + j + 1) * dv], op=ALU.add)
        em.op("vector", fac, r=(f"ps{ob}", "OBa"), w=("OBa",), tag="fl.ac")
        def fdn(eng, h=h, j=j, ob=ob):
            return eng.tensor_tensor(
                out=DN[:, 128 + h * 4 + j:128 + h * 4 + j + 1],
                in0=ctx.ps[ob][:, dv:dv + 1],
                in1=DN[:, 128 + h * 4 + j:128 + h * 4 + j + 1], op=ALU.add)
        em.op("vector", fdn, r=(f"ps{ob}", "DNa"), w=("DNa",), tag="fl.dc")


def partial_out_fat(ctx, cfg, f):
    em = ctx.em
    OF, RT = ctx.sb["OF"], ctx.sb["RT"]
    OW = ctx.sb["OB"][:][:, 4096:8192].rearrange("p (c f2) -> p c f2", f2=2048)
    for ts_ in range(4):
        for half in range(2):
            for mf in (2 * half, 2 * half + 1):
                bank = mf % 2 + 2
                for c in range(2):
                    def fm(eng, ts_=ts_, mf=mf, c=c, bank=bank):
                        return eng.matmul(
                            ctx.ps[bank][:, 0:512],
                            lhsT=OF[:, c * 512 + ts_ * 128:c * 512 + (ts_ + 1) * 128],
                            rhs=OW[:, c, mf * 512:(mf + 1) * 512],
                            start=(c == 0), stop=(c == 1))
                    em.op("tensor", fm, r=("OFr", "OB_W"), w=(f"ps{bank}",),
                          tag="po.mm")
                def fe(eng, mf=mf, bank=bank):
                    return eng.copy(out=RT[:, (mf % 2) * 512:(mf % 2) * 512 + 512],
                                    in_=ctx.ps[bank][:, 0:512])
                em.op("scalar", fe, r=(f"ps{bank}", "RTpo"), w=("RTpo",),
                      tag="po.ev")
            def fst(eng, f=f, ts_=ts_, half=half):
                return eng.dma_start(
                    out=ctx.dram["rs_in"][f * 512 + ts_ * 128:f * 512 + (ts_ + 1) * 128,
                                          half * 1024:(half + 1) * 1024],
                    in_=RT[:, 0:1024])
            em.op("sync", fst, r=("RTpo",), w=("rs_in",), dma=True, tag="po.st")


USE_GELU_TANH_LUT = True


def gelu_tanh_chain(ctx, gate_ps, psres, up_ps, upres, out_ap, out_res):
    em = ctx.em
    SC, SC2 = ctx.sb["SC"], ctx.sb["SC2"]
    if USE_GELU_TANH_LUT:
        def f1(eng):
            return eng.activation(out=SC[:, 0:512], in_=gate_ps,
                                  func=AF.Gelu_apprx_tanh)
        em.op("scalar", f1, r=(psres,), w=("SC_A",), tag="gl.g")
        def f2(eng):
            return eng.tensor_tensor(out=out_ap, in0=SC[:, 0:512], in1=up_ps,
                                     op=ALU.mult)
        em.op("vector", f2, r=("SC_A", upres), w=(out_res,), tag="gl.m")
        return
    def f1(eng):
        return eng.activation(out=SC[:, 0:512], in_=gate_ps, func=AF.Square)
    em.op("scalar", f1, r=(psres,), w=("SC_A",), tag="gl.sq")
    def f2(eng):
        return eng.tensor_scalar(out=SC[:, 0:512], in0=SC[:, 0:512],
                                 scalar1=0.044715 * 0.7978845608028654,
                                 scalar2=0.7978845608028654,
                                 op0=ALU.mult, op1=ALU.add)
    em.op("vector", f2, r=("SC_A",), w=("SC_A",), tag="gl.af")
    def f3(eng):
        return eng.tensor_tensor(out=SC[:, 512:1024], in0=SC[:, 0:512],
                                 in1=gate_ps, op=ALU.mult)
    em.op("vector", f3, r=("SC_A", psres), w=("SC_B",), tag="gl.p")
    def f4(eng):
        return eng.activation(out=SC2[:, 0:512], in_=SC[:, 512:1024],
                              func=AF.Tanh)
    em.op("scalar", f4, r=("SC_B",), w=("gl_t",), tag="gl.t")
    def f5(eng):
        return eng.tensor_scalar(out=SC[:, 512:1024], in0=SC2[:, 0:512],
                                 scalar1=1.0, scalar2=0.5,
                                 op0=ALU.add, op1=ALU.mult)
    em.op("vector", f5, r=("gl_t",), w=("SC_B",), tag="gl.h")
    def f6(eng):
        return eng.tensor_tensor(out=SC[:, 0:512], in0=SC[:, 512:1024],
                                 in1=gate_ps, op=ALU.mult)
    em.op("vector", f6, r=("SC_B", psres), w=("SC_A",), tag="gl.xg")
    def f7(eng):
        return eng.tensor_tensor(out=out_ap, in0=SC[:, 0:512], in1=up_ps,
                                 op=ALU.mult)
    em.op("vector", f7, r=("SC_A", upres), w=(out_res,), tag="gl.up")


def ffn_down_group(ctx, w2name, gi, w2row0, dacc, first):
    em = ctx.em
    QB, WB, SC = ctx.sb["QB"], ctx.sb["WB"], ctx.sb["SC"]
    for dmf in range(4):
        def fw(eng, dmf=dmf, w2row0=w2row0):
            return eng.dma_start(
                out=WB[:, 0:8192].rearrange("p (c f2) -> p c f2", f2=512),
                in_=ctx.wc[w2name][w2row0:w2row0 + 2048,
                                   dmf * 512:(dmf + 1) * 512]
                    .rearrange("(c p) f2 -> p c f2", p=128))
        em.op("sync", fw, r=(("wc", w2name), "WB"), w=("WB",), dma=True,
              tag="fd.w")
        for ts_ in range(8):
            bank = 2 + ts_ % 2
            for cg in range(16):
                def fm(eng, ts_=ts_, cg=cg, bank=bank):
                    return eng.matmul(
                        ctx.ps[bank][:, 0:512],
                        lhsT=QB[:, cg * 1024 + ts_ * 128:cg * 1024 + (ts_ + 1) * 128],
                        rhs=WB[:, cg * 512:(cg + 1) * 512],
                        start=(cg == 0), stop=(cg == 15))
                em.op("tensor", fm, r=("QB", "WB"), w=(f"ps{bank}",), tag="fd.mm")
            sl = (ts_ % 2) * 512
            def fe(eng, bank=bank, sl=sl):
                return eng.tensor_copy(out=SC[:, sl:sl + 512],
                                       in_=ctx.ps[bank][:, 0:512])
            em.op("vector", fe, r=(f"ps{bank}",), w=(f"fd{sl}",), tag="fd.ev")
            def fst(eng, ts_=ts_, dmf=dmf, sl=sl, first=first):
                return eng.dma_start(
                    out=dacc[ts_ * 128:(ts_ + 1) * 128,
                             dmf * 512:(dmf + 1) * 512],
                    in_=SC[:, sl:sl + 512],
                    accum_op=(ALU.bypass if first else ALU.add))
            em.op("gpsimd", fst, r=(f"fd{sl}", "dacc"), w=("dacc",), dma=True,
                  tag="fd.st")


def ffn_site(ctx, site):
    em = ctx.em
    base = BASE.get(site, site)
    QB, WB, SC, ST, DN = (ctx.sb["QB"], ctx.sb["WB"], ctx.sb["SC"],
                          ctx.sb["ST"], ctx.sb["DN"])
    XN = ctx.sb["XN"]
    dacc = ctx.dram["dacc"]
    ln_to_xn(ctx, kind="ln")
    em.claim("QB", "WB", tag="ffn.cl")
    if base == "l1f":
        for c0 in range(0, 16, 8):
            def fwr(eng, c0=c0):
                return eng.dma_start(out=ctx.sb["RT"][0:16, 0:1024],
                                     in_=ctx.inp["L1_ffn_Wr"][:, c0 * 128:(c0 + 8) * 128])
            em.op("gpsimd", fwr, r=("RT",), w=("RT",), dma=True, tag="rt.ld")
            for j in range(8):
                def ft(eng, j=j):
                    return eng.transpose(ctx.ps["tr"][:, j * 16:(j + 1) * 16],
                                         ctx.sb["RT"][0:16, j * 128:(j + 1) * 128],
                                         ctx.sb["c_ident"][0:16, 0:16])
                em.op("tensor", ft, r=("RT", "c_ident"), w=("ps_tr",), tag="rt.t")
            def fcp(eng, c0=c0):
                return eng.tensor_copy(out=ctx.sb["SB16"][:, c0 * 16:(c0 + 8) * 16],
                                       in_=ctx.ps["tr"][:, 0:128])
            em.op("vector", fcp, r=("ps_tr",), w=("wrT",), tag="rt.cp")
        def fbr(eng):
            return eng.dma_start(out=ctx.sb["SG"][:, 32:48],
                                 in_=ctx.inp["L1_ffn_br"].unsqueeze(0)
                                 .broadcast_to((128, 16)))
        em.op("gpsimd", fbr, r=("SG",), w=("SG",), dma=True, tag="br.ld")
        n_exp = 16
    if base == "l3f":
        def fwg(eng):
            return eng.dma_start(out=WB[:, 0:128].rearrange("p (c f2) -> p c f2", f2=8),
                                 in_=ctx.inp["L3_ffn_wg"]
                                 .rearrange("(c p) e -> p c e", p=128))
        em.op("gpsimd", fwg, r=("WB",), w=("WB",), dma=True, tag="wg.ld")
        n_exp = 8
    if base in ("l1f", "l3f"):
        for ts_ in range(8):
            fat, tt0 = divmod(ts_, 4)
            for c in range(NCH):
                def fm(eng, c=c, ts_=ts_, fat=fat, tt0=tt0):
                    rhs = (ctx.sb["SB16"][:, c * 16:(c + 1) * 16] if base == "l1f"
                           else WB[:, c * 8:(c + 1) * 8])
                    return eng.matmul(ctx.ps[6][:, 0:n_exp],
                                      lhsT=XN[:, c, ts_ * 128:(ts_ + 1) * 128],
                                      rhs=rhs, start=(c == 0),
                                      stop=(c == NCH - 1))
                em.op("tensor", fm,
                      r=("XN", "wrT" if base == "l1f" else "WB"),
                      w=("ps6",), tag="gt.mm")
            if base == "l1f":
                def fl_(eng, ts_=ts_):
                    return eng.tensor_tensor(out=SC[:, 0:n_exp],
                                             in0=ctx.ps[6][:, 0:n_exp],
                                             in1=ctx.sb["SG"][:, 32:48],
                                             op=ALU.add)
                em.op("vector", fl_, r=("ps6", "SG"), w=("SC_A",), tag="gt.br")
            else:
                def fl_(eng, ts_=ts_):
                    return eng.tensor_copy(out=SC[:, 0:n_exp],
                                           in_=ctx.ps[6][:, 0:n_exp])
                em.op("vector", fl_, r=("ps6",), w=("SC_A",), tag="gt.cp")
            def fz(eng, ts_=ts_):
                return eng.memset(DN[:, ts_ * 16:ts_ * 16 + n_exp], 0.0)
            em.op("vector", fz, w=("DNa",), tag="gt.z")
            for it in range(4):
                def f1(eng):
                    return eng.reduce_max(out=ST[:, 0:1], in_=SC[:, 0:n_exp],
                                          axis=AXX)
                em.op("vector", f1, r=("SC_A",), w=("gtm",), tag="gt.mx")
                def f2(eng):
                    return eng.tensor_scalar(out=SC[:, 64:64 + n_exp],
                                             in0=SC[:, 0:n_exp],
                                             scalar1=ST[:, 0:1], scalar2=None,
                                             op0=ALU.is_ge)
                em.op("vector", f2, r=("SC_A", "gtm"), w=("gtge",), tag="gt.ge")
                def f3(eng, ts_=ts_):
                    return eng.tensor_tensor(out=DN[:, ts_ * 16:ts_ * 16 + n_exp],
                                             in0=DN[:, ts_ * 16:ts_ * 16 + n_exp],
                                             in1=SC[:, 64:64 + n_exp], op=ALU.add)
                em.op("vector", f3, r=("gtge", "DNa"), w=("DNa",), tag="gt.acc")
                def f4a(eng):
                    return eng.tensor_scalar(out=SC[:, 96:96 + n_exp],
                                             in0=SC[:, 64:64 + n_exp],
                                             scalar1=-1e30, scalar2=None,
                                             op0=ALU.mult)
                em.op("vector", f4a, r=("gtge",), w=("gtsu",), tag="gt.sa")
                def f4(eng):
                    return eng.tensor_tensor(out=SC[:, 0:n_exp],
                                             in0=SC[:, 96:96 + n_exp],
                                             in1=SC[:, 0:n_exp], op=ALU.add)
                em.op("vector", f4, r=("gtsu", "SC_A"), w=("SC_A",), tag="gt.su")
    if base == "l2f":
        n_chunks, w2name = 128, "l2f_d"
        def gu_tile(name, c, ng):
            r = ng // 16
            return ctx.wc[name][2048 * r + c * 128:2048 * r + (c + 1) * 128,
                               (ng % 16) * 128:(ng % 16 + 1) * 128]
        for gi in range(8):
            for cg in range(16):
                ng = gi * 16 + cg
                def fwg_(eng, ng=ng):
                    return eng.dma_start(
                        out=WB[:, 0:2048].rearrange("p (c f2) -> p c f2", f2=128),
                        in_=ctx.wc["l2f_g"][2048 * (ng // 16):2048 * (ng // 16) + 2048,
                                            (ng % 16) * 128:(ng % 16 + 1) * 128]
                            .rearrange("(c p) f2 -> p c f2", p=128))
                em.op("sync", fwg_, r=(("wc", "l2f_g"), "WB"), w=("WB",),
                      dma=True, tag="l2.w")
                def fwu_(eng, ng=ng):
                    return eng.dma_start(
                        out=WB[:, 2048:4096].rearrange("p (c f2) -> p c f2", f2=128),
                        in_=ctx.wc["l2f_u"][2048 * (ng // 16):2048 * (ng // 16) + 2048,
                                            (ng % 16) * 128:(ng % 16 + 1) * 128]
                            .rearrange("(c p) f2 -> p c f2", p=128))
                em.op("sync", fwu_, r=(("wc", "l2f_u"), "WB"), w=("WB",),
                      dma=True, tag="l2.wu")
                for fat in range(2):
                    for c in range(NCH):
                        def fg(eng, c=c, fat=fat):
                            return eng.matmul(ctx.ps[0][:, 0:512],
                                              lhsT=WB[:, c * 128:(c + 1) * 128],
                                              rhs=XN[:, c, fat * 512:(fat + 1) * 512],
                                              start=(c == 0), stop=(c == NCH - 1))
                        em.op("tensor", fg, r=("WB", "XN"), w=("ps0",), tag="l2.g")
                        def fu(eng, c=c, fat=fat):
                            return eng.matmul(ctx.ps[1][:, 0:512],
                                              lhsT=WB[:, 2048 + c * 128:2048 + (c + 1) * 128],
                                              rhs=XN[:, c, fat * 512:(fat + 1) * 512],
                                              start=(c == 0), stop=(c == NCH - 1))
                        em.op("tensor", fu, r=("WB", "XN"), w=("ps1",), tag="l2.u")
                    out_ap = QB[:, cg * 1024 + fat * 512:cg * 1024 + (fat + 1) * 512]
                    gelu_tanh_chain(ctx, ctx.ps[0][:, 0:512], "ps0",
                                    ctx.ps[1][:, 0:512], "ps1", out_ap, "QB")
            ffn_down_group(ctx, w2name, gi, gi * 2048, dacc, gi == 0)
    else:
        w1name = "l1f_w1" if base == "l1f" else "l3f_w1"
        w2name = "l1f_w2" if base == "l1f" else "l3f_w2"
        n_groups = 4
        for gi in range(n_groups):
            for ii in range(4):
                ncol0 = gi * 2048 + ii * 512
                def fw1(eng, ncol0=ncol0):
                    return eng.dma_start(
                        out=WB[:, 0:8192].rearrange("p (c f2) -> p c f2", f2=512),
                        in_=ctx.wc[w1name][:, ncol0:ncol0 + 512]
                            .rearrange("(c p) f2 -> p c f2", p=128))
                em.op("sync", fw1, r=(("wc", w1name), "WB"), w=("WB",),
                      dma=True, tag="f13.w1")
                for ts_ in range(8):
                    fat, tt0 = divmod(ts_, 4)
                    bank = ts_ % 2
                    for c in range(NCH):
                        def fm(eng, c=c, ts_=ts_, bank=bank, fat=fat, tt0=tt0):
                            return eng.matmul(
                                ctx.ps[bank][:, 0:512],
                                lhsT=XN[:, c, fat * 512 + tt0 * 128:fat * 512 + (tt0 + 1) * 128],
                                rhs=WB[:, c * 512:(c + 1) * 512],
                                start=(c == 0), stop=(c == NCH - 1))
                        em.op("tensor", fm, r=("XN", "WB"), w=(f"ps{bank}",),
                              tag="f13.h")
                    ncol_abs = ncol0
                    if base == "l1f":
                        blk_i = ncol_abs // 512
                        def fe1(eng, bank=bank):
                            return eng.activation(out=SC[:, 0:512],
                                                  in_=ctx.ps[bank][:, 0:512],
                                                  func=AF.Erf,
                                                  scale=0.7071067811865476)
                        em.op("scalar", fe1, r=(f"ps{bank}",), w=("SC_A",),
                              tag="f13.erf")
                        def fe2(eng, bank=bank):
                            return eng.scalar_tensor_tensor(
                                out=SC[:, 512:1024], in0=SC[:, 0:512],
                                scalar=1.0, in1=ctx.ps[bank][:, 0:512],
                                op0=ALU.add, op1=ALU.mult)
                        em.op("vector", fe2, r=("SC_A", f"ps{bank}"),
                              w=("SC_B",), tag="f13.g")
                        def fe3(eng, ts_=ts_, blk_i=blk_i):
                            return eng.tensor_scalar(
                                out=ctx.sb["RT"][:, 0:512], in0=SC[:, 512:1024],
                                scalar1=DN[:, ts_ * 16 + blk_i:ts_ * 16 + blk_i + 1],
                                scalar2=0.5, op0=ALU.mult, op1=ALU.mult)
                        em.op("vector", fe3, r=("SC_B", "DNa"), w=("RTff",),
                              tag="f13.m")
                    else:
                        e_i = ncol_abs // 1024
                        def fe3(eng, bank=bank, ts_=ts_, e_i=e_i):
                            return eng.tensor_scalar(
                                out=ctx.sb["RT"][:, 0:512],
                                in0=ctx.ps[bank][:, 0:512],
                                scalar1=0.0,
                                scalar2=DN[:, ts_ * 16 + e_i:ts_ * 16 + e_i + 1],
                                op0=ALU.max, op1=ALU.mult)
                        em.op("vector", fe3, r=(f"ps{bank}", "DNa"), w=("RTff",),
                              tag="f13.rm")
                    for nc_ in range(4):
                        def ft(eng, nc_=nc_):
                            return eng.transpose(
                                ctx.ps["tr"][:, nc_ * 128:(nc_ + 1) * 128],
                                ctx.sb["RT"][:, nc_ * 128:(nc_ + 1) * 128],
                                ctx.sb["c_ident"][:])
                        em.op("tensor", ft, r=("RTff", "c_ident"), w=("ps_tr",),
                              tag="f13.t")
                    def fcp2(eng, ii=ii, ts_=ts_):
                        return eng.tensor_copy(
                            out=QB[:].rearrange("p (a b2) -> p a b2", b2=1024)
                                [:, ii * 4:(ii + 1) * 4, ts_ * 128:(ts_ + 1) * 128],
                            in_=ctx.ps["tr"][:, 0:512]
                                .rearrange("p (a b2) -> p a b2", b2=128))
                    em.op("vector", fcp2, r=("ps_tr",), w=("QB",), tag="f13.cp")
            ffn_down_group(ctx, w2name, gi, gi * 2048, dacc, gi == 0)
    scale = 0.5
    add_tokmajor_dram_to_R(ctx, dacc, scale, extra_r=("dacc",))


def build_program(subs):
    nc = bass.Bass("TRN2", target_bir_lowering=False, debug=False)
    em = Emitter()
    ctx = Ctx(nc, em, subs)
    bases = [BASE.get(s, s) for s in subs]
    needed_in = set()
    needed_w = []
    for b in bases:
        for x in SUB_IN[b]:
            needed_in.add(x)
        for wname in SUB_W[b]:
            if wname not in needed_w:
                needed_w.append(wname)
    ctx.inp["res_in"] = nc.dram_tensor("res_in", [D, TPC], F32,
                                       kind="ExternalInput").ap()
    for name in sorted(needed_in):
        shp, dt = INPUT_SHAPES[name]
        ctx.inp[name] = nc.dram_tensor(name, list(shp), dt,
                                       kind="ExternalInput").ap()
    for name, (shape, dt) in SBUF_CONSTS.items():
        ctx.inp[name] = nc.dram_tensor(name, shape, dt, kind="ExternalInput").ap()
    any_rope = any(b in ("l2a", "l3a") for b in bases)
    for name, (shape, dt) in DRAM_CONSTS.items():
        if any_rope:
            ctx.inp[name] = nc.dram_tensor(name, shape, dt,
                                           kind="ExternalInput").ap()
    ctx.out_res = nc.dram_tensor("res_out", [D, TPC], F32,
                                 kind="ExternalOutput").ap()
    for wname in needed_w:
        s = WSPEC[wname]
        ctx.wc[wname] = nc.dram_tensor("wc_" + wname, [s["rows"], s["cols"]],
                                       BF16, kind="Internal").ap()
        if s["kind"] in ("deq_gu", "deq_d"):
            ctx.dram["gin_" + wname] = nc.dram_tensor(
                "gin_" + wname, [2048, 2048], BF16, kind="Internal").ap()
    has_attn = any(b in ("l0f", "l0a", "l1a", "l2a", "l3a") for b in bases)
    if has_attn:
        ctx.dram["xg_in"] = nc.dram_tensor("xg_in", [D, TPC], BF16,
                                           kind="Internal").ap()
        ctx.dram["xg"] = nc.dram_tensor("xg", [NCORE * D, TPC], BF16,
                                        kind="Internal").ap()
        ctx.dram["kloc"] = nc.dram_tensor("kloc", [256, T + 16], BF16,
                                          kind="Internal").ap()
        ctx.dram["vloc"] = nc.dram_tensor("vloc", [T + 16, 256], BF16,
                                          kind="Internal").ap()
        ctx.dram["rs_in"] = nc.dram_tensor("rs_in", [T, D], BF16,
                                           kind="Internal").ap()
        ctx.dram["rs_out"] = nc.dram_tensor("rs_out", [TPC, D], BF16,
                                            kind="Internal").ap()
    if any(b in ("l1f", "l2f", "l3f") for b in bases):
        ctx.dram["dacc"] = nc.dram_tensor("dacc", [TPC, D], F32,
                                          kind="Internal").ap()
    dbg_shapes = {"xn": ([128, NCH, TPC], BF16), "xg": ([NCORE * D, TPC], BF16),
                  "kloc": ([256, T + 16], BF16), "vloc": ([T + 16, 256], BF16),
                  "rsin": ([T, D], BF16), "rsout": ([TPC, D], BF16),
                  "wcq": ([2048, 256], BF16), "sg": ([128, 16], F32),
                  "qb": ([128, 16384], BF16)}
    for t in TAPS:
        shp, dt = dbg_shapes[t]
        ctx.dram["dbg_" + t] = nc.dram_tensor("dbg_" + t, list(shp), dt,
                                              kind="ExternalOutput").ap()

    with nc.Block() as block:
        alloc_common(ctx)
        def fwarm0(eng):
            return eng.memset(ctx.sb["LR"][32:33, 504:506], 0.0)
        em.op("vector", fwarm0, w=("warm",), tag="warmup0")
        def fsgz(eng):
            return eng.memset(ctx.sb["SG"][:], 0.0)
        em.op("vector", fsgz, w=("SG",), tag="sgz")
        def fwarm(eng):
            return eng.activation(out=ctx.sb["LR"][32:33, 506:508],
                                  in_=ctx.sb["LR"][32:33, 504:506], func=AF.Exp)
        em.op("scalar", fwarm, r=("warm",), w=("warm",), tag="warmup")
        def fres(eng):
            return eng.dma_start(
                out=ctx.sb["R"][:],
                in_=ctx.inp["res_in"].rearrange("(c p) t -> p c t", p=128))
        em.op("sync", fres, w=("R",), dma=True, tag="ld.res")
        prep_tables(ctx, subs)
        for wname in needed_w:
            prep_matrix(ctx, wname)
        for site in subs:
            base = BASE.get(site, site)
            if base in ("l0f", "l0a", "l1a", "l2a", "l3a"):
                attn_site(ctx, site)
            else:
                ffn_site(ctx, site)
        def fout(eng):
            return eng.dma_start(
                out=ctx.out_res.rearrange("(c p) t -> p c t", p=128),
                in_=ctx.sb["R"][:])
        em.op("sync", fout, r=("R",), w=("res_out",), dma=True, tag="st.res")
        em.emit(nc, block)
    return nc


INPUT_SHAPES = {}


def _init_input_shapes():
    sp = {
        "L0_ffn_Wp": ((2048, 2048), F32),
        "hs_l0f_wstd": ((2,), F32), "hs_l0f_wrec": ((2,), F32),
        "hs_l0f_waq": ((2048, 240), F32), "hs_l0f_wak": ((2048, 240), F32),
        "hs_l0f_waql": ((2048, 16), F32), "hs_l0f_wakl": ((2048, 16), F32),
        "hs_l0f_wv": ((2048, 256), F32), "hs_l0f_wosl": ((256, 2048), F32),
        "L0_attn_global_tokens": ((1, 16, 2048), F32),
        "L0_attn_ln_scale": ((2048,), F32), "L0_attn_ln_bias": ((2048,), F32),
        "L1_ffn_W1": ((8192, 2048), F32), "L1_ffn_W2": ((2048, 8192), F32),
        "L1_ffn_Wr": ((16, 2048), F32), "L1_ffn_br": ((16,), F32),
        "L1_attn_gamma": ((128,), F32), "L1_attn_beta": ((128,), F32),
        "L2_attn_W_kv": ((256, 2048), F32),
        "hs_l2a_wkh": ((2, 128, 128), F32), "hs_l2a_wvh": ((2, 128, 128), F32),
        "hs_l2f_gpk": ((2048, 1024), I32), "hs_l2f_gsc": ((2048, 64), F32),
        "hs_l2f_upk": ((2048, 1024), I32), "hs_l2f_usc": ((2048, 64), F32),
        "hs_l2f_dpk": ((2048, 1024), I32), "hs_l2f_dsc": ((2048, 64), F32),
        "L3_ffn_wg": ((2048, 8), F32),
        "L3_ffn_w1": ((8, 1024, 2048), F32), "L3_ffn_w2": ((8, 2048, 1024), F32),
        "L3_attn_rms_w": ((2048,), F32),
        "hs_l3a_wq": ((256, 2048), F32), "hs_l3a_wkv": ((128, 2048), F32),
        "hs_l3a_wosl": ((256, 2048), F32),
    }
    for tag in ("l0a", "l1a"):
        for w in ("wq", "wk", "wv", "wosl"):
            sp[f"hs_{tag}_{w}"] = ((256, 2048), F32)
    sp["hs_l2a_wq"] = ((256, 2048), F32)
    sp["hs_l2a_wosl"] = ((256, 2048), F32)
    INPUT_SHAPES.update(sp)


_init_input_shapes()


def kernel(**inputs):
    consts = _host_consts()
    x = np.asarray(inputs["L0_attn_x"])[0]
    res_shards = []
    for c in range(NCORE):
        bA, bB = OWN_BLOCKS[c]
        sl = np.concatenate([x[bA * BLK:(bA + 1) * BLK],
                             x[bB * BLK:(bB + 1) * BLK]], 0)
        res_shards.append(np.ascontiguousarray(sl.T))
    slices = [make_host_slices(inputs, c) for c in range(NCORE)]
    for group in LAUNCH_GROUPS:
        nc = build_program(group)
        bases = [BASE.get(s, s) for s in group]
        needed = set()
        for b in bases:
            needed.update(SUB_IN[b])
        any_rope = any(b in ("l2a", "l3a") for b in bases)
        in_maps = []
        for c in range(NCORE):
            m = {"res_in": res_shards[c]}
            for name in needed:
                m[name] = (slices[c][name] if name.startswith("hs_")
                           else np.ascontiguousarray(inputs[name]))
            for cname in SBUF_CONSTS:
                m[cname] = consts[cname]
            if any_rope:
                for cname in DRAM_CONSTS:
                    m[cname] = consts[cname]
            in_maps.append(m)
        res = run_bass_kernel_spmd(nc, in_maps, core_ids=list(range(NCORE)))
        if TAPS:
            TAP_RESULTS.append([{k: np.asarray(v) for k, v in r.items()
                                 if k.startswith("dbg_")} for r in res.results])
        res_shards = [np.asarray(r["res_out"]) for r in res.results]
    out = np.empty((T, D), np.float32)
    for c in range(NCORE):
        bA, bB = OWN_BLOCKS[c]
        sh = res_shards[c].T
        out[bA * BLK:(bA + 1) * BLK] = sh[0:BLK]
        out[bB * BLK:(bB + 1) * BLK] = sh[BLK:]
    return out[None]
